# version 32
# baseline (speedup 1.0000x reference)
import numpy as np
import ml_dtypes
from contextlib import ExitStack

import concourse.bass as bass
import concourse.mybir as mybir
from concourse.bass_utils import run_bass_kernel_spmd

F32 = mybir.dt.float32
BF16 = mybir.dt.bfloat16
AF = mybir.ActivationFunctionType
ALU = mybir.AluOpType
AX = mybir.AxisListType

SEQ = 2048
DM = 1024
NT = SEQ // 128
EPS_LN = 1e-5
EPS_RMS = 1e-6
ENGINES = ("sync", "gpsimd", "vector", "scalar", "tensor")
CSEM = {"tensor": "pe", "vector": "dv", "scalar": "ac", "gpsimd": "gp"}
QSEM = {"sync": "ld", "gpsimd": "wd", "scalar": "sd"}


class Slot:
    __slots__ = ("name", "ready", "readers", "qsem")

    def __init__(self, name):
        self.name = name
        self.ready = None
        self.readers = []
        self.qsem = None


class PB:
    SERIAL_DMA = False

    def __init__(self, nc):
        self.nc = nc
        self.ops = {e: [] for e in ENGINES}
        self.sem = {}
        self.cnt = {}
        self.hwm = {e: {} for e in ENGINES}
        self.ctx = None

    def new_sem(self, ctx, name):
        self.sem[name] = ctx.enter_context(self.nc.semaphore(name))
        self.cnt[name] = 0

    def slot_qsem(self, s):
        if s.qsem is None:
            name = "q_" + s.name
            self.sem[name] = self.ctx.enter_context(self.nc.semaphore(name))
            self.cnt[name] = 0
            s.qsem = name
        return s.qsem

    def _wait(self, eng, sem, tgt):
        if self.hwm[eng].get(sem, 0) >= tgt:
            return
        self.hwm[eng][sem] = tgt
        h = self.sem[sem]
        self.ops[eng].append(lambda e, h=h, tgt=tgt: e.wait_ge(h, tgt))

    def wait_tok(self, eng, tok, cur_dma=False, raw=False):
        if tok is None:
            return
        sem, val = tok
        if not raw and not cur_dma and sem == CSEM.get(eng):
            return
        self._wait(eng, sem, val)

    def use(self, eng, f, reads=(), writes=(), dma=False, extra_tok=()):
        for s in reads:
            self.wait_tok(eng, s.ready, cur_dma=dma, raw=True)
        for s in writes:
            self.wait_tok(eng, s.ready, cur_dma=dma)
            for t in s.readers:
                self.wait_tok(eng, t, cur_dma=dma)
        for t in extra_tok:
            self.wait_tok(eng, t, cur_dma=dma, raw=True)
        if dma:
            sem = self.slot_qsem(writes[0]) if writes else QSEM[eng]
            by = 16
        else:
            sem = CSEM[eng]
            by = 1
        if dma and PB.SERIAL_DMA:
            self._wait(eng, sem, self.cnt[sem])
        self.cnt[sem] += by
        tok = (sem, self.cnt[sem])
        h = self.sem[sem]

        def run(e, f=f, h=h, by=by):
            ins = f(e)
            ins.then_inc(h, by)
        self.ops[eng].append(run)
        for s in writes:
            s.ready = tok
            s.readers = []
        for s in reads:
            s.readers.append(tok)
        return tok

    def raw(self, eng, f):
        self.ops[eng].append(f)

    def barrier(self):
        for s in list(self.sem):
            if s.startswith("q_") or s in ("ld", "wd", "sd"):
                self._wait("sync", s, self.cnt[s])
        for eng in ENGINES:
            own = CSEM.get(eng)
            if own is not None:
                h2 = self.sem[own]
                tgt2 = self.cnt[own]
                self.ops[eng].append(lambda e, h2=h2, tgt2=tgt2: e.wait_ge(h2, tgt2))
                self.hwm[eng][own] = max(self.hwm[eng].get(own, 0), tgt2)
            self.cnt["bar"] += 1
            h = self.sem["bar"]
            self.ops[eng].append(lambda e, h=h: e.nop().then_inc(h, 1))
        tgt = self.cnt["bar"]
        for eng in ENGINES:
            self._wait(eng, "bar", tgt)


def _pretile_kmajor(w_t, M):
    K = w_t.shape[0]
    return np.ascontiguousarray(
        w_t.reshape(K // 128, 128, M).transpose(1, 0, 2).reshape(128, (K // 128) * M))


def _f32(x):
    return np.ascontiguousarray(np.asarray(x, dtype=np.float32))


def prep_inputs(I):
    t = {}
    t["cw_in"] = _pretile_kmajor(_f32(I["L0_attn_in_proj_w"]).T, 2048)
    for nm, key in (("cw_q", "L0_attn_q_proj_w"), ("cw_k", "L0_attn_k_proj_w"),
                    ("cw_v", "L0_attn_v_proj_w"), ("cw_i", "L0_attn_i_gate_w"),
                    ("cw_f", "L0_attn_f_gate_w"), ("cw_o", "L0_attn_o_gate_w"),
                    ("cw_out", "L0_attn_out_proj_w")):
        t[nm] = _pretile_kmajor(_f32(I[key]).T, 1024)
    for nm, key in (("bi_c", "L0_attn_i_gate_b"), ("bf_c", "L0_attn_f_gate_b"),
                    ("bo_c", "L0_attn_o_gate_b")):
        t[nm] = np.ascontiguousarray(_f32(I[key]).reshape(8, 128).T)
    t["qnw_c"] = np.ascontiguousarray(np.tile(_f32(I["L0_attn_q_norm_w"]), 2).reshape(128, 1))
    t["knw_c"] = np.ascontiguousarray(np.tile(_f32(I["L0_attn_k_norm_w"]), 2).reshape(128, 1))
    t["csel"] = _pretile_kmajor(_f32(I["L0_ffn_expert_sel"]).T, 16)
    keys = _f32(I["L0_ffn_keys"])
    t["ckeys"] = np.ascontiguousarray(keys.reshape(16, 8, 128, 128).transpose(2, 0, 1, 3).reshape(128, 16 * 8 * 128))
    vals = _f32(I["L0_ffn_values"])
    t["cvalues"] = np.ascontiguousarray(vals.transpose(1, 0, 2).reshape(128, 16 * 1024))
    t["cWqkv"] = _pretile_kmajor(_f32(I["L1_attn_W_qkv"]), 3072)
    t["cgw"] = _pretile_kmajor(_f32(I["L1_ffn_gate_w"]).T, 8)
    t["gate_b_bc"] = np.ascontiguousarray(np.tile(_f32(I["L1_ffn_gate_b"]), (128, 1)))
    w1 = _f32(I["L1_ffn_w1"])
    t["cw1"] = np.ascontiguousarray(w1.reshape(8, 8, 128, 1408).transpose(0, 2, 1, 3).reshape(8, 128, 8 * 1408))
    w3 = _f32(I["L1_ffn_w3"])
    t["cw3"] = np.ascontiguousarray(w3.reshape(8, 8, 128, 1408).transpose(0, 2, 1, 3).reshape(8, 128, 8 * 1408))
    w2 = _f32(I["L1_ffn_w2"])
    t["cw2"] = np.ascontiguousarray(w2.reshape(8, 11, 128, 1024).transpose(0, 2, 1, 3).reshape(8, 128, 11 * 1024))
    t["b1c"] = np.ascontiguousarray(_f32(I["L1_ffn_b1"]).reshape(8, 11, 128).transpose(0, 2, 1))
    t["b3c"] = np.ascontiguousarray(_f32(I["L1_ffn_b3"]).reshape(8, 11, 128).transpose(0, 2, 1))
    t["b2flat"] = np.ascontiguousarray(_f32(I["L1_ffn_b2"]).reshape(1, 8 * 1024))
    t["csw1"] = _pretile_kmajor(_f32(I["L1_ffn_sw1"]), 1408)
    t["csw3"] = _pretile_kmajor(_f32(I["L1_ffn_sw3"]), 1408)
    sw2 = _f32(I["L1_ffn_sw2"])
    t["csw2"] = np.ascontiguousarray(sw2.reshape(11, 128, 1024).transpose(1, 0, 2).reshape(128, 11 * 1024))
    t["sb1c"] = np.ascontiguousarray(_f32(I["L1_ffn_sb1"]).reshape(11, 128).T)
    t["sb3c"] = np.ascontiguousarray(_f32(I["L1_ffn_sb3"]).reshape(11, 128).T)
    t["sb2flat"] = np.ascontiguousarray(_f32(I["L1_ffn_sb2"]).reshape(1, 1024))
    t["ident_bf"] = np.eye(128, dtype=ml_dtypes.bfloat16)
    s_i = np.arange(128)[:, None]
    t_i = np.arange(128)[None, :]
    t["mask_at"] = ((s_i <= t_i) & (s_i // 64 == t_i // 64)).astype(ml_dtypes.bfloat16)
    t["causal_neg"] = np.where(t_i > s_i, np.float32(-1e30), np.float32(0.0))
    ind_all = np.zeros((16, 8 * 128), dtype=ml_dtypes.bfloat16)
    bones = np.zeros((128, 8 * 16), dtype=ml_dtypes.bfloat16)
    for kc in range(8):
        for p in range(128):
            h = 2 * kc + (p // 64)
            ind_all[h, kc * 128 + p] = 1
            bones[p, kc * 16 + h] = 1
    t["ind_all"] = ind_all
    t["bones"] = bones
    return t


PHASE_ALL = ("A0", "A1", "A2", "B", "C", "D0", "D", "E")
B_TILES = 32
B_STAGE = 9
B_SUB = 9
B_VAR = 0


def build_program(phases=PHASE_ALL, debug=()):
    nc = bass.Bass("TRN2", target_bir_lowering=False)
    dbg = set(debug)
    phases = set(phases)

    def IN(name, shape, dt=F32):
        return nc.dram_tensor(name, list(shape), dt, kind="ExternalInput").ap()

    x0 = IN("x0", [NT, 128, DM])
    cw_in = IN("cw_in", [128, 8 * 2048])
    cw_q = IN("cw_q", [128, 8 * 1024]); cw_k = IN("cw_k", [128, 8 * 1024])
    cw_v = IN("cw_v", [128, 8 * 1024]); cw_i = IN("cw_i", [128, 8 * 1024])
    cw_f = IN("cw_f", [128, 8 * 1024]); cw_o = IN("cw_o", [128, 8 * 1024])
    cw_out = IN("cw_out", [128, 8 * 1024])
    bi_c = IN("bi_c", [128, 8]); bf_c = IN("bf_c", [128, 8]); bo_c = IN("bo_c", [128, 8])
    qnw_c = IN("qnw_c", [128, 1]); knw_c = IN("knw_c", [128, 1])
    csel = IN("csel", [128, 8 * 16])
    ckeys = IN("ckeys", [128, 16 * 8 * 128])
    cvalues = IN("cvalues", [128, 16 * 1024])
    cWqkv = IN("cWqkv", [128, 8 * 3072])
    cgw = IN("cgw", [128, 8 * 8])
    gate_b_bc = IN("gate_b_bc", [128, 8])
    cw1 = IN("cw1", [8, 128, 8 * 1408]); cw3 = IN("cw3", [8, 128, 8 * 1408])
    cw2 = IN("cw2", [8, 128, 11 * 1024])
    b1c = IN("b1c", [8, 128, 11]); b3c = IN("b3c", [8, 128, 11])
    b2flat = IN("b2flat", [1, 8 * 1024])
    csw1 = IN("csw1", [128, 8 * 1408]); csw3 = IN("csw3", [128, 8 * 1408])
    csw2 = IN("csw2", [128, 11 * 1024])
    sb1c = IN("sb1c", [128, 11]); sb3c = IN("sb3c", [128, 11])
    sb2flat = IN("sb2flat", [1, 1024])
    ident_bf = IN("ident_bf", [128, 128], BF16)
    mask_at = IN("mask_at", [128, 128], BF16)
    causal_neg = IN("causal_neg", [128, 128])
    ind_all = IN("ind_all", [16, 8 * 128], BF16)
    bones = IN("bones", [128, 8 * 16], BF16)

    yout = nc.dram_tensor("y", [NT, 128, DM], F32, kind="ExternalOutput").ap()

    gsig_h = nc.dram_tensor("gsig_h", [8, 128, SEQ], BF16, kind="Internal").ap()
    og_h = nc.dram_tensor("og_h", [8, 128, SEQ], BF16, kind="Internal").ap()
    g0_h = nc.dram_tensor("g0_h", [NT, 128, DM], BF16, kind="Internal").ap()
    hg_h = nc.dram_tensor("hg_h", [8, 128, SEQ], BF16, kind="Internal").ap()
    x1_h = nc.dram_tensor("x1_h", [NT, 128, DM], F32, kind="Internal").ap()
    s1_h = nc.dram_tensor("s1_h", [NT, 128, DM], F32, kind="Internal").ap()

    dbg_specs = {
        "xn": ([128, 8 * SEQ], BF16), "xinner": ([128, 8 * SEQ], BF16),
        "gsig": ([8, 128, SEQ], BF16), "og": ([8, 128, SEQ], BF16),
        "g0": ([NT, 128, DM], BF16), "hg": ([8, 128, SEQ], BF16),
        "x1": ([NT, 128, DM], F32), "s1": ([NT, 128, DM], F32),
        "xn1": ([128, 8 * SEQ], BF16), "rw": ([128, NT * 8], F32),
    }
    dbg_out = {k: nc.dram_tensor("dbg_" + k, list(s[0]), s[1], kind="ExternalOutput").ap()
               for k, s in dbg_specs.items() if k in dbg}

    with ExitStack() as ctx:
        E = ctx.enter_context
        pb = PB(nc)
        pb.ctx = ctx
        for s in ("ld", "wd", "sd", "pe", "dv", "ac", "gp", "bar"):
            pb.new_sem(ctx, s)

        sb_ident = E(nc.sbuf_tensor("sb_ident", [128, 128], BF16))
        sb_ones = E(nc.sbuf_tensor("sb_ones", [128, 512], BF16))
        sb_zeros = E(nc.sbuf_tensor("sb_zeros", [128, 64], F32))
        sb_cst = E(nc.sbuf_tensor("sb_cst", [128, 3], F32))
        xn_stk = ExitStack()
        xin_stk = ExitStack()
        a1w_stk = ExitStack()
        a2w_stk = ExitStack()
        x_inner = xin_stk.enter_context(nc.sbuf_tensor("x_inner", [128, 8 * SEQ], BF16, side="right"))
        xn_fm = xn_stk.enter_context(nc.sbuf_tensor("xn_fm", [128, 8 * SEQ], BF16, side="right"))
        keys_sb = a2w_stk.enter_context(nc.sbuf_tensor("keys_sb", [128, 16 * 8 * 128], BF16, side="right"))
        vals_sb = a2w_stk.enter_context(nc.sbuf_tensor("vals_sb", [128, 16 * 1024], BF16, side="right"))
        sel_sb = a2w_stk.enter_context(nc.sbuf_tensor("sel_sb", [128, 8 * 16], BF16, side="right"))
        w_in_sb = a1w_stk.enter_context(nc.sbuf_tensor("w_in_sb", [128, 8 * 2048], BF16, side="right"))
        w_o_sb = a1w_stk.enter_context(nc.sbuf_tensor("w_o_sb", [128, 8 * 1024], BF16, side="right"))
        rw_tok = E(nc.sbuf_tensor("rw_tok", [128, NT * 8], F32))

        PS = [E(nc.psum_tensor("psb%d" % i, [128, 512], F32)) for i in range(6)]
        PT1 = E(nc.psum_tensor("pt1", [128, 1024], BF16))
        PT2 = E(nc.psum_tensor("pt2", [128, 1024], BF16))

        S = {}
        def slot(name):
            S[name] = Slot(name)
            return S[name]
        for nm in ("consts", "xnf0", "xnf1", "xnf2", "xnf3", "x_inner", "rw_tok",
                   "ps0", "ps1", "ps2", "ps3", "ps4", "ps5", "pt1", "pt2",
                   "gsig_h", "og_h", "g0_h", "hg_h", "x1_h", "s1_h"):
            slot(nm)
        PSLOT = [S["ps%d" % i] for i in range(6)]

        xn_ap = xn_fm.ap().rearrange("p (c s) -> p c s", s=SEQ)
        xin_ap = x_inner.ap().rearrange("p (c s) -> p c s", s=SEQ)

        for dst, src in ((sb_ident, ident_bf),):
            pb.use("sync", lambda e, dst=dst, src=src: e.dma_start(out=dst[:], in_=src),
                   writes=(S["consts"],), dma=True)
        pb.use("vector", lambda e: e.memset(sb_ones[:], 1.0), writes=(S["consts"],))
        pb.use("vector", lambda e: e.memset(sb_zeros[:], 0.0), writes=(S["consts"],))
        pb.use("vector", lambda e: e.memset(sb_cst[:, 0:1], EPS_LN), writes=(S["consts"],))
        pb.use("vector", lambda e: e.memset(sb_cst[:, 1:2], EPS_RMS), writes=(S["consts"],))
        pb.use("vector", lambda e: e.memset(sb_cst[:, 2:3], 1.0), writes=(S["consts"],))
        sl_win = slot("w_in"); sl_wo = slot("w_o")
        sl_keys = slot("keys"); sl_vals = slot("vals"); sl_selw = slot("selw")
        if "A1" in phases:
            pb.use("gpsimd", lambda e: e.dma_start(out=w_in_sb[:], in_=cw_in), writes=(sl_win,), dma=True)
            pb.use("gpsimd", lambda e: e.dma_start(out=w_o_sb[:], in_=cw_o), writes=(sl_wo,), dma=True)
        if "A2" in phases:
            pb.use("gpsimd", lambda e: e.dma_start(out=keys_sb[:], in_=ckeys), writes=(sl_keys,), dma=True)
            pb.use("gpsimd", lambda e: e.dma_start(out=vals_sb[:], in_=cvalues), writes=(sl_vals,), dma=True)
            pb.use("gpsimd", lambda e: e.dma_start(out=sel_sb[:], in_=csel), writes=(sl_selw,), dma=True)

        a01_stk = ExitStack()
        if "A0" in phases:
            A0P = a01_stk.enter_context
            xt4 = A0P(nc.sbuf_tensor("xt4", [128, 4, DM], F32))
            sqd = A0P(nc.sbuf_tensor("sqd", [128, DM], F32))
            a0_xn = A0P(nc.sbuf_tensor("a0_xn", [128, 2, DM], BF16))
            a0_st = A0P(nc.sbuf_tensor("a0_st", [128, 6, 16], F32))
            sl_x4 = [slot("xt4_%d" % j) for j in range(4)]
            sl_sqd = slot("a0_sqd")
            sl_xn = [slot("a0_xn0"), slot("a0_xn1")]
            sl_sA = slot("a0_sA"); sl_sB = slot("a0_sB")
            sl_nm = slot("a0_nm"); sl_rs = slot("a0_rs")
            sl_t4 = slot("a0_t4"); sl_t5 = slot("a0_t5")
            sA = a0_st.ap()[:, 0]; sB = a0_st.ap()[:, 1]
            nm = a0_st.ap()[:, 2]; rs = a0_st.ap()[:, 3]
            q4 = a0_st.ap()[:, 4]; q5 = a0_st.ap()[:, 5]

            def emit_a0_group(g):
                g4 = slice(g * 4, g * 4 + 4)
                for j in range(4):
                    t = g * 4 + j
                    pb.use("sync" if j % 2 == 0 else "scalar",
                           lambda e, t=t, j=j: e.dma_start(out=xt4[:, j], in_=x0[t]),
                           writes=(sl_x4[j],), dma=True)
                    pb.use("vector", lambda e, j=j, t=t, sA=sA: e.reduce_sum(
                        sA[:, t:t + 1], xt4[:, j][:, :], AX.X, negate=True),
                        reads=(sl_x4[j],), writes=(sl_sA,))
                    pb.use("scalar", lambda e, j=j, t=t, sB=sB: e.activation(
                        sqd[:], xt4[:, j][:, :], AF.Square, accum_out=sB[:, t:t + 1]),
                        reads=(sl_x4[j], sl_sqd), writes=(sl_sqd, sl_sB))
                pb.use("vector", lambda e, g4=g4, nm=nm, sA=sA: e.tensor_scalar_mul(
                    nm[:, g4], sA[:, g4], 1.0 / DM), reads=(sl_sA,), writes=(sl_nm,))
                pb.use("vector", lambda e, g4=g4, nm=nm, q4=q4: e.tensor_tensor(
                    q4[:, g4], nm[:, g4], nm[:, g4], ALU.mult), reads=(sl_nm,), writes=(sl_t4,))
                pb.use("vector", lambda e, g4=g4, sB=sB, q4=q4, q5=q5: e.scalar_tensor_tensor(
                    q5[:, g4], sB[:, g4], 1.0 / DM, q4[:, g4], ALU.mult, ALU.subtract),
                    reads=(sl_sB, sl_t4), writes=(sl_t5,))
                pb.use("scalar", lambda e, g4=g4, q4=q4, q5=q5: e.activation(
                    q4[:, g4], q5[:, g4], AF.Ln, bias=sb_cst[:, 0:1], scale=1.0),
                    reads=(sl_t5, S["consts"]), writes=(sl_t4,))
                pb.use("scalar", lambda e, g4=g4, rs=rs, q4=q4: e.activation(
                    rs[:, g4], q4[:, g4], AF.Exp, bias=0.0, scale=-0.5),
                    reads=(sl_t4,), writes=(sl_rs,))
                for j in range(4):
                    t = g * 4 + j
                    p = j % 2
                    PT = PT1 if p == 0 else PT2
                    pb.use("vector", lambda e, j=j, p=p, t=t, nm=nm, rs=rs: e.tensor_scalar(
                        a0_xn[:, p][:, :], xt4[:, j][:, :], nm[:, t:t + 1], rs[:, t:t + 1],
                        ALU.add, ALU.mult),
                        reads=(sl_x4[j], sl_nm, sl_rs), writes=(sl_xn[p],))
                    def tp(e, p=p, PT=PT):
                        for c in range(8):
                            ins = e.transpose(PT[:, c * 128:(c + 1) * 128],
                                              a0_xn[:, p][:, c * 128:(c + 1) * 128], sb_ident[:])
                        return ins
                    pb.use("tensor", tp, reads=(sl_xn[p], S["consts"]), writes=(S["pt%d" % (p + 1)],))
                    pb.use("vector", lambda e, t=t, PT=PT: e.tensor_copy(
                        xn_ap[:, :, t * 128:(t + 1) * 128],
                        PT.ap()[:, 0:1024].rearrange("p (c s) -> p c s", s=128)),
                        reads=(S["pt%d" % (p + 1)],), writes=(S["xnf%d" % (t // 4)],))
            if "A1" not in phases:
                for g in range(4):
                    emit_a0_group(g)
            if "xn" in dbg and "A1" not in phases:
                pb.barrier()
                pb.use("scalar", lambda e: e.dma_start(out=dbg_out["xn"], in_=xn_fm[:]),
                       reads=(S["xnf0"], S["xnf1"], S["xnf2"], S["xnf3"]), dma=True)
                pb.barrier()

        if "A1" in phases:
            P = a01_stk.enter_context
            a1_bo = P(nc.sbuf_tensor("a1_bo", [128, 9], F32))
            a1_ev = P(nc.sbuf_tensor("a1_ev", [128, 2, 512], BF16))
            a1_sg = P(nc.sbuf_tensor("a1_sg", [128, 2, 512], BF16))
            sl_bo = slot("a1_bo")
            sl_ev = [slot("a1_ev0"), slot("a1_ev1")]
            sl_sg = [slot("a1_sg0"), slot("a1_sg1")]
            pb.use("sync", lambda e: e.dma_start(out=a1_bo[:, 0:8], in_=bo_c), writes=(sl_bo,), dma=True)
            pb.use("vector", lambda e: e.tensor_scalar_mul(a1_bo[:, 0:8], a1_bo[:, 0:8], 0.5),
                   reads=(sl_bo,), writes=(sl_bo,))

            for st in range(4):
                emit_a0_group(st)
                for oc in range(24):
                    pi = oc % 2
                    bank, sbank = PS[pi], PSLOT[pi]
                    def mms(e, st=st, oc=oc, bank=bank):
                        for kc in range(8):
                            if oc < 16:
                                lhs = w_in_sb[:, kc * 2048 + oc * 128: kc * 2048 + oc * 128 + 128]
                                rhs = xn_ap[:, kc, st * 512:(st + 1) * 512]
                            else:
                                o2 = oc - 16
                                lhs = w_o_sb[:, kc * 1024 + o2 * 128: kc * 1024 + o2 * 128 + 128]
                                rhs = xin_ap[:, kc, st * 512:(st + 1) * 512]
                            ins = e.matmul(bank[:, 0:512], lhs, rhs,
                                           start=(kc == 0), stop=(kc == 7), skip_group_check=True)
                        return ins
                    pb.use("tensor", mms,
                           reads=(S["xnf%d" % st], sl_win) if oc < 16 else (S["x_inner"], sl_wo),
                           writes=(sbank,))
                    if oc < 8:
                        pb.use("scalar", lambda e, oc=oc, st=st, bank=bank: e.activation(
                            xin_ap[:, oc, st * 512:(st + 1) * 512], bank[:, 0:512],
                            AF.Identity, bias=0.0),
                            reads=(sbank,), writes=(S["x_inner"],))
                    else:
                        if oc < 16:
                            c2, dsth = oc - 8, gsig_h
                            bias_ap = None
                        else:
                            c2, dsth = oc - 16, og_h
                            bias_ap = a1_bo
                        ep = oc % 2
                        def tanh_op(e, bank=bank, bias_ap=bias_ap, c2=c2, ep=ep):
                            b = bias_ap[:, c2:c2 + 1] if bias_ap is not None else 0.0
                            return e.activation(a1_ev[:, ep][:, :], bank[:, 0:512], AF.Tanh,
                                                bias=b, scale=0.5)
                        pb.use("scalar", tanh_op, reads=(sbank,) + ((sl_bo,) if bias_ap is not None else ()),
                               writes=(sl_ev[ep],))
                        pb.use("vector", lambda e, ep=ep: e.tensor_scalar(
                            a1_sg[:, ep][:, :], a1_ev[:, ep][:, :], 0.5, 0.5, ALU.mult, ALU.add),
                            reads=(sl_ev[ep],), writes=(sl_sg[ep],))
                        pb.use("scalar", lambda e, dsth=dsth, c2=c2, st=st, ep=ep: e.dma_start(
                            out=dsth[c2, :, st * 512:(st + 1) * 512], in_=a1_sg[:, ep][:, :]),
                            reads=(sl_sg[ep],), writes=(S["gsig_h" if dsth is gsig_h else "og_h"],),
                            dma=True)
            pb.barrier()
            if "xinner" in dbg:
                pb.use("scalar", lambda e: e.dma_start(out=dbg_out["xinner"], in_=x_inner[:]),
                       reads=(S["x_inner"],), dma=True)
            if "gsig" in dbg:
                pb.use("scalar", lambda e: e.dma_start(out=dbg_out["gsig"], in_=gsig_h), dma=True)
            if "og" in dbg:
                pb.use("scalar", lambda e: e.dma_start(out=dbg_out["og"], in_=og_h), dma=True)
            if "xn" in dbg:
                pb.use("scalar", lambda e: e.dma_start(out=dbg_out["xn"], in_=xn_fm[:]),
                       reads=(S["xnf0"], S["xnf1"], S["xnf2"], S["xnf3"]), dma=True)
            pb.barrier()
            a1w_stk.close()
        a01_stk.close()

        if "A2" in phases:
          with ExitStack() as pctx:
            P = pctx.enter_context
            selt = P(nc.sbuf_tensor("selt", [16, 512], F32))
            selbf = P(nc.sbuf_tensor("selbf", [16, 512], BF16))
            selflat = P(nc.sbuf_tensor("selflat", [1, 16 * 512], BF16))
            rbuf = P(nc.sbuf_tensor("rbuf", [128, 16 * 512], BF16))
            rp = P(nc.sbuf_tensor("rp", [128, 2, 128], BF16))
            a2_t = P(nc.sbuf_tensor("a2_t", [128, 2, 512], F32))
            a2_g = P(nc.sbuf_tensor("a2_g", [128, DM], BF16))
            sl_selt = slot("selt"); sl_selbf = slot("selbf"); sl_selflat = slot("selflat")
            sl_rbuf = slot("rbuf")
            sl_rp = [slot("rp0"), slot("rp1")]
            sl_a2t = [slot("a2_t0"), slot("a2_t1")]
            sl_a2g = slot("a2_g")

            for st in range(4):
                def selmm(e, st=st):
                    for kc in range(8):
                        ins = e.matmul(PS[4][0:16, 0:512], sel_sb[:, kc * 16:(kc + 1) * 16],
                                       xn_ap[:, kc, st * 512:(st + 1) * 512],
                                       start=(kc == 0), stop=(kc == 7), skip_group_check=True)
                    return ins
                pb.use("tensor", selmm, reads=(S["xnf%d" % st], sl_selw), writes=(S["ps4"],))
                pb.use("scalar", lambda e: e.activation(selt[0:16, :], PS[4][0:16, 0:512],
                                                        AF.Tanh, bias=0.0, scale=0.5),
                       reads=(S["ps4"],), writes=(sl_selt,))
                pb.use("vector", lambda e: e.tensor_scalar(selbf[0:16, :], selt[0:16, :],
                                                           0.5, 0.5, ALU.mult, ALU.add),
                       reads=(sl_selt,), writes=(sl_selbf,))
                pb.use("gpsimd", lambda e: e.dma_start(out=selflat[0:1, 0:8192], in_=selbf[0:16, 0:512]),
                       reads=(sl_selbf,), writes=(sl_selflat,), dma=True)
                for ex in range(16):
                    pi = ex % 2
                    def scmm(e, st=st, ex=ex, pi=pi):
                        for kc in range(8):
                            ins = e.matmul(PS[pi][:, 0:512],
                                           keys_sb[:, ex * 1024 + kc * 128: ex * 1024 + kc * 128 + 128],
                                           xn_ap[:, kc, st * 512:(st + 1) * 512],
                                           start=(kc == 0), stop=(kc == 7), skip_group_check=True)
                        return ins
                    pb.use("tensor", scmm, reads=(S["xnf%d" % st], sl_keys), writes=(PSLOT[pi],))
                    pb.use("tensor", lambda e, ex=ex, pi=pi: e.matmul(
                        PS[2 + pi][:, 0:512], sb_ones[0:1, 0:128],
                        selflat[0:1, ex * 512: ex * 512 + 512],
                        start=True, stop=True, skip_group_check=True),
                        reads=(sl_selflat, S["consts"]), writes=(S["ps%d" % (2 + pi)],))
                    pb.use("scalar", lambda e, ex=ex, pi=pi: e.activation(
                        rbuf[:, ex * 512:(ex + 1) * 512], PS[pi][:, 0:512], AF.Relu),
                        reads=(PSLOT[pi],), writes=(sl_rbuf,))
                    pb.use("vector", lambda e, ex=ex, pi=pi: e.tensor_tensor(
                        rbuf[:, ex * 512:(ex + 1) * 512], rbuf[:, ex * 512:(ex + 1) * 512],
                        PS[2 + pi][:, 0:512], ALU.mult),
                        reads=(sl_rbuf, S["ps%d" % (2 + pi)]), writes=(sl_rbuf,))
                for i in range(4):
                    tg = st * 4 + i
                    pA, pB2 = (4, 5) if i % 2 == 0 else (2, 3)
                    def vmm_all(e, i=i, pA=pA, pB2=pB2):
                        for ex in range(16):
                            lh = rbuf[:, ex * 512 + i * 128: ex * 512 + i * 128 + 128]
                            e.matmul(PS[pA][:, 0:512], lh, vals_sb[:, ex * 1024: ex * 1024 + 512],
                                     start=(ex == 0), stop=(ex == 15), skip_group_check=True)
                            ins = e.matmul(PS[pB2][:, 0:512], lh,
                                           vals_sb[:, ex * 1024 + 512: ex * 1024 + 1024],
                                           start=(ex == 0), stop=(ex == 15), skip_group_check=True)
                        return ins
                    pb.use("tensor", vmm_all, reads=(sl_rbuf, sl_vals),
                           writes=(S["ps%d" % pA], S["ps%d" % pB2]))
                    for v, pv in enumerate((pA, pB2)):
                        pb.use("scalar", lambda e, v=v, pv=pv: e.activation(
                            a2_t[:, v], PS[pv][:, 0:512], AF.Tanh, bias=0.0, scale=0.5),
                            reads=(S["ps%d" % pv],), writes=(sl_a2t[v],))
                        pb.use("vector", lambda e, v=v: e.tensor_scalar(
                            a2_t[:, v], a2_t[:, v], 0.5, 0.5, ALU.mult, ALU.add),
                            reads=(sl_a2t[v],), writes=(sl_a2t[v],))
                        pb.use("vector", lambda e, v=v, pv=pv: e.tensor_tensor(
                            a2_g[:, v * 512:(v + 1) * 512], a2_t[:, v], PS[pv][:, 0:512], ALU.mult),
                            reads=(sl_a2t[v], S["ps%d" % pv]), writes=(sl_a2g,))
                    pb.use("scalar", lambda e, tg=tg: e.dma_start(out=g0_h[tg], in_=a2_g[:]),
                           reads=(sl_a2g,), writes=(S["g0_h"],), dma=True)
            pb.barrier()
            if "g0" in dbg:
                pb.use("scalar", lambda e: e.dma_start(out=dbg_out["g0"], in_=g0_h), dma=True)
                pb.barrier()
          a2w_stk.close()

        a1w_stk.close()
        a2w_stk.close()
        xn_stk.close()

        if "B" in phases:
          with ExitStack() as pctx:
            P = pctx.enter_context
            sb_maskat = P(nc.sbuf_tensor("sb_maskat", [128, 128], BF16))
            sb_maskat4 = P(nc.sbuf_tensor("sb_maskat4", [128, 512], BF16))
            sb_ind = P(nc.sbuf_tensor("sb_ind", [16, 8 * 128], BF16))
            sb_bones = P(nc.sbuf_tensor("sb_bones", [128, 8 * 16], BF16))
            slot("bconst")
            for dst, src in ((sb_maskat, mask_at), (sb_ind, ind_all), (sb_bones, bones)):
                pb.use("sync", lambda e, dst=dst, src=src: e.dma_start(out=dst[:], in_=src),
                       writes=(S["bconst"],), dma=True)
            wq_sb = P(nc.sbuf_tensor("wq_sb", [128, 8 * 1024], BF16))
            wk_sb = P(nc.sbuf_tensor("wk_sb", [128, 8 * 1024], BF16))
            wv_sb = P(nc.sbuf_tensor("wv_sb", [128, 8 * 1024], BF16))
            wi_sb = P(nc.sbuf_tensor("wi_sb", [128, 8 * 1024], BF16))
            wf_sb = P(nc.sbuf_tensor("wf_sb", [128, 8 * 1024], BF16))
            bcol = P(nc.sbuf_tensor("bcol", [128, 16], F32))
            nwc = P(nc.sbuf_tensor("nwc", [128, 2], F32))
            qhat = P(nc.sbuf_tensor("qhat", [128, 8 * 256], BF16))
            khat = P(nc.sbuf_tensor("khat", [128, 8 * 256], BF16))
            vbf = P(nc.sbuf_tensor("vbf", [128, 8 * 256], BF16))
            ibf = P(nc.sbuf_tensor("ibf", [128, 8 * 256], BF16))
            qraw = P(nc.sbuf_tensor("qraw", [128, 8 * 256], BF16))
            rstdb = P(nc.sbuf_tensor("rstdb", [16, 256], BF16))
            rstdf = P(nc.sbuf_tensor("rstdf", [16, 256], F32))
            fpre = P(nc.sbuf_tensor("fpre", [128, 8 * 256], F32))
            fb1 = P(nc.sbuf_tensor("fb1", [128, 8 * 256], F32))
            lfb = P(nc.sbuf_tensor("lfb", [128, 8 * 256], F32))
            cumf = P(nc.sbuf_tensor("cumf", [128, 8 * 128], F32))
            eqb = P(nc.sbuf_tensor("eqb", [128, 8 * 128], BF16))
            enb = P(nc.sbuf_tensor("enb", [128, 8 * 128], BF16))
            qtl = P(nc.sbuf_tensor("qtl", [128, 8 * 128], BF16))
            qtlz = P(nc.sbuf_tensor("qtlz", [128, 2, 8 * 128], BF16))
            ktl = P(nc.sbuf_tensor("ktl", [128, 8 * 128], BF16))
            ktok = P(nc.sbuf_tensor("ktok", [128, 8 * 128], BF16))
            vaug = P(nc.sbuf_tensor("vaug", [128, 16 * 65], BF16))
            asb = P(nc.sbuf_tensor("asb", [128, 16 * 128], BF16))
            call = P(nc.sbuf_tensor("call", [128, 8 * 65], F32))
            cbfz = P(nc.sbuf_tensor("cbfz", [128, 2, 8 * 65], BF16))
            fend = P(nc.sbuf_tensor("fend", [128, 2, 8], F32))
            dens = P(nc.sbuf_tensor("dens", [128, 16], F32))
            rden = P(nc.sbuf_tensor("rden", [128, 16], F32))
            htok = P(nc.sbuf_tensor("htok", [128, 1024], BF16))
            ogs = P(nc.sbuf_tensor("ogs", [128, 8 * 128], BF16))
            for nm in ("wq", "wk", "wv", "wi", "wf", "bcol", "nwc", "qhat", "khat",
                       "vbf", "ibf", "qraw", "sqb", "rstdb", "rstdf", "fpre", "fb1", "lfb",
                       "cumf", "eqb", "enb", "qtl", "qtlz0", "qtlz1", "ktl", "ktok", "vaug", "asb",
                       "call", "cbfz", "fend", "dens", "rden", "htok", "ogs", "hgf"):
                slot(nm)
            for w, src, sn in ((wf_sb, cw_f, "wf"), (wq_sb, cw_q, "wq"), (wk_sb, cw_k, "wk"),
                               (wv_sb, cw_v, "wv"), (wi_sb, cw_i, "wi")):
                pb.use("gpsimd", lambda e, w=w, src=src: e.dma_start(out=w[:], in_=src),
                       writes=(S[sn],), dma=True)
            pb.use("sync", lambda e: e.dma_start(out=bcol[:, 0:8], in_=bi_c), writes=(S["bcol"],), dma=True)
            pb.use("sync", lambda e: e.dma_start(out=bcol[:, 8:16], in_=bf_c), writes=(S["bcol"],), dma=True)
            pb.use("sync", lambda e: e.dma_start(out=nwc[:, 0:1], in_=qnw_c), writes=(S["nwc"],), dma=True)
            pb.use("sync", lambda e: e.dma_start(out=nwc[:, 1:2], in_=knw_c), writes=(S["nwc"],), dma=True)
            pb.use("vector", lambda e: e.memset(vaug[:], 1.0), writes=(S["vaug"],))
            def mk_mask4(e):
                for c in range(4):
                    ins = e.tensor_copy(sb_maskat4[:, c * 128:(c + 1) * 128], sb_maskat[:])
                return ins
            pb.use("vector", mk_mask4, reads=(S["bconst"],), writes=(S["bconst"],))
            pb.use("vector", lambda e: e.memset(qtlz[:], 0.0), writes=(S["qtlz0"], S["qtlz1"]))
            pb.use("vector", lambda e: e.memset(call[:], 0.0), writes=(S["call"],))
            pb.use("vector", lambda e: e.memset(cbfz[:], 0.0), writes=(S["cbfz"],))

            qhat3 = qhat.ap().rearrange("p (c s) -> p c s", s=256)
            khat3 = khat.ap().rearrange("p (c s) -> p c s", s=256)
            vbf3 = vbf.ap().rearrange("p (c s) -> p c s", s=256)
            ibf3 = ibf.ap().rearrange("p (c s) -> p c s", s=256)
            c65 = call.ap().rearrange("p (b c) -> p b c", c=65)

            vaug65 = vaug.ap().rearrange("p (b c) -> p b c", c=65)
            eq3 = eqb.ap().rearrange("p (c s) -> p c s", s=128)
            HOUT = [PS[0], PS[1], PS[4], PS[5]]
            HSL = [S["ps0"], S["ps1"], S["ps4"], S["ps5"]]
            AT_ORDER = [0, 2, 4, 6, 8, 10, 12, 14, 1, 3, 5, 7, 9, 11, 13, 15]
            AT_POS = [AT_ORDER.index(h) for h in range(16)]

            def proj_norm(wsb, wsl, st, dst, dstsl, nwslice):
                for oc in range(8):
                    pi = oc % 2
                    def mm(e, oc=oc, st=st, wsb=wsb, pi=pi):
                        for kc in range(8):
                            ins = e.matmul(PS[pi][:, 0:256],
                                           wsb[:, kc * 1024 + oc * 128: kc * 1024 + oc * 128 + 128],
                                           xin_ap[:, kc, st * 256:(st + 1) * 256],
                                           start=(kc == 0), stop=(kc == 7), skip_group_check=True)
                        return ins
                    pb.use("tensor", mm, reads=(S["x_inner"], wsl), writes=(PSLOT[pi],))
                    pb.use("vector", lambda e, oc=oc, pi=pi: e.tensor_copy(
                        qraw[:, oc * 256:(oc + 1) * 256], PS[pi][:, 0:256]),
                        reads=(PSLOT[pi],), writes=(S["qraw"],))
                pb.use("vector", lambda e: e.tensor_tensor(asb[:], qraw[:], qraw[:], ALU.mult),
                       reads=(S["qraw"],), writes=(S["asb"],))
                def statmm(e):
                    for kc in range(8):
                        ins = e.matmul(PS[2][0:16, 0:256], sb_bones[:, kc * 16:(kc + 1) * 16],
                                       asb[:, kc * 256:(kc + 1) * 256],
                                       start=(kc == 0), stop=(kc == 7), skip_group_check=True)
                    return ins
                pb.use("tensor", statmm, reads=(S["asb"], S["bconst"]), writes=(S["ps2"],))
                pb.use("scalar", lambda e: e.activation(rstdf[0:16, 0:256], PS[2][0:16, 0:256],
                                                        AF.Ln, bias=sb_cst[0:16, 1:2], scale=1.0 / 64),
                       reads=(S["ps2"], S["consts"]), writes=(S["rstdf"],))
                pb.use("scalar", lambda e: e.activation(rstdb[0:16, 0:256], rstdf[0:16, 0:256],
                                                        AF.Exp, bias=0.0, scale=-0.5),
                       reads=(S["rstdf"],), writes=(S["rstdb"],))
                for oc in range(8):
                    pi = oc % 2
                    pb.use("tensor", lambda e, oc=oc, pi=pi: e.matmul(
                        PS[2 + pi][:, 0:256], sb_ind[:, oc * 128:(oc + 1) * 128],
                        rstdb[0:16, 0:256], start=True, stop=True, skip_group_check=True),
                        reads=(S["rstdb"], S["bconst"]), writes=(S["ps%d" % (2 + pi)],))
                    pb.use("vector", lambda e, oc=oc, pi=pi, dst=dst, nwslice=nwslice: e.scalar_tensor_tensor(
                        dst[:, oc * 256:(oc + 1) * 256], qraw[:, oc * 256:(oc + 1) * 256],
                        nwslice, PS[2 + pi][:, 0:256], ALU.mult, ALU.mult),
                        reads=(S["qraw"], S["nwc"], S["ps%d" % (2 + pi)]), writes=(dstsl,))

            for st in range(8):
                for oc in range(8):
                    pi = 2 + oc % 2
                    def mm(e, oc=oc, st=st, pi=pi):
                        for kc in range(8):
                            ins = e.matmul(PS[pi][:, 0:256],
                                           wf_sb[:, kc * 1024 + oc * 128: kc * 1024 + oc * 128 + 128],
                                           xin_ap[:, kc, st * 256:(st + 1) * 256],
                                           start=(kc == 0), stop=(kc == 7), skip_group_check=True)
                        return ins
                    pb.use("tensor", mm, reads=(S["x_inner"], S["wf"]), writes=(S["ps%d" % pi],))
                    pb.use("scalar", lambda e, oc=oc, pi=pi: e.activation(
                        fpre[:, oc * 256:(oc + 1) * 256], PS[pi][:, 0:256],
                        AF.Identity, bias=bcol[:, 8 + oc:9 + oc], scale=1.0),
                        reads=(S["ps%d" % pi], S["bcol"]), writes=(S["fpre"],))
                pb.use("scalar", lambda e: e.activation(fb1[:], fpre[:], AF.Abs),
                       reads=(S["fpre"],), writes=(S["fb1"],))
                pb.use("scalar", lambda e: e.activation(lfb[:], fb1[:], AF.Exp, bias=0.0, scale=-1.0),
                       reads=(S["fb1"],), writes=(S["lfb"],))
                pb.use("scalar", lambda e: e.activation(fb1[:], lfb[:], AF.Ln, bias=sb_cst[:, 2:3], scale=1.0),
                       reads=(S["lfb"], S["consts"]), writes=(S["fb1"],))
                pb.use("vector", lambda e: e.tensor_scalar(lfb[:], fpre[:], 0.0, None, ALU.min),
                       reads=(S["fpre"],), writes=(S["lfb"],))
                pb.use("vector", lambda e: e.tensor_tensor(lfb[:], lfb[:], fb1[:], ALU.subtract),
                       reads=(S["lfb"], S["fb1"]), writes=(S["lfb"],))

                def emit_prep(i, st=st):
                    if st * 2 + i >= B_TILES:
                        return
                    def scans(e, i=i):
                        for kc in range(8):
                            for cc in range(2):
                                ins = e.tensor_tensor_scan(
                                    cumf[:, kc * 128 + cc * 64: kc * 128 + cc * 64 + 64],
                                    lfb[:, kc * 256 + i * 128 + cc * 64: kc * 256 + i * 128 + cc * 64 + 64],
                                    sb_zeros[:, 0:64], 0.0, ALU.add, ALU.add)
                        return ins
                    pb.use("vector", scans, reads=(S["lfb"], S["consts"]), writes=(S["cumf"],))
                    if B_STAGE < 1:
                        return
                    def eewave(e):
                        e.activation(eqb[:], cumf[:], AF.Exp, bias=0.0, scale=1.0)
                        return e.activation(enb[:], cumf[:], AF.Exp, bias=0.0, scale=-1.0)
                    pb.use("scalar", eewave, reads=(S["cumf"],), writes=(S["eqb"], S["enb"]))

                emit_prep(0)
                proj_norm(wq_sb, S["wq"], st, qhat, S["qhat"], nwc[:, 0:1])
                proj_norm(wk_sb, S["wk"], st, khat, S["khat"], nwc[:, 1:2])
                for which, wsb, wsl in (("v", wv_sb, S["wv"]), ("i", wi_sb, S["wi"])):
                    for oc in range(8):
                        pi = oc % 2
                        def mm(e, oc=oc, st=st, wsb=wsb, pi=pi):
                            for kc in range(8):
                                ins = e.matmul(PS[pi][:, 0:256],
                                               wsb[:, kc * 1024 + oc * 128: kc * 1024 + oc * 128 + 128],
                                               xin_ap[:, kc, st * 256:(st + 1) * 256],
                                               start=(kc == 0), stop=(kc == 7), skip_group_check=True)
                            return ins
                        pb.use("tensor", mm, reads=(S["x_inner"], wsl), writes=(PSLOT[pi],))
                        if which == "v":
                            pb.use("vector", lambda e, oc=oc, pi=pi: e.tensor_copy(
                                vbf[:, oc * 256:(oc + 1) * 256], PS[pi][:, 0:256]),
                                reads=(PSLOT[pi],), writes=(S["vbf"],))
                        else:
                            pb.use("scalar", lambda e, oc=oc, pi=pi: e.activation(
                                ibf[:, oc * 256:(oc + 1) * 256], PS[pi][:, 0:256],
                                AF.Exp, bias=bcol[:, oc:oc + 1], scale=1.0),
                                reads=(PSLOT[pi], S["bcol"]), writes=(S["ibf"],))

                for i in range(2):
                    T0 = st * 256 + i * 128
                    if st * 2 + i >= B_TILES:
                        continue
                    if B_STAGE < 1:
                        continue
                    def qkwave(e, i=i):
                        e.tensor_tensor(qtl.ap().rearrange("p (c s) -> p c s", s=128),
                                        qhat3[:, :, i * 128:(i + 1) * 128], eq3, ALU.mult)
                        e.tensor_tensor(ktl.ap().rearrange("p (c s) -> p c s", s=128),
                                        khat3[:, :, i * 128:(i + 1) * 128],
                                        ibf3[:, :, i * 128:(i + 1) * 128], ALU.mult)
                        for cc2 in range(2):
                            e.tensor_copy(
                                qtlz.ap()[:, cc2].rearrange("p (c s) -> p c s", s=128)[:, :, cc2 * 64:(cc2 + 1) * 64],
                                qtl.ap().rearrange("p (c s) -> p c s", s=128)[:, :, cc2 * 64:(cc2 + 1) * 64])
                        for cc3 in range(2):
                            e.tensor_copy(
                                fend.ap()[:, cc3].rearrange("p (b c) -> p b c", c=1),
                                eq3[:, :, cc3 * 64 + 63: cc3 * 64 + 64])
                        return e.tensor_tensor(
                            ktl.ap().rearrange("p (c s) -> p c s", s=128),
                            ktl.ap().rearrange("p (c s) -> p c s", s=128),
                            enb.ap().rearrange("p (c s) -> p c s", s=128), ALU.mult)
                    pb.use("vector", qkwave,
                           reads=(S["qhat"], S["eqb"], S["khat"], S["ibf"], S["enb"]),
                           writes=(S["qtl"], S["qtlz0"], S["qtlz1"], S["ktl"], S["fend"]))
                    if i == 0:
                        emit_prep(1)
                    if B_STAGE < 2:
                        continue
                    def tpk(e):
                        for c in range(8):
                            ins = e.transpose(PT1[:, c * 128:(c + 1) * 128],
                                              ktl[:, c * 128:(c + 1) * 128], sb_ident[:])
                        return ins
                    pb.use("tensor", tpk, reads=(S["ktl"], S["consts"]), writes=(S["pt1"],))
                    pb.use("vector", lambda e: e.tensor_copy(ktok[:], PT1.ap()[:, 0:1024]),
                           reads=(S["pt1"],), writes=(S["ktok"],))
                    def tpv(e, i=i):
                        for c in range(8):
                            ins = e.transpose(PT2[:, c * 128:(c + 1) * 128],
                                              vbf[:, c * 256 + i * 128: c * 256 + i * 128 + 128],
                                              sb_ident[:])
                        return ins
                    pb.use("tensor", tpv, reads=(S["vbf"], S["consts"]), writes=(S["pt2"],))
                    pb.use("vector", lambda e: e.tensor_copy(
                        vaug65[:, :, 0:64],
                        PT2.ap()[:, 0:1024].rearrange("p (b c) -> p b c", c=64)),
                        reads=(S["pt2"],), writes=(S["vaug"],))
                    if B_STAGE < 3:
                        continue
                    for g in range(4):
                        pg = 2 + g % 2
                        def atmm(e, g=g, pg=pg):
                            for j in range(4):
                                h = AT_ORDER[g * 4 + j]
                                rows = slice(64 * (h % 2), 64 * (h % 2) + 64)
                                cols = slice((h // 2) * 128, (h // 2) * 128 + 128)
                                ins = e.matmul(PS[pg][:, j * 128:(j + 1) * 128],
                                               ktl[rows, cols], qtl[rows, cols],
                                               start=(j == 0), stop=(j == 3),
                                               skip_group_check=True)
                            return ins
                        pb.use("tensor", atmm, reads=(S["ktl"], S["qtl"]),
                               writes=(S["ps%d" % pg],))
                        pb.use("vector", lambda e, g=g, pg=pg: e.tensor_tensor(
                            asb[:, g * 512:(g + 1) * 512], PS[pg][:, 0:512],
                            sb_maskat4[:], ALU.mult),
                            reads=(S["ps%d" % pg], S["bconst"]), writes=(S["asb"],))
                    if B_STAGE < 4:
                        continue
                    for bk in range(4):
                        def intmm(e, bk=bk):
                            for bl in range(4):
                                h = bk * 4 + bl
                                ins = e.matmul(HOUT[bk][:, bl * 128: bl * 128 + 65],
                                               asb[:, AT_POS[h] * 128:AT_POS[h] * 128 + 128],
                                               vaug[:, h * 65:(h + 1) * 65],
                                               start=(bl == 0), stop=False, skip_group_check=True)
                            return ins
                        pb.use("tensor", intmm, reads=(S["asb"], S["vaug"]), writes=(HSL[bk],))
                    if B_STAGE < 5:
                        continue
                    for cc in range(2 if B_SUB >= 4 else 1):
                        def intermm(e, cc=cc):
                            for h in range(16):
                                bk, bl = h // 4, h % 4
                                hcols = slice((h // 2) * 128, (h // 2) * 128 + 128)
                                ins = e.matmul(HOUT[bk][:, bl * 128: bl * 128 + 65],
                                               qtlz[:, cc][:, hcols],
                                               cbfz[:, h % 2][:, (h // 2) * 65:(h // 2) * 65 + 65],
                                               start=False, stop=(cc == 1), skip_group_check=True)
                            return ins
                        pb.use("tensor", intermm, reads=(S["qtlz%d" % cc], S["cbfz"]),
                               writes=(HSL[0], HSL[1], HSL[2], HSL[3]))
                        if B_SUB >= 2:
                            for par in range(2):
                                def dmm(e, cc=cc, par=par):
                                    for h2 in range(8):
                                        h = h2 * 2 + par
                                        rows = slice(64 * par, 64 * par + 64)
                                        e.matmul(PS[2][rows, (h // 2) * 64: (h // 2) * 64 + 64],
                                                 ktok[cc * 64:(cc + 1) * 64, h * 64:(h + 1) * 64],
                                                 vaug[cc * 64:(cc + 1) * 64, h * 65: h * 65 + 64],
                                                 start=True, stop=True, skip_group_check=True)
                                        ins = e.matmul(PS[3][rows, (h // 2) + 8 * cc: (h // 2) + 8 * cc + 1],
                                                       ktok[cc * 64:(cc + 1) * 64, h * 64:(h + 1) * 64],
                                                       vaug[cc * 64:(cc + 1) * 64, h * 65 + 64: h * 65 + 65],
                                                       start=True, stop=True, skip_group_check=True)
                                    return ins
                                pb.use("tensor", dmm, reads=(S["ktok"], S["vaug"]),
                                       writes=(S["ps2"], S["ps3"]))
                        if B_SUB < 3:
                            continue
                        def supd1(e, cc=cc):
                            e.tensor_tensor(c65[:, :, 0:64], c65[:, :, 0:64],
                                            PS[2].ap()[:, 0:512].rearrange("p (b c) -> p b c", c=64),
                                            ALU.add)
                            return e.tensor_tensor(
                                c65[:, :, 64:65], c65[:, :, 64:65],
                                PS[3].ap()[:, 8 * cc: 8 * cc + 8].rearrange("p (b c) -> p b c", c=1),
                                ALU.add)
                        pb.use("vector", supd1, reads=(S["ps2"], S["ps3"], S["call"]),
                               writes=(S["call"],))
                        def supd2(e, cc=cc):
                            for b in range(8):
                                e.tensor_scalar(call[:, b * 65:(b + 1) * 65],
                                                call[:, b * 65:(b + 1) * 65],
                                                fend[:, cc][:, b:b + 1], None, ALU.mult)
                            e.tensor_copy(cbfz[:, 0][0:64, :], call[0:64, :])
                            return e.tensor_copy(cbfz[:, 1][64:128, :], call[64:128, :])
                        pb.use("vector", supd2, reads=(S["call"], S["fend"]),
                               writes=(S["call"], S["cbfz"]))
                    if B_STAGE < 6:
                        continue
                    def fdens(e):
                        for bk in range(4):
                            ins = e.tensor_scalar(
                                dens.ap()[:, bk * 4:(bk + 1) * 4].rearrange("p (b c) -> p b c", c=1),
                                HOUT[bk].ap()[:, 0:512].rearrange("p (b c) -> p b c", c=128)[:, :, 64:65],
                                1.0, None, ALU.max)
                        return ins
                    pb.use("vector", fdens, reads=(HSL[0], HSL[1], HSL[2], HSL[3]),
                           writes=(S["dens"],))
                    pb.use("vector", lambda e: e.reciprocal(rden[:], dens[:]),
                           reads=(S["dens"],), writes=(S["rden"],))
                    def fhtok(e):
                        for h in range(16):
                            bk, bl = h // 4, h % 4
                            ins = e.tensor_scalar(
                                htok[:, h * 64:(h + 1) * 64], HOUT[bk][:, bl * 128: bl * 128 + 64],
                                rden[:, h:h + 1], None, ALU.mult)
                        return ins
                    pb.use("vector", fhtok, reads=(HSL[0], HSL[1], HSL[2], HSL[3], S["rden"]),
                           writes=(S["htok"],))
                    if B_STAGE < 7:
                        continue
                    def tph(e):
                        for c in range(8):
                            ins = e.transpose(PT1[:, c * 128:(c + 1) * 128],
                                              htok[:, c * 128:(c + 1) * 128], sb_ident[:])
                        return ins
                    pb.use("tensor", tph, reads=(S["htok"], S["consts"]), writes=(S["pt1"],))
                    pb.use("sync", lambda e, T0=T0: e.dma_start(
                        out=ogs[:], in_=og_h[:, :, T0:T0 + 128].rearrange("o p f -> p o f")),
                        reads=(S["og_h"],), writes=(S["ogs"],), dma=True)
                    pb.use("vector", lambda e: e.tensor_tensor(ogs[:], PT1.ap()[:, 0:1024], ogs[:], ALU.mult),
                           reads=(S["pt1"], S["ogs"]), writes=(S["ogs"],))
                    pb.use("scalar", lambda e, T0=T0: e.dma_start(
                        out=hg_h[:, :, T0:T0 + 128].rearrange("o p f -> p o f"), in_=ogs[:]),
                        reads=(S["ogs"],), writes=(S["hg_h"],), dma=True)
            pb.barrier()
            if "hg" in dbg:
                pb.use("scalar", lambda e: e.dma_start(out=dbg_out["hg"], in_=hg_h), dma=True)
                pb.barrier()
        xin_stk.close()

        if "D0" in phases:
            xn1_fm = E(nc.sbuf_tensor("xn1_fm", [128, 8 * SEQ], BF16))
            xn1_ap = xn1_fm.ap().rearrange("p (c s) -> p c s", s=SEQ)
            for _n in ("xn1f0", "xn1f1", "xn1f2", "xn1f3"):
                slot(_n)
        if "C" in phases:
          with ExitStack() as pctx:
            P = pctx.enter_context
            wout_sb = P(nc.sbuf_tensor("wout_sb", [128, 8 * 1024], BF16))
            slot("wout")
            pb.use("gpsimd", lambda e: e.dma_start(out=wout_sb[:], in_=cw_out),
                   writes=(S["wout"],), dma=True)
            if "D0" in phases:
                gw_sb = P(nc.sbuf_tensor("gw_sb", [128, 8 * 8], BF16))
                gb_sb = P(nc.sbuf_tensor("gb_sb", [128, 8], F32))
                d0_xn = P(nc.sbuf_tensor("d0_xn", [128, 2, DM], BF16))
                d0_sq = P(nc.sbuf_tensor("d0_sq", [128, 2, DM], F32))
                d0_st = P(nc.sbuf_tensor("d0_st", [128, 6, 16], F32))
                d0_r = P(nc.sbuf_tensor("d0_r", [128, 4, 32], F32))
                for nm in ("gw", "gb", "d0_xn0", "d0_xn1", "d0_sq", "d0_r",
                           "d0_sA", "d0_sB", "d0_nm", "d0_rs", "d0_t4", "d0_t5"):
                    slot(nm)
                pb.use("gpsimd", lambda e: e.dma_start(out=gw_sb[:], in_=cgw), writes=(S["gw"],), dma=True)
                pb.use("sync", lambda e: e.dma_start(out=gb_sb[:], in_=gate_b_bc), writes=(S["gb"],), dma=True)
            hg_ld = P(nc.sbuf_tensor("hg_ld", [128, 8 * 512], BF16))
            gs_ld = P(nc.sbuf_tensor("gs_ld", [128, 8 * 512], BF16))
            hgm = P(nc.sbuf_tensor("hgm", [128, 8 * 512], BF16))
            s_bf = P(nc.sbuf_tensor("s_bf", [128, 8 * 512], BF16))
            g0_ld = P(nc.sbuf_tensor("g0_ld", [128, 2, DM], BF16))
            x0_ld = P(nc.sbuf_tensor("x0_ld", [128, 2, DM], F32))
            sgf = P(nc.sbuf_tensor("sgf", [128, DM], F32))
            x1b = P(nc.sbuf_tensor("x1b", [128, 2, DM], F32))
            for nm in ("wout", "hg_ld", "gs_ld", "hgm", "s_bf", "g0_ld0", "g0_ld1",
                       "x0_ld0", "x0_ld1", "sgf", "x1b0", "x1b1"):
                slot(nm)
            for st in range(4):
                pb.use("sync", lambda e, st=st: e.dma_start(
                    out=hg_ld.ap().rearrange("p (c s) -> p c s", s=512),
                    in_=hg_h[:, :, st * 512:(st + 1) * 512].rearrange("o p f -> p o f")),
                    reads=(S["hg_h"],), writes=(S["hg_ld"],), dma=True)
                pb.use("sync", lambda e, st=st: e.dma_start(
                    out=gs_ld.ap().rearrange("p (c s) -> p c s", s=512),
                    in_=gsig_h[:, :, st * 512:(st + 1) * 512].rearrange("o p f -> p o f")),
                    reads=(S["gsig_h"],), writes=(S["gs_ld"],), dma=True)
                pb.use("vector", lambda e: e.tensor_tensor(hgm[:], hg_ld[:], gs_ld[:], ALU.mult),
                       reads=(S["hg_ld"], S["gs_ld"]), writes=(S["hgm"],))
                for oc in range(8):
                    pi = oc % 2
                    def mm(e, oc=oc, pi=pi):
                        for kc in range(8):
                            ins = e.matmul(PS[pi][:, 0:512],
                                           wout_sb[:, kc * 1024 + oc * 128: kc * 1024 + oc * 128 + 128],
                                           hgm[:, kc * 512:(kc + 1) * 512],
                                           start=(kc == 0), stop=(kc == 7), skip_group_check=True)
                        return ins
                    pb.use("tensor", mm, reads=(S["hgm"], S["wout"]), writes=(PSLOT[pi],))
                    pb.use("vector", lambda e, oc=oc, pi=pi: e.tensor_copy(
                        s_bf[:, oc * 512:(oc + 1) * 512], PS[pi][:, 0:512]),
                        reads=(PSLOT[pi],), writes=(S["s_bf"],))
                for i in range(4):
                    tg = st * 4 + i
                    p2 = i % 2
                    def tps(e, i=i):
                        for c in range(8):
                            ins = e.transpose(PT1[:, c * 128:(c + 1) * 128],
                                              s_bf[:, c * 512 + i * 128: c * 512 + i * 128 + 128],
                                              sb_ident[:])
                        return ins
                    pb.use("tensor", tps, reads=(S["s_bf"], S["consts"]), writes=(S["pt1"],))
                    pb.use("sync", lambda e, tg=tg, p2=p2: e.dma_start(out=g0_ld[:, p2], in_=g0_h[tg]),
                           reads=(S["g0_h"],), writes=(S["g0_ld%d" % p2],), dma=True)
                    pb.use("sync", lambda e, tg=tg, p2=p2: e.dma_start(out=x0_ld[:, p2], in_=x0[tg]),
                           writes=(S["x0_ld%d" % p2],), dma=True)
                    pb.use("vector", lambda e, p2=p2: e.tensor_tensor(
                        sgf[:], PT1.ap()[:, 0:1024], g0_ld[:, p2], ALU.mult),
                        reads=(S["pt1"], S["g0_ld%d" % p2]), writes=(S["sgf"],))
                    pb.use("vector", lambda e, p2=p2: e.tensor_tensor(
                        x1b[:, p2], sgf[:], x0_ld[:, p2], ALU.add),
                        reads=(S["sgf"], S["x0_ld%d" % p2]), writes=(S["x1b%d" % p2],))
                    pb.use("scalar", lambda e, tg=tg, p2=p2: e.dma_start(out=x1_h[tg], in_=x1b[:, p2]),
                           reads=(S["x1b%d" % p2],), writes=(S["x1_h"],), dma=True)
                    if "D0" not in phases:
                        continue
                    t = tg
                    sAd = d0_st.ap()[:, 0]; sBd = d0_st.ap()[:, 1]
                    pb.use("vector", lambda e, t=t, p2=p2, sAd=sAd: e.reduce_sum(
                        sAd[:, t:t + 1], x1b[:, p2][:, :], AX.X, negate=True),
                        reads=(S["x1b%d" % p2],), writes=(S["d0_sA"],))
                    pb.use("scalar", lambda e, t=t, p2=p2, sBd=sBd: e.activation(
                        d0_sq[:, p2][:, :], x1b[:, p2][:, :], AF.Square, accum_out=sBd[:, t:t + 1]),
                        reads=(S["x1b%d" % p2], S["d0_sq"]), writes=(S["d0_sq"], S["d0_sB"]))
            if "D0" in phases:
                sAd = d0_st.ap()[:, 0]; sBd = d0_st.ap()[:, 1]
                nm1 = d0_st.ap()[:, 2]; rs1 = d0_st.ap()[:, 3]
                t4d = d0_st.ap()[:, 4]; t5d = d0_st.ap()[:, 5]
                pb.use("vector", lambda e, nm1=nm1, sAd=sAd: e.tensor_scalar_mul(
                    nm1[:], sAd[:], 1.0 / DM), reads=(S["d0_sA"],), writes=(S["d0_nm"],))
                pb.use("vector", lambda e, nm1=nm1, t4d=t4d: e.tensor_tensor(
                    t4d[:], nm1[:], nm1[:], ALU.mult), reads=(S["d0_nm"],), writes=(S["d0_t4"],))
                pb.use("vector", lambda e, sBd=sBd, t4d=t4d, t5d=t5d: e.scalar_tensor_tensor(
                    t5d[:], sBd[:], 1.0 / DM, t4d[:], ALU.mult, ALU.subtract),
                    reads=(S["d0_sB"], S["d0_t4"]), writes=(S["d0_t5"],))
                pb.use("scalar", lambda e, t4d=t4d, t5d=t5d: e.activation(
                    t4d[:], t5d[:], AF.Ln, bias=sb_cst[:, 0:1], scale=1.0),
                    reads=(S["d0_t5"], S["consts"]), writes=(S["d0_t4"],))
                pb.use("scalar", lambda e, rs1=rs1, t4d=t4d: e.activation(
                    rs1[:], t4d[:], AF.Exp, bias=0.0, scale=-0.5),
                    reads=(S["d0_t4"],), writes=(S["d0_rs"],))
                for st in range(4):
                    for i in range(4):
                        t = st * 4 + i
                        p = t % 2
                        PT = PT1 if p == 0 else PT2
                        pb.use("sync", lambda e, t=t, p=p: e.dma_start(
                            out=x0_ld[:, p], in_=x1_h[t]),
                            reads=(S["x1_h"],), writes=(S["x0_ld%d" % p],), dma=True)
                        pb.use("vector", lambda e, t=t, p=p, nm1=nm1, rs1=rs1: e.tensor_scalar(
                            d0_xn[:, p][:, :], x0_ld[:, p][:, :], nm1[:, t:t + 1], rs1[:, t:t + 1],
                            ALU.add, ALU.mult),
                            reads=(S["x0_ld%d" % p], S["d0_nm"], S["d0_rs"]), writes=(S["d0_xn%d" % p],))
                        def tp(e, p=p, PT=PT):
                            for c in range(8):
                                ins = e.transpose(PT[:, c * 128:(c + 1) * 128],
                                                  d0_xn[:, p][:, c * 128:(c + 1) * 128], sb_ident[:])
                            return ins
                        pb.use("tensor", tp, reads=(S["d0_xn%d" % p], S["consts"]),
                               writes=(S["pt%d" % (p + 1)],))
                        pb.use("vector", lambda e, t=t, PT=PT: e.tensor_copy(
                            xn1_ap[:, :, t * 128:(t + 1) * 128],
                            PT.ap()[:, 0:1024].rearrange("p (c s) -> p c s", s=128)),
                            reads=(S["pt%d" % (p + 1)],), writes=(S["xn1f%d" % st],))
                        def rmm(e, t=t, i=i):
                            for kc in range(8):
                                ins = e.matmul(PS[2 + i][:, 0:8], xn1_ap[:, kc, t * 128:(t + 1) * 128],
                                               gw_sb[:, kc * 8:(kc + 1) * 8],
                                               start=(kc == 0), stop=(kc == 7), skip_group_check=True)
                            return ins
                        pb.use("tensor", rmm, reads=(S["xn1f%d" % st], S["gw"]),
                               writes=(S["ps%d" % (2 + i)],))
                    def u1(e, st=st):
                        for i in range(4):
                            ins = e.reduce_max(d0_r[:, i][:, 8:9], PS[2 + i][:, 0:8], AX.X, negate=True)
                        return ins
                    pb.use("vector", u1, reads=(S["ps2"], S["ps3"], S["ps4"], S["ps5"]),
                           writes=(S["d0_r"],))
                    def u2(e, st=st):
                        for i in range(4):
                            ins = e.activation(d0_r[:, i][:, 0:8], PS[2 + i][:, 0:8], AF.Exp,
                                               bias=d0_r[:, i][:, 8:9], scale=1.0)
                        return ins
                    pb.use("scalar", u2, reads=(S["ps2"], S["ps3"], S["ps4"], S["ps5"], S["d0_r"]),
                           writes=(S["d0_r"],))
                    def u3(e, st=st):
                        R = [d0_r[:, i] for i in range(4)]
                        for r in R:
                            e.reduce_sum(r[:, 9:10], r[:, 0:8], AX.X)
                        for r in R:
                            e.reciprocal(r[:, 10:11], r[:, 9:10])
                        for r in R:
                            e.scalar_tensor_tensor(r[:, 0:8], r[:, 0:8], r[:, 10:11],
                                                   gb_sb[:], ALU.mult, ALU.add)
                        for r in R:
                            e.reduce_max(r[:, 11:12], r[:, 0:8], AX.X)
                        for r in R:
                            e.tensor_scalar(r[:, 12:20], r[:, 0:8], r[:, 11:12], None, ALU.is_ge)
                        for r in R:
                            e.scalar_tensor_tensor(r[:, 21:29], r[:, 12:20], -1e30, r[:, 0:8],
                                                   ALU.mult, ALU.add)
                        for r in R:
                            e.reduce_max(r[:, 30:31], r[:, 21:29], AX.X)
                        for r in R:
                            e.tensor_scalar(r[:, 12:20], r[:, 0:8], r[:, 30:31], None, ALU.is_ge)
                        for r in R:
                            e.tensor_tensor(r[:, 0:8], r[:, 0:8], gb_sb[:], ALU.subtract)
                        for i, r in enumerate(R):
                            ins = e.tensor_tensor(rw_tok[:, (st * 4 + i) * 8:(st * 4 + i + 1) * 8],
                                                  r[:, 0:8], r[:, 12:20], ALU.mult)
                        return ins
                    pb.use("vector", u3, reads=(S["d0_r"], S["gb"]),
                           writes=(S["d0_r"], S["rw_tok"]))
            pb.barrier()
            if "xn1" in dbg:
                pb.use("scalar", lambda e: e.dma_start(out=dbg_out["xn1"], in_=xn1_fm[:]),
                       reads=(S["xn1f0"], S["xn1f1"], S["xn1f2"], S["xn1f3"]), dma=True)
            if "rw" in dbg:
                pb.use("scalar", lambda e: e.dma_start(out=dbg_out["rw"], in_=rw_tok[:]),
                       reads=(S["rw_tok"],), dma=True)
            if "x1" in dbg:
                pb.use("scalar", lambda e: e.dma_start(out=dbg_out["x1"], in_=x1_h), dma=True)
                pb.barrier()

        if "D" in phases:
          with ExitStack() as pctx:
            P = pctx.enter_context
            wqkv_sb = P(nc.sbuf_tensor("wqkv_sb", [128, 8 * 3072], BF16))
            sb_causal = P(nc.sbuf_tensor("sb_causal", [128, 128], F32))
            slot("causal")
            pb.use("sync", lambda e: e.dma_start(out=sb_causal[:], in_=causal_neg),
                   writes=(S["causal"],), dma=True)
            K_sb = P(nc.sbuf_tensor("K_sb", [128, 8 * SEQ], BF16))
            Vt_sb = P(nc.sbuf_tensor("Vt_sb", [128, NT * 1024], BF16))
            qbuf = P(nc.sbuf_tensor("qbuf", [128, 8 * 512], BF16))
            vtmp = P(nc.sbuf_tensor("vtmp", [128, 8 * 512], BF16))
            srow = P(nc.sbuf_tensor("srow", [128, SEQ], F32))
            pbf = P(nc.sbuf_tensor("pbf", [128, 2, SEQ], BF16))
            pts = P(nc.sbuf_tensor("pts", [128, 2, 1024], BF16))
            dsc = P(nc.sbuf_tensor("dsc", [128, 2], F32))
            dscm = P(nc.sbuf_tensor("dscm", [128, 4], F32))
            s1b = P(nc.sbuf_tensor("s1b", [128, DM], F32))
            for nm in ("wqkv", "K_sb", "Vt_sb", "qbuf", "vtmp", "srow",
                       "pbf00", "pbf01", "pbf10", "pbf11",
                       "pts0", "pts1", "dsc", "dscm", "s1b"):
                slot(nm)
            pb.use("gpsimd", lambda e: e.dma_start(out=wqkv_sb[:], in_=cWqkv), writes=(S["wqkv"],), dma=True)

            def emit_pv(t):
                tp2 = t % 2
                for gr in range((t + 8) // 8):
                    j0 = gr * 8
                    jn = min(8, t + 1 - j0)
                    pj = gr % 2
                    PT = PT1 if pj == 0 else PT2
                    def tpg(e, j0=j0, jn=jn, PT=PT, tp2=tp2):
                        for jj in range(jn):
                            ins = e.transpose(PT[:, jj * 128:(jj + 1) * 128],
                                              pbf[:, tp2][:, (j0 + jj) * 128:(j0 + jj + 1) * 128],
                                              sb_ident[:])
                        return ins
                    pb.use("tensor", tpg, reads=(S["pbf%d%d" % (tp2, gr)], S["consts"]),
                           writes=(S["pt%d" % (pj + 1)],))
                    pb.use("vector", lambda e, pj=pj, jn=jn, PT=PT: e.tensor_copy(
                        pts[:, pj][:, 0:jn * 128], PT[:, 0:jn * 128]),
                        reads=(S["pt%d" % (pj + 1)],), writes=(S["pts%d" % pj],))
                    def pvmm(e, j0=j0, jn=jn, pj=pj, t=t):
                        for jj in range(jn):
                            j = j0 + jj
                            lh = pts[:, pj][:, jj * 128:(jj + 1) * 128]
                            e.matmul(PS[2][:, 0:512], lh, Vt_sb[:, j * 1024: j * 1024 + 512],
                                     start=(j == 0), stop=(j == t), skip_group_check=True)
                            e.matmul(PS[3][:, 0:512], lh, Vt_sb[:, j * 1024 + 512: j * 1024 + 1024],
                                     start=(j == 0), stop=(j == t), skip_group_check=True)
                            ins = e.matmul(PS[4][:, 0:1], lh, sb_ones[:, 0:1],
                                           start=(j == 0), stop=(j == t), skip_group_check=True)
                        return ins
                    pb.use("tensor", pvmm, reads=(S["pts%d" % pj], S["Vt_sb"], S["consts"]),
                           writes=(S["ps2"], S["ps3"], S["ps4"]))
                pb.use("vector", lambda e: e.reciprocal(dsc[:, 1:2], PS[4][:, 0:1]),
                       reads=(S["ps4"],), writes=(S["dsc"],))
                pb.use("vector", lambda e: e.tensor_scalar(
                    s1b[:, 0:512], PS[2][:, 0:512], dsc[:, 1:2], None, ALU.mult),
                    reads=(S["ps2"], S["dsc"]), writes=(S["s1b"],))
                pb.use("vector", lambda e: e.tensor_scalar(
                    s1b[:, 512:1024], PS[3][:, 0:512], dsc[:, 1:2], None, ALU.mult),
                    reads=(S["ps3"], S["dsc"]), writes=(S["s1b"],))
                pb.use("scalar", lambda e, t=t: e.dma_start(out=s1_h[t], in_=s1b[:]),
                       reads=(S["s1b"],), writes=(S["s1_h"],), dma=True)

            for st in range(4):
                for oc in range(24):
                    pi = oc % 2
                    def mm(e, oc=oc, st=st, pi=pi):
                        for kc in range(8):
                            ins = e.matmul(PS[pi][:, 0:512],
                                           wqkv_sb[:, kc * 3072 + oc * 128: kc * 3072 + oc * 128 + 128],
                                           xn1_ap[:, kc, st * 512:(st + 1) * 512],
                                           start=(kc == 0), stop=(kc == 7), skip_group_check=True)
                        return ins
                    pb.use("tensor", mm, reads=(S["xn1f%d" % st], S["wqkv"]), writes=(PSLOT[pi],))
                    if oc < 8:
                        dstt, dsl, off = qbuf, "qbuf", oc * 512
                        pb.use("vector", lambda e, pi=pi, off=off, dstt=dstt: e.tensor_copy(
                            dstt[:, off:off + 512], PS[pi][:, 0:512]), reads=(PSLOT[pi],), writes=(S[dsl],))
                    elif oc < 16:
                        off = (oc - 8) * SEQ + st * 512
                        pb.use("vector", lambda e, pi=pi, off=off: e.tensor_copy(
                            K_sb[:, off:off + 512], PS[pi][:, 0:512]), reads=(PSLOT[pi],), writes=(S["K_sb"],))
                    else:
                        off = (oc - 16) * 512
                        pb.use("vector", lambda e, pi=pi, off=off: e.tensor_copy(
                            vtmp[:, off:off + 512], PS[pi][:, 0:512]), reads=(PSLOT[pi],), writes=(S["vtmp"],))
                for i in range(4):
                    t = st * 4 + i
                    def tpv(e, i=i):
                        for c in range(8):
                            ins = e.transpose(PT1[:, c * 128:(c + 1) * 128],
                                              vtmp[:, c * 512 + i * 128: c * 512 + i * 128 + 128],
                                              sb_ident[:])
                        return ins
                    pb.use("tensor", tpv, reads=(S["vtmp"], S["consts"]), writes=(S["pt1"],))
                    pb.use("vector", lambda e, t=t: e.tensor_copy(
                        Vt_sb[:, t * 1024:(t + 1) * 1024], PT1.ap()[:, 0:1024]),
                        reads=(S["pt1"],), writes=(S["Vt_sb"],))
                    L = (t + 1) * 128
                    ng = (L + 511) // 512
                    for g in range(ng):
                        w = min(512, L - g * 512)
                        pi = g % 2
                        def smm(e, g=g, w=w, i=i, pi=pi):
                            for kc in range(8):
                                ins = e.matmul(PS[pi][:, 0:w],
                                               qbuf[:, kc * 512 + i * 128: kc * 512 + i * 128 + 128],
                                               K_sb[:, kc * SEQ + g * 512: kc * SEQ + g * 512 + w],
                                               start=(kc == 0), stop=(kc == 7), skip_group_check=True)
                            return ins
                        pb.use("tensor", smm, reads=(S["qbuf"], S["K_sb"]), writes=(PSLOT[pi],))
                        pb.use("vector", lambda e, g=g, w=w, pi=pi: e.tensor_scalar(
                            srow[:, g * 512: g * 512 + w], PS[pi][:, 0:w], 1.0 / 32, None, ALU.mult),
                            reads=(PSLOT[pi],), writes=(S["srow"],))
                        if g == ng - 1:
                            pb.use("vector", lambda e, t=t: e.tensor_tensor(
                                srow[:, t * 128:(t + 1) * 128], srow[:, t * 128:(t + 1) * 128],
                                sb_causal[:], ALU.add),
                                reads=(S["srow"], S["causal"]), writes=(S["srow"],))
                        pb.use("vector", lambda e, g=g, w=w: e.reduce_max(
                            dscm[:, g:g + 1], srow[:, g * 512: g * 512 + w], AX.X),
                            reads=(S["srow"],), writes=(S["dscm"],))
                    pb.use("vector", lambda e, ng=ng: e.reduce_max(
                        dsc[:, 0:1], dscm[:, 0:ng], AX.X, negate=True),
                        reads=(S["dscm"],), writes=(S["dsc"],))
                    tp2 = t % 2
                    for hh in range((L + 1023) // 1024):
                        w2h = min(1024, L - hh * 1024)
                        pb.use("scalar", lambda e, hh=hh, w2h=w2h, tp2=tp2: e.activation(
                            pbf[:, tp2][:, hh * 1024: hh * 1024 + w2h],
                            srow[:, hh * 1024: hh * 1024 + w2h],
                            AF.Exp, bias=dsc[:, 0:1], scale=1.0),
                            reads=(S["srow"], S["dsc"]), writes=(S["pbf%d%d" % (tp2, hh)],))
                    if t > 0:
                        emit_pv(t - 1)
            emit_pv(NT - 1)
            pb.barrier()
            if "s1" in dbg:
                pb.use("scalar", lambda e: e.dma_start(out=dbg_out["s1"], in_=s1_h), dma=True)
                pb.barrier()

        if "E" in phases:
          with ExitStack() as pctx:
            P = pctx.enter_context
            w1_sb = P(nc.sbuf_tensor("w1_sb", [128, 2, 8 * 1408], BF16))
            w3_sb = P(nc.sbuf_tensor("w3_sb", [128, 2, 8 * 1408], BF16))
            w2_sb = P(nc.sbuf_tensor("w2_sb", [128, 11 * 1024], BF16))
            b1c_sb = P(nc.sbuf_tensor("b1c_sb", [128, 11], F32))
            b3c_sb = P(nc.sbuf_tensor("b3c_sb", [128, 11], F32))
            b2m_sb = P(nc.sbuf_tensor("b2m_sb", [9, 1024], BF16))
            rwa = P(nc.sbuf_tensor("rwa", [128, 9], BF16))
            rwt = P(nc.sbuf_tensor("rwt", [9, 128], BF16))
            h_sb = P(nc.sbuf_tensor("h_sb", [128, 11 * 512], BF16))
            prod = P(nc.sbuf_tensor("prod", [128, 2, 512], F32))
            y_half = P(nc.sbuf_tensor("y_half", [128, 8 * 1024], F32))
            s1p = P(nc.sbuf_tensor("s1p", [128, 2, DM], F32))
            x1e = P(nc.sbuf_tensor("x1e", [128, DM], F32))
            slot("yout")
            for nm in ("w10", "w11", "w30", "w31", "w2", "b1c", "b3c", "b2m", "rwa", "rwt",
                       "h_sb", "prod0", "prod1", "s1p0", "s1p1", "x1e",
                       "yh0", "yh1", "yh2", "yh3", "yh4", "yh5", "yh6", "yh7"):
                slot(nm)

            def ld_s1(gi):
                bi = gi % 2
                pb.use("scalar", lambda e, gi=gi, bi=bi: e.dma_start(
                    out=s1p[:, bi], in_=s1_h[gi]),
                    reads=(S["s1_h"],), writes=(S["s1p%d" % bi],), dma=True)

            def ld_x1(gi):
                pb.use("sync", lambda e, gi=gi: e.dma_start(out=x1e[:], in_=x1_h[gi]),
                       reads=(S["x1_h"],), writes=(S["x1e"],), dma=True)
            pb.use("gpsimd", lambda e: e.dma_start(out=b2m_sb[0:8, :], in_=b2flat.rearrange("o (e v) -> (o e) v", v=1024)),
                   writes=(S["b2m"],), dma=True)
            pb.use("gpsimd", lambda e: e.dma_start(out=b2m_sb[8:9, :], in_=sb2flat),
                   writes=(S["b2m"],), dma=True)
            pb.use("vector", lambda e: e.memset(rwa[:, 8:9], 1.0), writes=(S["rwa"],))
            for half in range(2):
                for lt in range(8):
                    tg = half * 8 + lt
                    pb.use("vector", lambda e, tg=tg: e.tensor_copy(
                        rwa[:, 0:8], rw_tok[:, tg * 8:(tg + 1) * 8]),
                        reads=(S["rw_tok"],), writes=(S["rwa"],))
                    pb.use("tensor", lambda e: e.transpose(PT1[0:9, 0:128], rwa[:, 0:9], sb_ident[:]),
                           reads=(S["rwa"], S["consts"]), writes=(S["pt1"],))
                    pb.use("vector", lambda e: e.tensor_copy(rwt[0:9, :], PT1[0:9, 0:128]),
                           reads=(S["pt1"],), writes=(S["rwt"],))
                    def b2mm(e):
                        e.matmul(PS[2][:, 0:512], rwt[0:9, 0:128], b2m_sb[0:9, 0:512],
                                 start=True, stop=True, skip_group_check=True)
                        return e.matmul(PS[3][:, 0:512], rwt[0:9, 0:128], b2m_sb[0:9, 512:1024],
                                        start=True, stop=True, skip_group_check=True)
                    pb.use("tensor", b2mm, reads=(S["rwt"], S["b2m"]), writes=(S["ps2"], S["ps3"]))
                    pb.use("vector", lambda e, lt=lt: e.tensor_copy(
                        y_half[:, lt * 1024: lt * 1024 + 512], PS[2][:, 0:512]),
                        reads=(S["ps2"],), writes=(S["yh%d" % lt],))
                    pb.use("vector", lambda e, lt=lt: e.tensor_copy(
                        y_half[:, lt * 1024 + 512: lt * 1024 + 1024], PS[3][:, 0:512]),
                        reads=(S["ps3"],), writes=(S["yh%d" % lt],))
                for ex in range(9):
                    shared = ex == 8
                    wp = (half * 9 + ex) % 2
                    pb.use("gpsimd", lambda e, ex=ex, wp=wp: e.dma_start(
                        out=w1_sb[:, wp], in_=(csw1 if ex == 8 else cw1[ex])), writes=(S["w1%d" % wp],), dma=True)
                    pb.use("gpsimd", lambda e, ex=ex, wp=wp: e.dma_start(
                        out=w3_sb[:, wp], in_=(csw3 if ex == 8 else cw3[ex])), writes=(S["w3%d" % wp],), dma=True)
                    pb.use("gpsimd", lambda e, ex=ex: e.dma_start(
                        out=w2_sb[:], in_=(csw2 if ex == 8 else cw2[ex])), writes=(S["w2"],), dma=True)
                    pb.use("sync", lambda e, ex=ex: e.dma_start(
                        out=b1c_sb[:], in_=(sb1c if ex == 8 else b1c[ex])), writes=(S["b1c"],), dma=True)
                    pb.use("sync", lambda e, ex=ex: e.dma_start(
                        out=b3c_sb[:], in_=(sb3c if ex == 8 else b3c[ex])), writes=(S["b3c"],), dma=True)
                    if shared:
                        ld_s1(half * 8 + 0)
                        ld_s1(half * 8 + 1)
                        ld_x1(half * 8 + 0)
                    for st2 in range(2):
                        tok0 = half * 1024 + st2 * 512
                        for hc in range(11):
                            pA, pB = (0, 1) if hc % 2 == 0 else (4, 5)
                            pp = hc % 2
                            def mmA(e, hc=hc, tok0=tok0, pA=pA, wp=wp):
                                for kc in range(8):
                                    ins = e.matmul(PS[pA][:, 0:512],
                                                   w1_sb[:, wp][:, kc * 1408 + hc * 128: kc * 1408 + hc * 128 + 128],
                                                   xn1_ap[:, kc, tok0:tok0 + 512],
                                                   start=(kc == 0), stop=(kc == 7), skip_group_check=True)
                                return ins
                            pb.use("tensor", mmA, reads=(S["xn1f%d" % (half * 2 + st2)], S["w1%d" % wp]), writes=(S["ps%d" % pA],))
                            def mmB(e, hc=hc, tok0=tok0, pB=pB, wp=wp):
                                for kc in range(8):
                                    ins = e.matmul(PS[pB][:, 0:512],
                                                   w3_sb[:, wp][:, kc * 1408 + hc * 128: kc * 1408 + hc * 128 + 128],
                                                   xn1_ap[:, kc, tok0:tok0 + 512],
                                                   start=(kc == 0), stop=(kc == 7), skip_group_check=True)
                                return ins
                            pb.use("tensor", mmB, reads=(S["xn1f%d" % (half * 2 + st2)], S["w3%d" % wp]), writes=(S["ps%d" % pB],))
                            pb.use("scalar", lambda e, hc=hc, pB=pB, pp=pp: e.activation(
                                prod[:, pp], PS[pB][:, 0:512], AF.Identity, bias=b3c_sb[:, hc:hc + 1], scale=1.0),
                                reads=(S["ps%d" % pB], S["b3c"]), writes=(S["prod%d" % pp],))
                            pb.use("vector", lambda e, hc=hc, pA=pA, pp=pp: e.scalar_tensor_tensor(
                                prod[:, pp], PS[pA][:, 0:512], b1c_sb[:, hc:hc + 1], prod[:, pp],
                                ALU.add, ALU.mult),
                                reads=(S["ps%d" % pA], S["b1c"], S["prod%d" % pp]), writes=(S["prod%d" % pp],))
                            pb.use("scalar", lambda e, hc=hc, pp=pp: e.activation(
                                h_sb[:, hc * 512:(hc + 1) * 512], prod[:, pp], AF.Silu),
                                reads=(S["prod%d" % pp],), writes=(S["h_sb"],))
                        for i2 in range(4):
                            lt = st2 * 4 + i2
                            tg = half * 8 + lt
                            oA, oB = (2, 3) if i2 % 2 == 0 else (0, 1)
                            def omm(e, i2=i2, oA=oA, oB=oB):
                                for kc2 in range(11):
                                    e.matmul(PS[oA][:, 0:512],
                                             h_sb[:, kc2 * 512 + i2 * 128: kc2 * 512 + i2 * 128 + 128],
                                             w2_sb[:, kc2 * 1024: kc2 * 1024 + 512],
                                             start=(kc2 == 0), stop=(kc2 == 10), skip_group_check=True)
                                    ins = e.matmul(PS[oB][:, 0:512],
                                                   h_sb[:, kc2 * 512 + i2 * 128: kc2 * 512 + i2 * 128 + 128],
                                                   w2_sb[:, kc2 * 1024 + 512: kc2 * 1024 + 1024],
                                                   start=(kc2 == 0), stop=(kc2 == 10), skip_group_check=True)
                                return ins
                            pb.use("tensor", omm, reads=(S["h_sb"], S["w2"]), writes=(S["ps%d" % oA], S["ps%d" % oB]))
                            for v in range(2):
                                pv = oA if v == 0 else oB
                                if shared:
                                    pb.use("vector", lambda e, lt=lt, v=v, pv=pv: e.tensor_tensor(
                                        y_half[:, lt * 1024 + v * 512: lt * 1024 + v * 512 + 512],
                                        y_half[:, lt * 1024 + v * 512: lt * 1024 + v * 512 + 512],
                                        PS[pv][:, 0:512], ALU.add),
                                        reads=(S["ps%d" % pv], S["yh%d" % lt]), writes=(S["yh%d" % lt],))
                                else:
                                    pb.use("vector", lambda e, lt=lt, v=v, tg=tg, ex=ex, pv=pv: e.scalar_tensor_tensor(
                                        y_half[:, lt * 1024 + v * 512: lt * 1024 + v * 512 + 512],
                                        PS[pv][:, 0:512], rw_tok[:, tg * 8 + ex: tg * 8 + ex + 1],
                                        y_half[:, lt * 1024 + v * 512: lt * 1024 + v * 512 + 512],
                                        ALU.mult, ALU.add),
                                        reads=(S["ps%d" % pv], S["rw_tok"], S["yh%d" % lt]),
                                        writes=(S["yh%d" % lt],))
                        if shared:
                            for lt in range(st2 * 4, st2 * 4 + 4):
                                tg = half * 8 + lt
                                bi = lt % 2
                                yv = y_half[:, lt * 1024:(lt + 1) * 1024]
                                pb.use("scalar", lambda e, yv=yv: e.activation(yv, yv, AF.Silu),
                                       reads=(S["yh%d" % lt],), writes=(S["yh%d" % lt],))
                                pb.use("vector", lambda e, yv=yv, bi=bi: e.tensor_tensor(
                                    yv, yv, s1p[:, bi], ALU.mult),
                                    reads=(S["yh%d" % lt], S["s1p%d" % bi]), writes=(S["yh%d" % lt],))
                                if lt + 2 < 8:
                                    ld_s1(tg + 2)
                                pb.use("vector", lambda e, yv=yv: e.tensor_tensor(
                                    yv, yv, x1e[:], ALU.add),
                                    reads=(S["yh%d" % lt], S["x1e"]), writes=(S["yh%d" % lt],))
                                if lt + 1 < 8:
                                    ld_x1(tg + 1)
                                pb.use("sync", lambda e, tg=tg, yv=yv: e.dma_start(out=yout[tg], in_=yv),
                                       reads=(S["yh%d" % lt],), writes=(S["yout"],), dma=True)
            pb.barrier()

        blk = E(nc.Block())
        blk.sync(lambda e: [f(e) for f in pb.ops["sync"]])
        blk.gpsimd(lambda e: [f(e) for f in pb.ops["gpsimd"]])
        blk.vector(lambda e: [f(e) for f in pb.ops["vector"]])
        blk.scalar(lambda e: [f(e) for f in pb.ops["scalar"]])
        blk.tensor(lambda e: [f(e) for f in pb.ops["tensor"]])
    return nc


def kernel(**inputs):
    shared = prep_inputs(inputs)
    x = np.asarray(inputs["L0_attn_x"], dtype=np.float32)
    nc = build_program()
    in_maps = []
    for b in range(8):
        m = dict(shared)
        m["x0"] = np.ascontiguousarray(x[b].reshape(NT, 128, DM))
        in_maps.append(m)
    res = run_bass_kernel_spmd(nc, in_maps, core_ids=list(range(8)))
    out = np.stack([np.asarray(res.results[b]["y"], dtype=np.float32).reshape(SEQ, DM)
                    for b in range(8)], axis=0)
    return out


# version 33
# speedup vs baseline: 1.0481x; 1.0481x over previous
import numpy as np
import ml_dtypes
from contextlib import ExitStack

import concourse.bass as bass
import concourse.mybir as mybir
from concourse.bass_utils import run_bass_kernel_spmd

F32 = mybir.dt.float32
BF16 = mybir.dt.bfloat16
AF = mybir.ActivationFunctionType
ALU = mybir.AluOpType
AX = mybir.AxisListType

SEQ = 2048
DM = 1024
NT = SEQ // 128
EPS_LN = 1e-5
EPS_RMS = 1e-6
ENGINES = ("sync", "gpsimd", "vector", "scalar", "tensor")
CSEM = {"tensor": "pe", "vector": "dv", "scalar": "ac", "gpsimd": "gp"}
QSEM = {"sync": "ld", "gpsimd": "wd", "scalar": "sd"}


class Slot:
    __slots__ = ("name", "ready", "readers", "qsem")

    def __init__(self, name):
        self.name = name
        self.ready = None
        self.readers = []
        self.qsem = None


class PB:
    SERIAL_DMA = False

    def __init__(self, nc):
        self.nc = nc
        self.ops = {e: [] for e in ENGINES}
        self.sem = {}
        self.cnt = {}
        self.hwm = {e: {} for e in ENGINES}
        self.ctx = None

    def new_sem(self, ctx, name):
        self.sem[name] = ctx.enter_context(self.nc.semaphore(name))
        self.cnt[name] = 0

    def slot_qsem(self, s):
        if s.qsem is None:
            name = "q_" + s.name
            self.sem[name] = self.ctx.enter_context(self.nc.semaphore(name))
            self.cnt[name] = 0
            s.qsem = name
        return s.qsem

    def _wait(self, eng, sem, tgt):
        if self.hwm[eng].get(sem, 0) >= tgt:
            return
        self.hwm[eng][sem] = tgt
        h = self.sem[sem]
        self.ops[eng].append(lambda e, h=h, tgt=tgt: e.wait_ge(h, tgt))

    def wait_tok(self, eng, tok, cur_dma=False, raw=False):
        if tok is None:
            return
        sem, val = tok
        if not raw and not cur_dma and sem == CSEM.get(eng):
            return
        self._wait(eng, sem, val)

    def use(self, eng, f, reads=(), writes=(), dma=False, extra_tok=()):
        for s in reads:
            self.wait_tok(eng, s.ready, cur_dma=dma, raw=True)
        for s in writes:
            self.wait_tok(eng, s.ready, cur_dma=dma)
            for t in s.readers:
                self.wait_tok(eng, t, cur_dma=dma)
        for t in extra_tok:
            self.wait_tok(eng, t, cur_dma=dma, raw=True)
        if dma:
            sem = self.slot_qsem(writes[0]) if writes else QSEM[eng]
            by = 16
        else:
            sem = CSEM[eng]
            by = 1
        if dma and PB.SERIAL_DMA:
            self._wait(eng, sem, self.cnt[sem])
        self.cnt[sem] += by
        tok = (sem, self.cnt[sem])
        h = self.sem[sem]

        def run(e, f=f, h=h, by=by):
            ins = f(e)
            ins.then_inc(h, by)
        self.ops[eng].append(run)
        for s in writes:
            s.ready = tok
            s.readers = []
        for s in reads:
            s.readers.append(tok)
        return tok

    def raw(self, eng, f):
        self.ops[eng].append(f)

    def barrier(self):
        for s in list(self.sem):
            if s.startswith("q_") or s in ("ld", "wd", "sd"):
                self._wait("sync", s, self.cnt[s])
        for eng in ENGINES:
            own = CSEM.get(eng)
            if own is not None:
                h2 = self.sem[own]
                tgt2 = self.cnt[own]
                self.ops[eng].append(lambda e, h2=h2, tgt2=tgt2: e.wait_ge(h2, tgt2))
                self.hwm[eng][own] = max(self.hwm[eng].get(own, 0), tgt2)
            self.cnt["bar"] += 1
            h = self.sem["bar"]
            self.ops[eng].append(lambda e, h=h: e.nop().then_inc(h, 1))
        tgt = self.cnt["bar"]
        for eng in ENGINES:
            self._wait(eng, "bar", tgt)


def _pretile_kmajor(w_t, M):
    K = w_t.shape[0]
    return np.ascontiguousarray(
        w_t.reshape(K // 128, 128, M).transpose(1, 0, 2).reshape(128, (K // 128) * M))


def _f32(x):
    return np.ascontiguousarray(np.asarray(x, dtype=np.float32))


def prep_inputs(I):
    t = {}
    t["cw_in"] = _pretile_kmajor(_f32(I["L0_attn_in_proj_w"]).T, 2048)
    for nm, key in (("cw_q", "L0_attn_q_proj_w"), ("cw_k", "L0_attn_k_proj_w"),
                    ("cw_v", "L0_attn_v_proj_w"), ("cw_i", "L0_attn_i_gate_w"),
                    ("cw_f", "L0_attn_f_gate_w"), ("cw_o", "L0_attn_o_gate_w"),
                    ("cw_out", "L0_attn_out_proj_w")):
        t[nm] = _pretile_kmajor(_f32(I[key]).T, 1024)
    for nm, key in (("bi_c", "L0_attn_i_gate_b"), ("bf_c", "L0_attn_f_gate_b"),
                    ("bo_c", "L0_attn_o_gate_b")):
        t[nm] = np.ascontiguousarray(_f32(I[key]).reshape(8, 128).T)
    t["qnw_c"] = np.ascontiguousarray(np.tile(_f32(I["L0_attn_q_norm_w"]), 2).reshape(128, 1))
    t["knw_c"] = np.ascontiguousarray(np.tile(_f32(I["L0_attn_k_norm_w"]), 2).reshape(128, 1))
    t["csel"] = _pretile_kmajor(_f32(I["L0_ffn_expert_sel"]).T, 16)
    keys = _f32(I["L0_ffn_keys"])
    t["ckeys"] = np.ascontiguousarray(keys.reshape(16, 8, 128, 128).transpose(2, 0, 1, 3).reshape(128, 16 * 8 * 128))
    vals = _f32(I["L0_ffn_values"])
    t["cvalues"] = np.ascontiguousarray(vals.transpose(1, 0, 2).reshape(128, 16 * 1024))
    t["cWqkv"] = _pretile_kmajor(_f32(I["L1_attn_W_qkv"]), 3072)
    t["cgw"] = _pretile_kmajor(_f32(I["L1_ffn_gate_w"]).T, 8)
    t["gate_b_bc"] = np.ascontiguousarray(np.tile(_f32(I["L1_ffn_gate_b"]), (128, 1)))
    w1 = _f32(I["L1_ffn_w1"])
    t["cw1"] = np.ascontiguousarray(w1.reshape(8, 8, 128, 1408).transpose(0, 2, 1, 3).reshape(8, 128, 8 * 1408))
    w3 = _f32(I["L1_ffn_w3"])
    t["cw3"] = np.ascontiguousarray(w3.reshape(8, 8, 128, 1408).transpose(0, 2, 1, 3).reshape(8, 128, 8 * 1408))
    w2 = _f32(I["L1_ffn_w2"])
    t["cw2"] = np.ascontiguousarray(w2.reshape(8, 11, 128, 1024).transpose(0, 2, 1, 3).reshape(8, 128, 11 * 1024))
    t["b1c"] = np.ascontiguousarray(_f32(I["L1_ffn_b1"]).reshape(8, 11, 128).transpose(0, 2, 1))
    t["b3c"] = np.ascontiguousarray(_f32(I["L1_ffn_b3"]).reshape(8, 11, 128).transpose(0, 2, 1))
    t["b2flat"] = np.ascontiguousarray(_f32(I["L1_ffn_b2"]).reshape(1, 8 * 1024))
    t["csw1"] = _pretile_kmajor(_f32(I["L1_ffn_sw1"]), 1408)
    t["csw3"] = _pretile_kmajor(_f32(I["L1_ffn_sw3"]), 1408)
    sw2 = _f32(I["L1_ffn_sw2"])
    t["csw2"] = np.ascontiguousarray(sw2.reshape(11, 128, 1024).transpose(1, 0, 2).reshape(128, 11 * 1024))
    t["sb1c"] = np.ascontiguousarray(_f32(I["L1_ffn_sb1"]).reshape(11, 128).T)
    t["sb3c"] = np.ascontiguousarray(_f32(I["L1_ffn_sb3"]).reshape(11, 128).T)
    t["sb2flat"] = np.ascontiguousarray(_f32(I["L1_ffn_sb2"]).reshape(1, 1024))
    t["ident_bf"] = np.eye(128, dtype=ml_dtypes.bfloat16)
    s_i = np.arange(128)[:, None]
    t_i = np.arange(128)[None, :]
    t["mask_at"] = ((s_i <= t_i) & (s_i // 64 == t_i // 64)).astype(ml_dtypes.bfloat16)
    t["causal_neg"] = np.where(t_i > s_i, np.float32(-1e30), np.float32(0.0))
    ind_all = np.zeros((16, 8 * 128), dtype=ml_dtypes.bfloat16)
    bones = np.zeros((128, 8 * 16), dtype=ml_dtypes.bfloat16)
    for kc in range(8):
        for p in range(128):
            h = 2 * kc + (p // 64)
            ind_all[h, kc * 128 + p] = 1
            bones[p, kc * 16 + h] = 1
    t["ind_all"] = ind_all
    t["bones"] = bones
    return t


PHASE_ALL = ("A0", "A1", "A2", "B", "C", "D0", "D", "E")
B_TILES = 32
B_STAGE = 9
B_SUB = 9
B_VAR = 0


def build_program(phases=PHASE_ALL, debug=()):
    nc = bass.Bass("TRN2", target_bir_lowering=False)
    dbg = set(debug)
    phases = set(phases)

    def IN(name, shape, dt=F32):
        return nc.dram_tensor(name, list(shape), dt, kind="ExternalInput").ap()

    x0 = IN("x0", [NT, 128, DM])
    cw_in = IN("cw_in", [128, 8 * 2048])
    cw_q = IN("cw_q", [128, 8 * 1024]); cw_k = IN("cw_k", [128, 8 * 1024])
    cw_v = IN("cw_v", [128, 8 * 1024]); cw_i = IN("cw_i", [128, 8 * 1024])
    cw_f = IN("cw_f", [128, 8 * 1024]); cw_o = IN("cw_o", [128, 8 * 1024])
    cw_out = IN("cw_out", [128, 8 * 1024])
    bi_c = IN("bi_c", [128, 8]); bf_c = IN("bf_c", [128, 8]); bo_c = IN("bo_c", [128, 8])
    qnw_c = IN("qnw_c", [128, 1]); knw_c = IN("knw_c", [128, 1])
    csel = IN("csel", [128, 8 * 16])
    ckeys = IN("ckeys", [128, 16 * 8 * 128])
    cvalues = IN("cvalues", [128, 16 * 1024])
    cWqkv = IN("cWqkv", [128, 8 * 3072])
    cgw = IN("cgw", [128, 8 * 8])
    gate_b_bc = IN("gate_b_bc", [128, 8])
    cw1 = IN("cw1", [8, 128, 8 * 1408]); cw3 = IN("cw3", [8, 128, 8 * 1408])
    cw2 = IN("cw2", [8, 128, 11 * 1024])
    b1c = IN("b1c", [8, 128, 11]); b3c = IN("b3c", [8, 128, 11])
    b2flat = IN("b2flat", [1, 8 * 1024])
    csw1 = IN("csw1", [128, 8 * 1408]); csw3 = IN("csw3", [128, 8 * 1408])
    csw2 = IN("csw2", [128, 11 * 1024])
    sb1c = IN("sb1c", [128, 11]); sb3c = IN("sb3c", [128, 11])
    sb2flat = IN("sb2flat", [1, 1024])
    ident_bf = IN("ident_bf", [128, 128], BF16)
    mask_at = IN("mask_at", [128, 128], BF16)
    causal_neg = IN("causal_neg", [128, 128])
    ind_all = IN("ind_all", [16, 8 * 128], BF16)
    bones = IN("bones", [128, 8 * 16], BF16)

    yout = nc.dram_tensor("y", [NT, 128, DM], F32, kind="ExternalOutput").ap()

    gsig_h = nc.dram_tensor("gsig_h", [8, 128, SEQ], BF16, kind="Internal").ap()
    og_h = nc.dram_tensor("og_h", [8, 128, SEQ], BF16, kind="Internal").ap()
    g0_h = nc.dram_tensor("g0_h", [NT, 128, DM], BF16, kind="Internal").ap()
    hg_h = nc.dram_tensor("hg_h", [8, 128, SEQ], BF16, kind="Internal").ap()
    x1_h = nc.dram_tensor("x1_h", [NT, 128, DM], F32, kind="Internal").ap()
    s1_h = nc.dram_tensor("s1_h", [NT, 128, DM], F32, kind="Internal").ap()

    dbg_specs = {
        "xn": ([128, 8 * SEQ], BF16), "xinner": ([128, 8 * SEQ], BF16),
        "gsig": ([8, 128, SEQ], BF16), "og": ([8, 128, SEQ], BF16),
        "g0": ([NT, 128, DM], BF16), "hg": ([8, 128, SEQ], BF16),
        "x1": ([NT, 128, DM], F32), "s1": ([NT, 128, DM], F32),
        "xn1": ([128, 8 * SEQ], BF16), "rw": ([128, NT * 8], F32),
    }
    dbg_out = {k: nc.dram_tensor("dbg_" + k, list(s[0]), s[1], kind="ExternalOutput").ap()
               for k, s in dbg_specs.items() if k in dbg}

    with ExitStack() as ctx:
        E = ctx.enter_context
        pb = PB(nc)
        pb.ctx = ctx
        for s in ("ld", "wd", "sd", "pe", "dv", "ac", "gp", "bar"):
            pb.new_sem(ctx, s)

        sb_ident = E(nc.sbuf_tensor("sb_ident", [128, 128], BF16))
        sb_ones = E(nc.sbuf_tensor("sb_ones", [128, 512], BF16))
        sb_zeros = E(nc.sbuf_tensor("sb_zeros", [128, 64], F32))
        sb_cst = E(nc.sbuf_tensor("sb_cst", [128, 3], F32))
        xn_stk = ExitStack()
        xin_stk = ExitStack()
        a1w_stk = ExitStack()
        a2w_stk = ExitStack()
        x_inner = xin_stk.enter_context(nc.sbuf_tensor("x_inner", [128, 8 * SEQ], BF16, side="right"))
        xn_fm = xn_stk.enter_context(nc.sbuf_tensor("xn_fm", [128, 8 * SEQ], BF16, side="right"))
        keys_sb = a2w_stk.enter_context(nc.sbuf_tensor("keys_sb", [128, 16 * 8 * 128], BF16, side="right"))
        vals_sb = a2w_stk.enter_context(nc.sbuf_tensor("vals_sb", [128, 16 * 1024], BF16, side="right"))
        sel_sb = a2w_stk.enter_context(nc.sbuf_tensor("sel_sb", [128, 8 * 16], BF16, side="right"))
        w_in_sb = a1w_stk.enter_context(nc.sbuf_tensor("w_in_sb", [128, 8 * 2048], BF16, side="right"))
        w_o_sb = a1w_stk.enter_context(nc.sbuf_tensor("w_o_sb", [128, 8 * 1024], BF16, side="right"))
        rw_tok = E(nc.sbuf_tensor("rw_tok", [128, NT * 8], F32))

        PS = [E(nc.psum_tensor("psb%d" % i, [128, 512], F32)) for i in range(6)]
        PT1 = E(nc.psum_tensor("pt1", [128, 1024], BF16))
        PT2 = E(nc.psum_tensor("pt2", [128, 1024], BF16))

        S = {}
        def slot(name):
            S[name] = Slot(name)
            return S[name]
        for nm in ("consts", "xnf0", "xnf1", "xnf2", "xnf3", "x_inner", "rw_tok",
                   "ps0", "ps1", "ps2", "ps3", "ps4", "ps5", "pt1", "pt2",
                   "gsig_h", "og_h", "g0_h", "hg_h", "x1_h", "s1_h"):
            slot(nm)
        PSLOT = [S["ps%d" % i] for i in range(6)]

        xn_ap = xn_fm.ap().rearrange("p (c s) -> p c s", s=SEQ)
        xin_ap = x_inner.ap().rearrange("p (c s) -> p c s", s=SEQ)

        for dst, src in ((sb_ident, ident_bf),):
            pb.use("sync", lambda e, dst=dst, src=src: e.dma_start(out=dst[:], in_=src),
                   writes=(S["consts"],), dma=True)
        pb.use("vector", lambda e: e.memset(sb_ones[:], 1.0), writes=(S["consts"],))
        pb.use("vector", lambda e: e.memset(sb_zeros[:], 0.0), writes=(S["consts"],))
        pb.use("vector", lambda e: e.memset(sb_cst[:, 0:1], EPS_LN), writes=(S["consts"],))
        pb.use("vector", lambda e: e.memset(sb_cst[:, 1:2], EPS_RMS), writes=(S["consts"],))
        pb.use("vector", lambda e: e.memset(sb_cst[:, 2:3], 1.0), writes=(S["consts"],))
        sl_win = slot("w_in"); sl_wo = slot("w_o")
        sl_keys = slot("keys"); sl_vals = slot("vals"); sl_selw = slot("selw")
        if "A1" in phases:
            pb.use("gpsimd", lambda e: e.dma_start(out=w_in_sb[:], in_=cw_in), writes=(sl_win,), dma=True)
            pb.use("gpsimd", lambda e: e.dma_start(out=w_o_sb[:], in_=cw_o), writes=(sl_wo,), dma=True)
        if "A2" in phases:
            pb.use("gpsimd", lambda e: e.dma_start(out=keys_sb[:], in_=ckeys), writes=(sl_keys,), dma=True)
            pb.use("gpsimd", lambda e: e.dma_start(out=vals_sb[:], in_=cvalues), writes=(sl_vals,), dma=True)
            pb.use("gpsimd", lambda e: e.dma_start(out=sel_sb[:], in_=csel), writes=(sl_selw,), dma=True)

        a01_stk = ExitStack()
        if "A0" in phases:
            A0P = a01_stk.enter_context
            xt4 = A0P(nc.sbuf_tensor("xt4", [128, 4, DM], F32))
            sqd = A0P(nc.sbuf_tensor("sqd", [128, DM], F32))
            a0_xn = A0P(nc.sbuf_tensor("a0_xn", [128, 2, DM], BF16))
            a0_st = A0P(nc.sbuf_tensor("a0_st", [128, 6, 16], F32))
            sl_x4 = [slot("xt4_%d" % j) for j in range(4)]
            sl_sqd = slot("a0_sqd")
            sl_xn = [slot("a0_xn0"), slot("a0_xn1")]
            sl_sA = slot("a0_sA"); sl_sB = slot("a0_sB")
            sl_nm = slot("a0_nm"); sl_rs = slot("a0_rs")
            sl_t4 = slot("a0_t4"); sl_t5 = slot("a0_t5")
            sA = a0_st.ap()[:, 0]; sB = a0_st.ap()[:, 1]
            nm = a0_st.ap()[:, 2]; rs = a0_st.ap()[:, 3]
            q4 = a0_st.ap()[:, 4]; q5 = a0_st.ap()[:, 5]

            def emit_a0_group(g):
                g4 = slice(g * 4, g * 4 + 4)
                for j in range(4):
                    t = g * 4 + j
                    pb.use("sync" if j % 2 == 0 else "scalar",
                           lambda e, t=t, j=j: e.dma_start(out=xt4[:, j], in_=x0[t]),
                           writes=(sl_x4[j],), dma=True)
                    pb.use("vector", lambda e, j=j, t=t, sA=sA: e.reduce_sum(
                        sA[:, t:t + 1], xt4[:, j][:, :], AX.X, negate=True),
                        reads=(sl_x4[j],), writes=(sl_sA,))
                    pb.use("scalar", lambda e, j=j, t=t, sB=sB: e.activation(
                        sqd[:], xt4[:, j][:, :], AF.Square, accum_out=sB[:, t:t + 1]),
                        reads=(sl_x4[j], sl_sqd), writes=(sl_sqd, sl_sB))
                pb.use("vector", lambda e, g4=g4, nm=nm, sA=sA: e.tensor_scalar_mul(
                    nm[:, g4], sA[:, g4], 1.0 / DM), reads=(sl_sA,), writes=(sl_nm,))
                pb.use("vector", lambda e, g4=g4, nm=nm, q4=q4: e.tensor_tensor(
                    q4[:, g4], nm[:, g4], nm[:, g4], ALU.mult), reads=(sl_nm,), writes=(sl_t4,))
                pb.use("vector", lambda e, g4=g4, sB=sB, q4=q4, q5=q5: e.scalar_tensor_tensor(
                    q5[:, g4], sB[:, g4], 1.0 / DM, q4[:, g4], ALU.mult, ALU.subtract),
                    reads=(sl_sB, sl_t4), writes=(sl_t5,))
                pb.use("scalar", lambda e, g4=g4, q4=q4, q5=q5: e.activation(
                    q4[:, g4], q5[:, g4], AF.Ln, bias=sb_cst[:, 0:1], scale=1.0),
                    reads=(sl_t5, S["consts"]), writes=(sl_t4,))
                pb.use("scalar", lambda e, g4=g4, rs=rs, q4=q4: e.activation(
                    rs[:, g4], q4[:, g4], AF.Exp, bias=0.0, scale=-0.5),
                    reads=(sl_t4,), writes=(sl_rs,))
                for j in range(4):
                    t = g * 4 + j
                    p = j % 2
                    PT = PT1 if p == 0 else PT2
                    pb.use("vector", lambda e, j=j, p=p, t=t, nm=nm, rs=rs: e.tensor_scalar(
                        a0_xn[:, p][:, :], xt4[:, j][:, :], nm[:, t:t + 1], rs[:, t:t + 1],
                        ALU.add, ALU.mult),
                        reads=(sl_x4[j], sl_nm, sl_rs), writes=(sl_xn[p],))
                    def tp(e, p=p, PT=PT):
                        for c in range(8):
                            ins = e.transpose(PT[:, c * 128:(c + 1) * 128],
                                              a0_xn[:, p][:, c * 128:(c + 1) * 128], sb_ident[:])
                        return ins
                    pb.use("tensor", tp, reads=(sl_xn[p], S["consts"]), writes=(S["pt%d" % (p + 1)],))
                    pb.use("vector", lambda e, t=t, PT=PT: e.tensor_copy(
                        xn_ap[:, :, t * 128:(t + 1) * 128],
                        PT.ap()[:, 0:1024].rearrange("p (c s) -> p c s", s=128)),
                        reads=(S["pt%d" % (p + 1)],), writes=(S["xnf%d" % (t // 4)],))
            if "A1" not in phases:
                for g in range(4):
                    emit_a0_group(g)
            if "xn" in dbg and "A1" not in phases:
                pb.barrier()
                pb.use("scalar", lambda e: e.dma_start(out=dbg_out["xn"], in_=xn_fm[:]),
                       reads=(S["xnf0"], S["xnf1"], S["xnf2"], S["xnf3"]), dma=True)
                pb.barrier()

        if "A1" in phases:
            P = a01_stk.enter_context
            a1_bo = P(nc.sbuf_tensor("a1_bo", [128, 9], F32))
            a1_ev = P(nc.sbuf_tensor("a1_ev", [128, 2, 512], BF16))
            a1_sg = P(nc.sbuf_tensor("a1_sg", [128, 2, 512], BF16))
            sl_bo = slot("a1_bo")
            sl_ev = [slot("a1_ev0"), slot("a1_ev1")]
            sl_sg = [slot("a1_sg0"), slot("a1_sg1")]
            pb.use("sync", lambda e: e.dma_start(out=a1_bo[:, 0:8], in_=bo_c), writes=(sl_bo,), dma=True)
            pb.use("vector", lambda e: e.tensor_scalar_mul(a1_bo[:, 0:8], a1_bo[:, 0:8], 0.5),
                   reads=(sl_bo,), writes=(sl_bo,))

            for st in range(4):
                emit_a0_group(st)
                for oc in range(24):
                    pi = oc % 2
                    bank, sbank = PS[pi], PSLOT[pi]
                    def mms(e, st=st, oc=oc, bank=bank):
                        for kc in range(8):
                            if oc < 16:
                                lhs = w_in_sb[:, kc * 2048 + oc * 128: kc * 2048 + oc * 128 + 128]
                                rhs = xn_ap[:, kc, st * 512:(st + 1) * 512]
                            else:
                                o2 = oc - 16
                                lhs = w_o_sb[:, kc * 1024 + o2 * 128: kc * 1024 + o2 * 128 + 128]
                                rhs = xin_ap[:, kc, st * 512:(st + 1) * 512]
                            ins = e.matmul(bank[:, 0:512], lhs, rhs,
                                           start=(kc == 0), stop=(kc == 7), skip_group_check=True)
                        return ins
                    pb.use("tensor", mms,
                           reads=(S["xnf%d" % st], sl_win) if oc < 16 else (S["x_inner"], sl_wo),
                           writes=(sbank,))
                    if oc < 8:
                        pb.use("scalar", lambda e, oc=oc, st=st, bank=bank: e.activation(
                            xin_ap[:, oc, st * 512:(st + 1) * 512], bank[:, 0:512],
                            AF.Identity, bias=0.0),
                            reads=(sbank,), writes=(S["x_inner"],))
                    else:
                        if oc < 16:
                            c2, dsth = oc - 8, gsig_h
                            bias_ap = None
                        else:
                            c2, dsth = oc - 16, og_h
                            bias_ap = a1_bo
                        ep = oc % 2
                        def tanh_op(e, bank=bank, bias_ap=bias_ap, c2=c2, ep=ep):
                            b = bias_ap[:, c2:c2 + 1] if bias_ap is not None else 0.0
                            return e.activation(a1_ev[:, ep][:, :], bank[:, 0:512], AF.Tanh,
                                                bias=b, scale=0.5)
                        pb.use("scalar", tanh_op, reads=(sbank,) + ((sl_bo,) if bias_ap is not None else ()),
                               writes=(sl_ev[ep],))
                        pb.use("vector", lambda e, ep=ep: e.tensor_scalar(
                            a1_sg[:, ep][:, :], a1_ev[:, ep][:, :], 0.5, 0.5, ALU.mult, ALU.add),
                            reads=(sl_ev[ep],), writes=(sl_sg[ep],))
                        pb.use("scalar", lambda e, dsth=dsth, c2=c2, st=st, ep=ep: e.dma_start(
                            out=dsth[c2, :, st * 512:(st + 1) * 512], in_=a1_sg[:, ep][:, :]),
                            reads=(sl_sg[ep],), writes=(S["gsig_h" if dsth is gsig_h else "og_h"],),
                            dma=True)
            pb.barrier()
            if "xinner" in dbg:
                pb.use("scalar", lambda e: e.dma_start(out=dbg_out["xinner"], in_=x_inner[:]),
                       reads=(S["x_inner"],), dma=True)
            if "gsig" in dbg:
                pb.use("scalar", lambda e: e.dma_start(out=dbg_out["gsig"], in_=gsig_h), dma=True)
            if "og" in dbg:
                pb.use("scalar", lambda e: e.dma_start(out=dbg_out["og"], in_=og_h), dma=True)
            if "xn" in dbg:
                pb.use("scalar", lambda e: e.dma_start(out=dbg_out["xn"], in_=xn_fm[:]),
                       reads=(S["xnf0"], S["xnf1"], S["xnf2"], S["xnf3"]), dma=True)
            pb.barrier()
            a1w_stk.close()
        a01_stk.close()

        if "A2" in phases:
          with ExitStack() as pctx:
            P = pctx.enter_context
            selt = P(nc.sbuf_tensor("selt", [16, 512], F32))
            selbf = P(nc.sbuf_tensor("selbf", [16, 512], BF16))
            selflat = P(nc.sbuf_tensor("selflat", [1, 16 * 512], BF16))
            rbuf = P(nc.sbuf_tensor("rbuf", [128, 16 * 512], BF16))
            rp = P(nc.sbuf_tensor("rp", [128, 2, 128], BF16))
            a2_t = P(nc.sbuf_tensor("a2_t", [128, 2, 512], F32))
            a2_g = P(nc.sbuf_tensor("a2_g", [128, DM], BF16))
            sl_selt = slot("selt"); sl_selbf = slot("selbf"); sl_selflat = slot("selflat")
            sl_rbuf = slot("rbuf")
            sl_rp = [slot("rp0"), slot("rp1")]
            sl_a2t = [slot("a2_t0"), slot("a2_t1")]
            sl_a2g = slot("a2_g")

            for st in range(4):
                def selmm(e, st=st):
                    for kc in range(8):
                        ins = e.matmul(PS[4][0:16, 0:512], sel_sb[:, kc * 16:(kc + 1) * 16],
                                       xn_ap[:, kc, st * 512:(st + 1) * 512],
                                       start=(kc == 0), stop=(kc == 7), skip_group_check=True)
                    return ins
                pb.use("tensor", selmm, reads=(S["xnf%d" % st], sl_selw), writes=(S["ps4"],))
                pb.use("scalar", lambda e: e.activation(selt[0:16, :], PS[4][0:16, 0:512],
                                                        AF.Tanh, bias=0.0, scale=0.5),
                       reads=(S["ps4"],), writes=(sl_selt,))
                pb.use("vector", lambda e: e.tensor_scalar(selbf[0:16, :], selt[0:16, :],
                                                           0.5, 0.5, ALU.mult, ALU.add),
                       reads=(sl_selt,), writes=(sl_selbf,))
                pb.use("gpsimd", lambda e: e.dma_start(out=selflat[0:1, 0:8192], in_=selbf[0:16, 0:512]),
                       reads=(sl_selbf,), writes=(sl_selflat,), dma=True)
                for ex in range(16):
                    pi = ex % 2
                    def scmm(e, st=st, ex=ex, pi=pi):
                        for kc in range(8):
                            ins = e.matmul(PS[pi][:, 0:512],
                                           keys_sb[:, ex * 1024 + kc * 128: ex * 1024 + kc * 128 + 128],
                                           xn_ap[:, kc, st * 512:(st + 1) * 512],
                                           start=(kc == 0), stop=(kc == 7), skip_group_check=True)
                        return ins
                    pb.use("tensor", scmm, reads=(S["xnf%d" % st], sl_keys), writes=(PSLOT[pi],))
                    pb.use("tensor", lambda e, ex=ex, pi=pi: e.matmul(
                        PS[2 + pi][:, 0:512], sb_ones[0:1, 0:128],
                        selflat[0:1, ex * 512: ex * 512 + 512],
                        start=True, stop=True, skip_group_check=True),
                        reads=(sl_selflat, S["consts"]), writes=(S["ps%d" % (2 + pi)],))
                    pb.use("scalar", lambda e, ex=ex, pi=pi: e.activation(
                        rbuf[:, ex * 512:(ex + 1) * 512], PS[pi][:, 0:512], AF.Relu),
                        reads=(PSLOT[pi],), writes=(sl_rbuf,))
                    pb.use("vector", lambda e, ex=ex, pi=pi: e.tensor_tensor(
                        rbuf[:, ex * 512:(ex + 1) * 512], rbuf[:, ex * 512:(ex + 1) * 512],
                        PS[2 + pi][:, 0:512], ALU.mult),
                        reads=(sl_rbuf, S["ps%d" % (2 + pi)]), writes=(sl_rbuf,))
                for i in range(4):
                    tg = st * 4 + i
                    pA, pB2 = (4, 5) if i % 2 == 0 else (2, 3)
                    def vmm_all(e, i=i, pA=pA, pB2=pB2):
                        for ex in range(16):
                            lh = rbuf[:, ex * 512 + i * 128: ex * 512 + i * 128 + 128]
                            e.matmul(PS[pA][:, 0:512], lh, vals_sb[:, ex * 1024: ex * 1024 + 512],
                                     start=(ex == 0), stop=(ex == 15), skip_group_check=True)
                            ins = e.matmul(PS[pB2][:, 0:512], lh,
                                           vals_sb[:, ex * 1024 + 512: ex * 1024 + 1024],
                                           start=(ex == 0), stop=(ex == 15), skip_group_check=True)
                        return ins
                    pb.use("tensor", vmm_all, reads=(sl_rbuf, sl_vals),
                           writes=(S["ps%d" % pA], S["ps%d" % pB2]))
                    for v, pv in enumerate((pA, pB2)):
                        pb.use("scalar", lambda e, v=v, pv=pv: e.activation(
                            a2_t[:, v], PS[pv][:, 0:512], AF.Tanh, bias=0.0, scale=0.5),
                            reads=(S["ps%d" % pv],), writes=(sl_a2t[v],))
                        pb.use("vector", lambda e, v=v: e.tensor_scalar(
                            a2_t[:, v], a2_t[:, v], 0.5, 0.5, ALU.mult, ALU.add),
                            reads=(sl_a2t[v],), writes=(sl_a2t[v],))
                        pb.use("vector", lambda e, v=v, pv=pv: e.tensor_tensor(
                            a2_g[:, v * 512:(v + 1) * 512], a2_t[:, v], PS[pv][:, 0:512], ALU.mult),
                            reads=(sl_a2t[v], S["ps%d" % pv]), writes=(sl_a2g,))
                    pb.use("scalar", lambda e, tg=tg: e.dma_start(out=g0_h[tg], in_=a2_g[:]),
                           reads=(sl_a2g,), writes=(S["g0_h"],), dma=True)
            pb.barrier()
            if "g0" in dbg:
                pb.use("scalar", lambda e: e.dma_start(out=dbg_out["g0"], in_=g0_h), dma=True)
                pb.barrier()
          a2w_stk.close()

        a1w_stk.close()
        a2w_stk.close()
        xn_stk.close()

        if "B" in phases:
          with ExitStack() as pctx:
            P = pctx.enter_context
            sb_maskat = P(nc.sbuf_tensor("sb_maskat", [128, 128], BF16))
            sb_maskat4 = P(nc.sbuf_tensor("sb_maskat4", [128, 512], BF16))
            sb_ind = P(nc.sbuf_tensor("sb_ind", [16, 8 * 128], BF16))
            sb_bones = P(nc.sbuf_tensor("sb_bones", [128, 8 * 16], BF16))
            slot("bconst")
            for dst, src in ((sb_maskat, mask_at), (sb_ind, ind_all), (sb_bones, bones)):
                pb.use("sync", lambda e, dst=dst, src=src: e.dma_start(out=dst[:], in_=src),
                       writes=(S["bconst"],), dma=True)
            wq_sb = P(nc.sbuf_tensor("wq_sb", [128, 8 * 1024], BF16))
            wk_sb = P(nc.sbuf_tensor("wk_sb", [128, 8 * 1024], BF16))
            wv_sb = P(nc.sbuf_tensor("wv_sb", [128, 8 * 1024], BF16))
            wi_sb = P(nc.sbuf_tensor("wi_sb", [128, 8 * 1024], BF16))
            wf_sb = P(nc.sbuf_tensor("wf_sb", [128, 8 * 1024], BF16))
            bcol = P(nc.sbuf_tensor("bcol", [128, 16], F32))
            nwc = P(nc.sbuf_tensor("nwc", [128, 2], F32))
            qhat = P(nc.sbuf_tensor("qhat", [128, 8 * 256], BF16))
            khat = P(nc.sbuf_tensor("khat", [128, 8 * 256], BF16))
            vbf = P(nc.sbuf_tensor("vbf", [128, 8 * 256], BF16))
            ibf = P(nc.sbuf_tensor("ibf", [128, 8 * 256], BF16))
            qraw = P(nc.sbuf_tensor("qraw", [128, 8 * 256], BF16))
            rstdb = P(nc.sbuf_tensor("rstdb", [16, 256], BF16))
            rstdf = P(nc.sbuf_tensor("rstdf", [16, 256], F32))
            fpre = P(nc.sbuf_tensor("fpre", [128, 8 * 256], F32))
            fb1 = P(nc.sbuf_tensor("fb1", [128, 8 * 256], F32))
            lfb = P(nc.sbuf_tensor("lfb", [128, 8 * 256], F32))
            cumf = P(nc.sbuf_tensor("cumf", [128, 8 * 128], F32))
            eqb = P(nc.sbuf_tensor("eqb", [128, 8 * 128], BF16))
            enb = P(nc.sbuf_tensor("enb", [128, 8 * 128], BF16))
            qtl = P(nc.sbuf_tensor("qtl", [128, 8 * 128], BF16))
            qtlz = P(nc.sbuf_tensor("qtlz", [128, 2, 8 * 128], BF16))
            ktl = P(nc.sbuf_tensor("ktl", [128, 8 * 128], BF16))
            ktok = P(nc.sbuf_tensor("ktok", [128, 8 * 128], BF16))
            vaug = P(nc.sbuf_tensor("vaug", [128, 16 * 65], BF16))
            asb = P(nc.sbuf_tensor("asb", [128, 16 * 128], BF16))
            call = P(nc.sbuf_tensor("call", [128, 8 * 65], F32))
            cbfz = P(nc.sbuf_tensor("cbfz", [128, 2, 8 * 65], BF16))
            fend = P(nc.sbuf_tensor("fend", [128, 2, 8], F32))
            dens = P(nc.sbuf_tensor("dens", [128, 16], F32))
            rden = P(nc.sbuf_tensor("rden", [128, 16], F32))
            htok = P(nc.sbuf_tensor("htok", [128, 1024], BF16))
            ogs = P(nc.sbuf_tensor("ogs", [128, 8 * 128], BF16))
            for nm in ("wq", "wk", "wv", "wi", "wf", "bcol", "nwc", "qhat", "khat",
                       "vbf", "ibf", "qraw", "sqb", "rstdb", "rstdf", "fpre", "fb1", "lfb",
                       "cumf", "eqb", "enb", "qtl", "qtlz0", "qtlz1", "ktl", "ktok", "vaug", "asb",
                       "call", "cbfz", "fend", "dens", "rden", "htok", "ogs", "hgf"):
                slot(nm)
            for w, src, sn in ((wf_sb, cw_f, "wf"), (wq_sb, cw_q, "wq"), (wk_sb, cw_k, "wk"),
                               (wv_sb, cw_v, "wv"), (wi_sb, cw_i, "wi")):
                pb.use("gpsimd", lambda e, w=w, src=src: e.dma_start(out=w[:], in_=src),
                       writes=(S[sn],), dma=True)
            pb.use("sync", lambda e: e.dma_start(out=bcol[:, 0:8], in_=bi_c), writes=(S["bcol"],), dma=True)
            pb.use("sync", lambda e: e.dma_start(out=bcol[:, 8:16], in_=bf_c), writes=(S["bcol"],), dma=True)
            pb.use("sync", lambda e: e.dma_start(out=nwc[:, 0:1], in_=qnw_c), writes=(S["nwc"],), dma=True)
            pb.use("sync", lambda e: e.dma_start(out=nwc[:, 1:2], in_=knw_c), writes=(S["nwc"],), dma=True)
            pb.use("vector", lambda e: e.memset(vaug[:], 1.0), writes=(S["vaug"],))
            def mk_mask4(e):
                for c in range(4):
                    ins = e.tensor_copy(sb_maskat4[:, c * 128:(c + 1) * 128], sb_maskat[:])
                return ins
            pb.use("vector", mk_mask4, reads=(S["bconst"],), writes=(S["bconst"],))
            pb.use("vector", lambda e: e.memset(qtlz[:], 0.0), writes=(S["qtlz0"], S["qtlz1"]))
            pb.use("vector", lambda e: e.memset(call[:], 0.0), writes=(S["call"],))
            pb.use("vector", lambda e: e.memset(cbfz[:], 0.0), writes=(S["cbfz"],))

            qhat3 = qhat.ap().rearrange("p (c s) -> p c s", s=256)
            khat3 = khat.ap().rearrange("p (c s) -> p c s", s=256)
            vbf3 = vbf.ap().rearrange("p (c s) -> p c s", s=256)
            ibf3 = ibf.ap().rearrange("p (c s) -> p c s", s=256)
            c65 = call.ap().rearrange("p (b c) -> p b c", c=65)

            vaug65 = vaug.ap().rearrange("p (b c) -> p b c", c=65)
            eq3 = eqb.ap().rearrange("p (c s) -> p c s", s=128)
            HOUT = [PS[0], PS[1], PS[4], PS[5]]
            HSL = [S["ps0"], S["ps1"], S["ps4"], S["ps5"]]
            AT_ORDER = [0, 2, 4, 6, 8, 10, 12, 14, 1, 3, 5, 7, 9, 11, 13, 15]
            AT_POS = [AT_ORDER.index(h) for h in range(16)]

            def proj_norm(wsb, wsl, st, dst, dstsl, nwslice):
                for oc in range(8):
                    pi = oc % 2
                    def mm(e, oc=oc, st=st, wsb=wsb, pi=pi):
                        for kc in range(8):
                            ins = e.matmul(PS[pi][:, 0:256],
                                           wsb[:, kc * 1024 + oc * 128: kc * 1024 + oc * 128 + 128],
                                           xin_ap[:, kc, st * 256:(st + 1) * 256],
                                           start=(kc == 0), stop=(kc == 7), skip_group_check=True)
                        return ins
                    pb.use("tensor", mm, reads=(S["x_inner"], wsl), writes=(PSLOT[pi],))
                    pb.use("vector", lambda e, oc=oc, pi=pi: e.tensor_copy(
                        qraw[:, oc * 256:(oc + 1) * 256], PS[pi][:, 0:256]),
                        reads=(PSLOT[pi],), writes=(S["qraw"],))
                pb.use("vector", lambda e: e.tensor_tensor(asb[:], qraw[:], qraw[:], ALU.mult),
                       reads=(S["qraw"],), writes=(S["asb"],))
                def statmm(e):
                    for kc in range(8):
                        ins = e.matmul(PS[2][0:16, 0:256], sb_bones[:, kc * 16:(kc + 1) * 16],
                                       asb[:, kc * 256:(kc + 1) * 256],
                                       start=(kc == 0), stop=(kc == 7), skip_group_check=True)
                    return ins
                pb.use("tensor", statmm, reads=(S["asb"], S["bconst"]), writes=(S["ps2"],))
                pb.use("scalar", lambda e: e.activation(rstdf[0:16, 0:256], PS[2][0:16, 0:256],
                                                        AF.Ln, bias=sb_cst[0:16, 1:2], scale=1.0 / 64),
                       reads=(S["ps2"], S["consts"]), writes=(S["rstdf"],))
                pb.use("scalar", lambda e: e.activation(rstdb[0:16, 0:256], rstdf[0:16, 0:256],
                                                        AF.Exp, bias=0.0, scale=-0.5),
                       reads=(S["rstdf"],), writes=(S["rstdb"],))
                for oc in range(8):
                    pi = oc % 2
                    pb.use("tensor", lambda e, oc=oc, pi=pi: e.matmul(
                        PS[2 + pi][:, 0:256], sb_ind[:, oc * 128:(oc + 1) * 128],
                        rstdb[0:16, 0:256], start=True, stop=True, skip_group_check=True),
                        reads=(S["rstdb"], S["bconst"]), writes=(S["ps%d" % (2 + pi)],))
                    pb.use("vector", lambda e, oc=oc, pi=pi, dst=dst, nwslice=nwslice: e.scalar_tensor_tensor(
                        dst[:, oc * 256:(oc + 1) * 256], qraw[:, oc * 256:(oc + 1) * 256],
                        nwslice, PS[2 + pi][:, 0:256], ALU.mult, ALU.mult),
                        reads=(S["qraw"], S["nwc"], S["ps%d" % (2 + pi)]), writes=(dstsl,))

            for st in range(8):
                for oc in range(8):
                    pi = 2 + oc % 2
                    def mm(e, oc=oc, st=st, pi=pi):
                        for kc in range(8):
                            ins = e.matmul(PS[pi][:, 0:256],
                                           wf_sb[:, kc * 1024 + oc * 128: kc * 1024 + oc * 128 + 128],
                                           xin_ap[:, kc, st * 256:(st + 1) * 256],
                                           start=(kc == 0), stop=(kc == 7), skip_group_check=True)
                        return ins
                    pb.use("tensor", mm, reads=(S["x_inner"], S["wf"]), writes=(S["ps%d" % pi],))
                    pb.use("scalar", lambda e, oc=oc, pi=pi: e.activation(
                        fpre[:, oc * 256:(oc + 1) * 256], PS[pi][:, 0:256],
                        AF.Identity, bias=bcol[:, 8 + oc:9 + oc], scale=1.0),
                        reads=(S["ps%d" % pi], S["bcol"]), writes=(S["fpre"],))
                pb.use("scalar", lambda e: e.activation(fb1[:], fpre[:], AF.Abs),
                       reads=(S["fpre"],), writes=(S["fb1"],))
                pb.use("scalar", lambda e: e.activation(lfb[:], fb1[:], AF.Exp, bias=0.0, scale=-1.0),
                       reads=(S["fb1"],), writes=(S["lfb"],))
                pb.use("scalar", lambda e: e.activation(fb1[:], lfb[:], AF.Ln, bias=sb_cst[:, 2:3], scale=1.0),
                       reads=(S["lfb"], S["consts"]), writes=(S["fb1"],))
                pb.use("vector", lambda e: e.tensor_scalar(lfb[:], fpre[:], 0.0, None, ALU.min),
                       reads=(S["fpre"],), writes=(S["lfb"],))
                pb.use("vector", lambda e: e.tensor_tensor(lfb[:], lfb[:], fb1[:], ALU.subtract),
                       reads=(S["lfb"], S["fb1"]), writes=(S["lfb"],))

                def emit_prep(i, st=st):
                    if st * 2 + i >= B_TILES:
                        return
                    def scans(e, i=i):
                        for kc in range(8):
                            for cc in range(2):
                                ins = e.tensor_tensor_scan(
                                    cumf[:, kc * 128 + cc * 64: kc * 128 + cc * 64 + 64],
                                    lfb[:, kc * 256 + i * 128 + cc * 64: kc * 256 + i * 128 + cc * 64 + 64],
                                    sb_zeros[:, 0:64], 0.0, ALU.add, ALU.add)
                        return ins
                    pb.use("vector", scans, reads=(S["lfb"], S["consts"]), writes=(S["cumf"],))
                    if B_STAGE < 1:
                        return
                    def eewave(e):
                        e.activation(eqb[:], cumf[:], AF.Exp, bias=0.0, scale=1.0)
                        return e.activation(enb[:], cumf[:], AF.Exp, bias=0.0, scale=-1.0)
                    pb.use("scalar", eewave, reads=(S["cumf"],), writes=(S["eqb"], S["enb"]))

                emit_prep(0)
                proj_norm(wq_sb, S["wq"], st, qhat, S["qhat"], nwc[:, 0:1])
                proj_norm(wk_sb, S["wk"], st, khat, S["khat"], nwc[:, 1:2])
                for which, wsb, wsl in (("v", wv_sb, S["wv"]), ("i", wi_sb, S["wi"])):
                    for oc in range(8):
                        pi = oc % 2
                        def mm(e, oc=oc, st=st, wsb=wsb, pi=pi):
                            for kc in range(8):
                                ins = e.matmul(PS[pi][:, 0:256],
                                               wsb[:, kc * 1024 + oc * 128: kc * 1024 + oc * 128 + 128],
                                               xin_ap[:, kc, st * 256:(st + 1) * 256],
                                               start=(kc == 0), stop=(kc == 7), skip_group_check=True)
                            return ins
                        pb.use("tensor", mm, reads=(S["x_inner"], wsl), writes=(PSLOT[pi],))
                        if which == "v":
                            pb.use("vector", lambda e, oc=oc, pi=pi: e.tensor_copy(
                                vbf[:, oc * 256:(oc + 1) * 256], PS[pi][:, 0:256]),
                                reads=(PSLOT[pi],), writes=(S["vbf"],))
                        else:
                            pb.use("scalar", lambda e, oc=oc, pi=pi: e.activation(
                                ibf[:, oc * 256:(oc + 1) * 256], PS[pi][:, 0:256],
                                AF.Exp, bias=bcol[:, oc:oc + 1], scale=1.0),
                                reads=(PSLOT[pi], S["bcol"]), writes=(S["ibf"],))

                for i in range(2):
                    T0 = st * 256 + i * 128
                    if st * 2 + i >= B_TILES:
                        continue
                    if B_STAGE < 1:
                        continue
                    def qkwave_a(e, i=i):
                        e.tensor_tensor(qtl.ap().rearrange("p (c s) -> p c s", s=128),
                                        qhat3[:, :, i * 128:(i + 1) * 128], eq3, ALU.mult)
                        return e.tensor_tensor(ktl.ap().rearrange("p (c s) -> p c s", s=128),
                                               khat3[:, :, i * 128:(i + 1) * 128],
                                               ibf3[:, :, i * 128:(i + 1) * 128], ALU.mult)
                    pb.use("vector", qkwave_a,
                           reads=(S["qhat"], S["eqb"], S["khat"], S["ibf"]),
                           writes=(S["qtl"], S["ktl"]))
                    def qkwave_b(e):
                        for cc2 in range(2):
                            e.tensor_copy(
                                qtlz.ap()[:, cc2].rearrange("p (c s) -> p c s", s=128)[:, :, cc2 * 64:(cc2 + 1) * 64],
                                qtl.ap().rearrange("p (c s) -> p c s", s=128)[:, :, cc2 * 64:(cc2 + 1) * 64])
                        for cc3 in range(2):
                            e.tensor_copy(
                                fend.ap()[:, cc3].rearrange("p (b c) -> p b c", c=1),
                                eq3[:, :, cc3 * 64 + 63: cc3 * 64 + 64])
                        return e.tensor_tensor(
                            ktl.ap().rearrange("p (c s) -> p c s", s=128),
                            ktl.ap().rearrange("p (c s) -> p c s", s=128),
                            enb.ap().rearrange("p (c s) -> p c s", s=128), ALU.mult)
                    pb.use("vector", qkwave_b,
                           reads=(S["qtl"], S["eqb"], S["ktl"], S["enb"]),
                           writes=(S["qtlz0"], S["qtlz1"], S["ktl"], S["fend"]))
                    if i == 0:
                        emit_prep(1)
                    if B_STAGE < 2:
                        continue
                    def tpk(e):
                        for c in range(8):
                            ins = e.transpose(PT1[:, c * 128:(c + 1) * 128],
                                              ktl[:, c * 128:(c + 1) * 128], sb_ident[:])
                        return ins
                    pb.use("tensor", tpk, reads=(S["ktl"], S["consts"]), writes=(S["pt1"],))
                    pb.use("vector", lambda e: e.tensor_copy(ktok[:], PT1.ap()[:, 0:1024]),
                           reads=(S["pt1"],), writes=(S["ktok"],))
                    def tpv(e, i=i):
                        for c in range(8):
                            ins = e.transpose(PT2[:, c * 128:(c + 1) * 128],
                                              vbf[:, c * 256 + i * 128: c * 256 + i * 128 + 128],
                                              sb_ident[:])
                        return ins
                    pb.use("tensor", tpv, reads=(S["vbf"], S["consts"]), writes=(S["pt2"],))
                    pb.use("vector", lambda e: e.tensor_copy(
                        vaug65[:, :, 0:64],
                        PT2.ap()[:, 0:1024].rearrange("p (b c) -> p b c", c=64)),
                        reads=(S["pt2"],), writes=(S["vaug"],))
                    if B_STAGE < 3:
                        continue
                    for g in range(4):
                        pg = 2 + g % 2
                        def atmm(e, g=g, pg=pg):
                            for j in range(4):
                                h = AT_ORDER[g * 4 + j]
                                rows = slice(64 * (h % 2), 64 * (h % 2) + 64)
                                cols = slice((h // 2) * 128, (h // 2) * 128 + 128)
                                ins = e.matmul(PS[pg][:, j * 128:(j + 1) * 128],
                                               ktl[rows, cols], qtl[rows, cols],
                                               start=(j == 0), stop=(j == 3),
                                               skip_group_check=True)
                            return ins
                        pb.use("tensor", atmm, reads=(S["ktl"], S["qtl"]),
                               writes=(S["ps%d" % pg],))
                        pb.use("vector", lambda e, g=g, pg=pg: e.tensor_tensor(
                            asb[:, g * 512:(g + 1) * 512], PS[pg][:, 0:512],
                            sb_maskat4[:], ALU.mult),
                            reads=(S["ps%d" % pg], S["bconst"]), writes=(S["asb"],))
                    if B_STAGE < 4:
                        continue
                    for bk in range(4):
                        def intmm(e, bk=bk):
                            for bl in range(4):
                                h = bk * 4 + bl
                                ins = e.matmul(HOUT[bk][:, bl * 128: bl * 128 + 65],
                                               asb[:, AT_POS[h] * 128:AT_POS[h] * 128 + 128],
                                               vaug[:, h * 65:(h + 1) * 65],
                                               start=(bl == 0), stop=False, skip_group_check=True)
                            return ins
                        pb.use("tensor", intmm, reads=(S["asb"], S["vaug"]), writes=(HSL[bk],))
                    if B_STAGE < 5:
                        continue
                    for cc in range(2 if B_SUB >= 4 else 1):
                        def intermm(e, cc=cc):
                            for h in range(16):
                                bk, bl = h // 4, h % 4
                                hcols = slice((h // 2) * 128, (h // 2) * 128 + 128)
                                ins = e.matmul(HOUT[bk][:, bl * 128: bl * 128 + 65],
                                               qtlz[:, cc][:, hcols],
                                               cbfz[:, h % 2][:, (h // 2) * 65:(h // 2) * 65 + 65],
                                               start=False, stop=(cc == 1), skip_group_check=True)
                            return ins
                        pb.use("tensor", intermm, reads=(S["qtlz%d" % cc], S["cbfz"]),
                               writes=(HSL[0], HSL[1], HSL[2], HSL[3]))
                        if B_SUB >= 2:
                            for par in range(2):
                                def dmm(e, cc=cc, par=par):
                                    for h2 in range(8):
                                        h = h2 * 2 + par
                                        rows = slice(64 * par, 64 * par + 64)
                                        e.matmul(PS[2][rows, (h // 2) * 64: (h // 2) * 64 + 64],
                                                 ktok[cc * 64:(cc + 1) * 64, h * 64:(h + 1) * 64],
                                                 vaug[cc * 64:(cc + 1) * 64, h * 65: h * 65 + 64],
                                                 start=True, stop=True, skip_group_check=True)
                                        ins = e.matmul(PS[3][rows, (h // 2) + 8 * cc: (h // 2) + 8 * cc + 1],
                                                       ktok[cc * 64:(cc + 1) * 64, h * 64:(h + 1) * 64],
                                                       vaug[cc * 64:(cc + 1) * 64, h * 65 + 64: h * 65 + 65],
                                                       start=True, stop=True, skip_group_check=True)
                                    return ins
                                pb.use("tensor", dmm, reads=(S["ktok"], S["vaug"]),
                                       writes=(S["ps2"], S["ps3"]))
                        if B_SUB < 3:
                            continue
                        def supd1(e, cc=cc):
                            e.tensor_tensor(c65[:, :, 0:64], c65[:, :, 0:64],
                                            PS[2].ap()[:, 0:512].rearrange("p (b c) -> p b c", c=64),
                                            ALU.add)
                            return e.tensor_tensor(
                                c65[:, :, 64:65], c65[:, :, 64:65],
                                PS[3].ap()[:, 8 * cc: 8 * cc + 8].rearrange("p (b c) -> p b c", c=1),
                                ALU.add)
                        pb.use("vector", supd1, reads=(S["ps2"], S["ps3"], S["call"]),
                               writes=(S["call"],))
                        def supd2(e, cc=cc):
                            for b in range(8):
                                ins = e.tensor_scalar(call[:, b * 65:(b + 1) * 65],
                                                      call[:, b * 65:(b + 1) * 65],
                                                      fend[:, cc][:, b:b + 1], None, ALU.mult)
                            return ins
                        pb.use("vector", supd2, reads=(S["call"], S["fend"]),
                               writes=(S["call"],))
                        def supd3(e):
                            e.tensor_copy(cbfz[:, 0][0:64, :], call[0:64, :])
                            return e.tensor_copy(cbfz[:, 1][64:128, :], call[64:128, :])
                        pb.use("vector", supd3, reads=(S["call"],), writes=(S["cbfz"],))
                    if B_STAGE < 6:
                        continue
                    def fdens(e):
                        for bk in range(4):
                            ins = e.tensor_scalar(
                                dens.ap()[:, bk * 4:(bk + 1) * 4].rearrange("p (b c) -> p b c", c=1),
                                HOUT[bk].ap()[:, 0:512].rearrange("p (b c) -> p b c", c=128)[:, :, 64:65],
                                1.0, None, ALU.max)
                        return ins
                    pb.use("vector", fdens, reads=(HSL[0], HSL[1], HSL[2], HSL[3]),
                           writes=(S["dens"],))
                    pb.use("vector", lambda e: e.reciprocal(rden[:], dens[:]),
                           reads=(S["dens"],), writes=(S["rden"],))
                    def fhtok(e):
                        for h in range(16):
                            bk, bl = h // 4, h % 4
                            ins = e.tensor_scalar(
                                htok[:, h * 64:(h + 1) * 64], HOUT[bk][:, bl * 128: bl * 128 + 64],
                                rden[:, h:h + 1], None, ALU.mult)
                        return ins
                    pb.use("vector", fhtok, reads=(HSL[0], HSL[1], HSL[2], HSL[3], S["rden"]),
                           writes=(S["htok"],))
                    if B_STAGE < 7:
                        continue
                    def tph(e):
                        for c in range(8):
                            ins = e.transpose(PT1[:, c * 128:(c + 1) * 128],
                                              htok[:, c * 128:(c + 1) * 128], sb_ident[:])
                        return ins
                    pb.use("tensor", tph, reads=(S["htok"], S["consts"]), writes=(S["pt1"],))
                    pb.use("sync", lambda e, T0=T0: e.dma_start(
                        out=ogs[:], in_=og_h[:, :, T0:T0 + 128].rearrange("o p f -> p o f")),
                        reads=(S["og_h"],), writes=(S["ogs"],), dma=True)
                    pb.use("vector", lambda e: e.tensor_tensor(ogs[:], PT1.ap()[:, 0:1024], ogs[:], ALU.mult),
                           reads=(S["pt1"], S["ogs"]), writes=(S["ogs"],))
                    pb.use("scalar", lambda e, T0=T0: e.dma_start(
                        out=hg_h[:, :, T0:T0 + 128].rearrange("o p f -> p o f"), in_=ogs[:]),
                        reads=(S["ogs"],), writes=(S["hg_h"],), dma=True)
            pb.barrier()
            if "hg" in dbg:
                pb.use("scalar", lambda e: e.dma_start(out=dbg_out["hg"], in_=hg_h), dma=True)
                pb.barrier()
        xin_stk.close()

        if "D0" in phases:
            xn1_fm = E(nc.sbuf_tensor("xn1_fm", [128, 8 * SEQ], BF16))
            xn1_ap = xn1_fm.ap().rearrange("p (c s) -> p c s", s=SEQ)
            for _n in ("xn1f0", "xn1f1", "xn1f2", "xn1f3"):
                slot(_n)
        if "C" in phases:
          with ExitStack() as pctx:
            P = pctx.enter_context
            wout_sb = P(nc.sbuf_tensor("wout_sb", [128, 8 * 1024], BF16))
            slot("wout")
            pb.use("gpsimd", lambda e: e.dma_start(out=wout_sb[:], in_=cw_out),
                   writes=(S["wout"],), dma=True)
            if "D0" in phases:
                gw_sb = P(nc.sbuf_tensor("gw_sb", [128, 8 * 8], BF16))
                gb_sb = P(nc.sbuf_tensor("gb_sb", [128, 8], F32))
                d0_xn = P(nc.sbuf_tensor("d0_xn", [128, 2, DM], BF16))
                d0_sq = P(nc.sbuf_tensor("d0_sq", [128, 2, DM], F32))
                d0_st = P(nc.sbuf_tensor("d0_st", [128, 6, 16], F32))
                d0_r = P(nc.sbuf_tensor("d0_r", [128, 4, 32], F32))
                for nm in ("gw", "gb", "d0_xn0", "d0_xn1", "d0_sq", "d0_r",
                           "d0_sA", "d0_sB", "d0_nm", "d0_rs", "d0_t4", "d0_t5"):
                    slot(nm)
                pb.use("gpsimd", lambda e: e.dma_start(out=gw_sb[:], in_=cgw), writes=(S["gw"],), dma=True)
                pb.use("sync", lambda e: e.dma_start(out=gb_sb[:], in_=gate_b_bc), writes=(S["gb"],), dma=True)
            hg_ld = P(nc.sbuf_tensor("hg_ld", [128, 8 * 512], BF16))
            gs_ld = P(nc.sbuf_tensor("gs_ld", [128, 8 * 512], BF16))
            hgm = P(nc.sbuf_tensor("hgm", [128, 8 * 512], BF16))
            s_bf = P(nc.sbuf_tensor("s_bf", [128, 8 * 512], BF16))
            g0_ld = P(nc.sbuf_tensor("g0_ld", [128, 2, DM], BF16))
            x0_ld = P(nc.sbuf_tensor("x0_ld", [128, 2, DM], F32))
            sgf = P(nc.sbuf_tensor("sgf", [128, DM], F32))
            x1b = P(nc.sbuf_tensor("x1b", [128, 2, DM], F32))
            for nm in ("wout", "hg_ld", "gs_ld", "hgm", "s_bf", "g0_ld0", "g0_ld1",
                       "x0_ld0", "x0_ld1", "sgf", "x1b0", "x1b1"):
                slot(nm)
            for st in range(4):
                pb.use("sync", lambda e, st=st: e.dma_start(
                    out=hg_ld.ap().rearrange("p (c s) -> p c s", s=512),
                    in_=hg_h[:, :, st * 512:(st + 1) * 512].rearrange("o p f -> p o f")),
                    reads=(S["hg_h"],), writes=(S["hg_ld"],), dma=True)
                pb.use("sync", lambda e, st=st: e.dma_start(
                    out=gs_ld.ap().rearrange("p (c s) -> p c s", s=512),
                    in_=gsig_h[:, :, st * 512:(st + 1) * 512].rearrange("o p f -> p o f")),
                    reads=(S["gsig_h"],), writes=(S["gs_ld"],), dma=True)
                pb.use("vector", lambda e: e.tensor_tensor(hgm[:], hg_ld[:], gs_ld[:], ALU.mult),
                       reads=(S["hg_ld"], S["gs_ld"]), writes=(S["hgm"],))
                for oc in range(8):
                    pi = oc % 2
                    def mm(e, oc=oc, pi=pi):
                        for kc in range(8):
                            ins = e.matmul(PS[pi][:, 0:512],
                                           wout_sb[:, kc * 1024 + oc * 128: kc * 1024 + oc * 128 + 128],
                                           hgm[:, kc * 512:(kc + 1) * 512],
                                           start=(kc == 0), stop=(kc == 7), skip_group_check=True)
                        return ins
                    pb.use("tensor", mm, reads=(S["hgm"], S["wout"]), writes=(PSLOT[pi],))
                    pb.use("vector", lambda e, oc=oc, pi=pi: e.tensor_copy(
                        s_bf[:, oc * 512:(oc + 1) * 512], PS[pi][:, 0:512]),
                        reads=(PSLOT[pi],), writes=(S["s_bf"],))
                for i in range(4):
                    tg = st * 4 + i
                    p2 = i % 2
                    def tps(e, i=i):
                        for c in range(8):
                            ins = e.transpose(PT1[:, c * 128:(c + 1) * 128],
                                              s_bf[:, c * 512 + i * 128: c * 512 + i * 128 + 128],
                                              sb_ident[:])
                        return ins
                    pb.use("tensor", tps, reads=(S["s_bf"], S["consts"]), writes=(S["pt1"],))
                    pb.use("sync", lambda e, tg=tg, p2=p2: e.dma_start(out=g0_ld[:, p2], in_=g0_h[tg]),
                           reads=(S["g0_h"],), writes=(S["g0_ld%d" % p2],), dma=True)
                    pb.use("sync", lambda e, tg=tg, p2=p2: e.dma_start(out=x0_ld[:, p2], in_=x0[tg]),
                           writes=(S["x0_ld%d" % p2],), dma=True)
                    pb.use("vector", lambda e, p2=p2: e.tensor_tensor(
                        sgf[:], PT1.ap()[:, 0:1024], g0_ld[:, p2], ALU.mult),
                        reads=(S["pt1"], S["g0_ld%d" % p2]), writes=(S["sgf"],))
                    pb.use("vector", lambda e, p2=p2: e.tensor_tensor(
                        x1b[:, p2], sgf[:], x0_ld[:, p2], ALU.add),
                        reads=(S["sgf"], S["x0_ld%d" % p2]), writes=(S["x1b%d" % p2],))
                    pb.use("scalar", lambda e, tg=tg, p2=p2: e.dma_start(out=x1_h[tg], in_=x1b[:, p2]),
                           reads=(S["x1b%d" % p2],), writes=(S["x1_h"],), dma=True)
                    if "D0" not in phases:
                        continue
                    t = tg
                    sAd = d0_st.ap()[:, 0]; sBd = d0_st.ap()[:, 1]
                    pb.use("vector", lambda e, t=t, p2=p2, sAd=sAd: e.reduce_sum(
                        sAd[:, t:t + 1], x1b[:, p2][:, :], AX.X, negate=True),
                        reads=(S["x1b%d" % p2],), writes=(S["d0_sA"],))
                    pb.use("scalar", lambda e, t=t, p2=p2, sBd=sBd: e.activation(
                        d0_sq[:, p2][:, :], x1b[:, p2][:, :], AF.Square, accum_out=sBd[:, t:t + 1]),
                        reads=(S["x1b%d" % p2], S["d0_sq"]), writes=(S["d0_sq"], S["d0_sB"]))
            if "D0" in phases:
                sAd = d0_st.ap()[:, 0]; sBd = d0_st.ap()[:, 1]
                nm1 = d0_st.ap()[:, 2]; rs1 = d0_st.ap()[:, 3]
                t4d = d0_st.ap()[:, 4]; t5d = d0_st.ap()[:, 5]
                pb.use("vector", lambda e, nm1=nm1, sAd=sAd: e.tensor_scalar_mul(
                    nm1[:], sAd[:], 1.0 / DM), reads=(S["d0_sA"],), writes=(S["d0_nm"],))
                pb.use("vector", lambda e, nm1=nm1, t4d=t4d: e.tensor_tensor(
                    t4d[:], nm1[:], nm1[:], ALU.mult), reads=(S["d0_nm"],), writes=(S["d0_t4"],))
                pb.use("vector", lambda e, sBd=sBd, t4d=t4d, t5d=t5d: e.scalar_tensor_tensor(
                    t5d[:], sBd[:], 1.0 / DM, t4d[:], ALU.mult, ALU.subtract),
                    reads=(S["d0_sB"], S["d0_t4"]), writes=(S["d0_t5"],))
                pb.use("scalar", lambda e, t4d=t4d, t5d=t5d: e.activation(
                    t4d[:], t5d[:], AF.Ln, bias=sb_cst[:, 0:1], scale=1.0),
                    reads=(S["d0_t5"], S["consts"]), writes=(S["d0_t4"],))
                pb.use("scalar", lambda e, rs1=rs1, t4d=t4d: e.activation(
                    rs1[:], t4d[:], AF.Exp, bias=0.0, scale=-0.5),
                    reads=(S["d0_t4"],), writes=(S["d0_rs"],))
                for st in range(4):
                    for i in range(4):
                        t = st * 4 + i
                        p = t % 2
                        PT = PT1 if p == 0 else PT2
                        pb.use("sync", lambda e, t=t, p=p: e.dma_start(
                            out=x0_ld[:, p], in_=x1_h[t]),
                            reads=(S["x1_h"],), writes=(S["x0_ld%d" % p],), dma=True)
                        pb.use("vector", lambda e, t=t, p=p, nm1=nm1, rs1=rs1: e.tensor_scalar(
                            d0_xn[:, p][:, :], x0_ld[:, p][:, :], nm1[:, t:t + 1], rs1[:, t:t + 1],
                            ALU.add, ALU.mult),
                            reads=(S["x0_ld%d" % p], S["d0_nm"], S["d0_rs"]), writes=(S["d0_xn%d" % p],))
                        def tp(e, p=p, PT=PT):
                            for c in range(8):
                                ins = e.transpose(PT[:, c * 128:(c + 1) * 128],
                                                  d0_xn[:, p][:, c * 128:(c + 1) * 128], sb_ident[:])
                            return ins
                        pb.use("tensor", tp, reads=(S["d0_xn%d" % p], S["consts"]),
                               writes=(S["pt%d" % (p + 1)],))
                        pb.use("vector", lambda e, t=t, PT=PT: e.tensor_copy(
                            xn1_ap[:, :, t * 128:(t + 1) * 128],
                            PT.ap()[:, 0:1024].rearrange("p (c s) -> p c s", s=128)),
                            reads=(S["pt%d" % (p + 1)],), writes=(S["xn1f%d" % st],))
                        def rmm(e, t=t, i=i):
                            for kc in range(8):
                                ins = e.matmul(PS[2 + i][:, 0:8], xn1_ap[:, kc, t * 128:(t + 1) * 128],
                                               gw_sb[:, kc * 8:(kc + 1) * 8],
                                               start=(kc == 0), stop=(kc == 7), skip_group_check=True)
                            return ins
                        pb.use("tensor", rmm, reads=(S["xn1f%d" % st], S["gw"]),
                               writes=(S["ps%d" % (2 + i)],))
                    def u1(e, st=st):
                        for i in range(4):
                            ins = e.reduce_max(d0_r[:, i][:, 8:9], PS[2 + i][:, 0:8], AX.X, negate=True)
                        return ins
                    pb.use("vector", u1, reads=(S["ps2"], S["ps3"], S["ps4"], S["ps5"]),
                           writes=(S["d0_r"],))
                    def u2(e, st=st):
                        for i in range(4):
                            ins = e.activation(d0_r[:, i][:, 0:8], PS[2 + i][:, 0:8], AF.Exp,
                                               bias=d0_r[:, i][:, 8:9], scale=1.0)
                        return ins
                    pb.use("scalar", u2, reads=(S["ps2"], S["ps3"], S["ps4"], S["ps5"], S["d0_r"]),
                           writes=(S["d0_r"],))
                    R = [d0_r[:, i] for i in range(4)]
                    steps = [
                        lambda e, r: e.reduce_sum(r[:, 9:10], r[:, 0:8], AX.X),
                        lambda e, r: e.reciprocal(r[:, 10:11], r[:, 9:10]),
                        lambda e, r: e.scalar_tensor_tensor(r[:, 0:8], r[:, 0:8], r[:, 10:11],
                                                            gb_sb[:], ALU.mult, ALU.add),
                        lambda e, r: e.reduce_max(r[:, 11:12], r[:, 0:8], AX.X),
                        lambda e, r: e.tensor_scalar(r[:, 12:20], r[:, 0:8], r[:, 11:12],
                                                     None, ALU.is_ge),
                        lambda e, r: e.scalar_tensor_tensor(r[:, 21:29], r[:, 12:20], -1e30,
                                                            r[:, 0:8], ALU.mult, ALU.add),
                        lambda e, r: e.reduce_max(r[:, 30:31], r[:, 21:29], AX.X),
                        lambda e, r: e.tensor_scalar(r[:, 12:20], r[:, 0:8], r[:, 30:31],
                                                     None, ALU.is_ge),
                        lambda e, r: e.tensor_tensor(r[:, 0:8], r[:, 0:8], gb_sb[:], ALU.subtract),
                    ]
                    for stp in steps:
                        def wave(e, stp=stp):
                            for r in R:
                                ins = stp(e, r)
                            return ins
                        pb.use("vector", wave, reads=(S["d0_r"], S["gb"]), writes=(S["d0_r"],))
                    def wfin(e, st=st):
                        for i, r in enumerate(R):
                            ins = e.tensor_tensor(rw_tok[:, (st * 4 + i) * 8:(st * 4 + i + 1) * 8],
                                                  r[:, 0:8], r[:, 12:20], ALU.mult)
                        return ins
                    pb.use("vector", wfin, reads=(S["d0_r"],), writes=(S["rw_tok"],))
            pb.barrier()
            if "xn1" in dbg:
                pb.use("scalar", lambda e: e.dma_start(out=dbg_out["xn1"], in_=xn1_fm[:]),
                       reads=(S["xn1f0"], S["xn1f1"], S["xn1f2"], S["xn1f3"]), dma=True)
            if "rw" in dbg:
                pb.use("scalar", lambda e: e.dma_start(out=dbg_out["rw"], in_=rw_tok[:]),
                       reads=(S["rw_tok"],), dma=True)
            if "x1" in dbg:
                pb.use("scalar", lambda e: e.dma_start(out=dbg_out["x1"], in_=x1_h), dma=True)
                pb.barrier()

        if "D" in phases:
          with ExitStack() as pctx:
            P = pctx.enter_context
            wqkv_sb = P(nc.sbuf_tensor("wqkv_sb", [128, 8 * 3072], BF16))
            sb_causal = P(nc.sbuf_tensor("sb_causal", [128, 128], F32))
            slot("causal")
            pb.use("sync", lambda e: e.dma_start(out=sb_causal[:], in_=causal_neg),
                   writes=(S["causal"],), dma=True)
            K_sb = P(nc.sbuf_tensor("K_sb", [128, 8 * SEQ], BF16))
            Vt_sb = P(nc.sbuf_tensor("Vt_sb", [128, NT * 1024], BF16))
            qbuf = P(nc.sbuf_tensor("qbuf", [128, 8 * 512], BF16))
            vtmp = P(nc.sbuf_tensor("vtmp", [128, 8 * 512], BF16))
            srow = P(nc.sbuf_tensor("srow", [128, SEQ], F32))
            pbf = P(nc.sbuf_tensor("pbf", [128, 2, SEQ], BF16))
            pts = P(nc.sbuf_tensor("pts", [128, 2, 1024], BF16))
            dsc = P(nc.sbuf_tensor("dsc", [128, 2], F32))
            dscm = P(nc.sbuf_tensor("dscm", [128, 4], F32))
            s1b = P(nc.sbuf_tensor("s1b", [128, DM], F32))
            for nm in ("wqkv", "K_sb", "Vt_sb", "qbuf", "vtmp", "srow",
                       "pbf00", "pbf01", "pbf10", "pbf11",
                       "pts0", "pts1", "dsc", "dscm", "s1b"):
                slot(nm)
            pb.use("gpsimd", lambda e: e.dma_start(out=wqkv_sb[:], in_=cWqkv), writes=(S["wqkv"],), dma=True)

            def emit_pv(t):
                tp2 = t % 2
                for gr in range((t + 8) // 8):
                    j0 = gr * 8
                    jn = min(8, t + 1 - j0)
                    pj = gr % 2
                    PT = PT1 if pj == 0 else PT2
                    def tpg(e, j0=j0, jn=jn, PT=PT, tp2=tp2):
                        for jj in range(jn):
                            ins = e.transpose(PT[:, jj * 128:(jj + 1) * 128],
                                              pbf[:, tp2][:, (j0 + jj) * 128:(j0 + jj + 1) * 128],
                                              sb_ident[:])
                        return ins
                    pb.use("tensor", tpg, reads=(S["pbf%d%d" % (tp2, gr)], S["consts"]),
                           writes=(S["pt%d" % (pj + 1)],))
                    pb.use("vector", lambda e, pj=pj, jn=jn, PT=PT: e.tensor_copy(
                        pts[:, pj][:, 0:jn * 128], PT[:, 0:jn * 128]),
                        reads=(S["pt%d" % (pj + 1)],), writes=(S["pts%d" % pj],))
                    def pvmm(e, j0=j0, jn=jn, pj=pj, t=t):
                        for jj in range(jn):
                            j = j0 + jj
                            lh = pts[:, pj][:, jj * 128:(jj + 1) * 128]
                            e.matmul(PS[2][:, 0:512], lh, Vt_sb[:, j * 1024: j * 1024 + 512],
                                     start=(j == 0), stop=(j == t), skip_group_check=True)
                            e.matmul(PS[3][:, 0:512], lh, Vt_sb[:, j * 1024 + 512: j * 1024 + 1024],
                                     start=(j == 0), stop=(j == t), skip_group_check=True)
                            ins = e.matmul(PS[4][:, 0:1], lh, sb_ones[:, 0:1],
                                           start=(j == 0), stop=(j == t), skip_group_check=True)
                        return ins
                    pb.use("tensor", pvmm, reads=(S["pts%d" % pj], S["Vt_sb"], S["consts"]),
                           writes=(S["ps2"], S["ps3"], S["ps4"]))
                pb.use("vector", lambda e: e.reciprocal(dsc[:, 1:2], PS[4][:, 0:1]),
                       reads=(S["ps4"],), writes=(S["dsc"],))
                pb.use("vector", lambda e: e.tensor_scalar(
                    s1b[:, 0:512], PS[2][:, 0:512], dsc[:, 1:2], None, ALU.mult),
                    reads=(S["ps2"], S["dsc"]), writes=(S["s1b"],))
                pb.use("vector", lambda e: e.tensor_scalar(
                    s1b[:, 512:1024], PS[3][:, 0:512], dsc[:, 1:2], None, ALU.mult),
                    reads=(S["ps3"], S["dsc"]), writes=(S["s1b"],))
                pb.use("scalar", lambda e, t=t: e.dma_start(out=s1_h[t], in_=s1b[:]),
                       reads=(S["s1b"],), writes=(S["s1_h"],), dma=True)

            for st in range(4):
                for oc in range(24):
                    pi = oc % 2
                    def mm(e, oc=oc, st=st, pi=pi):
                        for kc in range(8):
                            ins = e.matmul(PS[pi][:, 0:512],
                                           wqkv_sb[:, kc * 3072 + oc * 128: kc * 3072 + oc * 128 + 128],
                                           xn1_ap[:, kc, st * 512:(st + 1) * 512],
                                           start=(kc == 0), stop=(kc == 7), skip_group_check=True)
                        return ins
                    pb.use("tensor", mm, reads=(S["xn1f%d" % st], S["wqkv"]), writes=(PSLOT[pi],))
                    if oc < 8:
                        dstt, dsl, off = qbuf, "qbuf", oc * 512
                        pb.use("vector", lambda e, pi=pi, off=off, dstt=dstt: e.tensor_copy(
                            dstt[:, off:off + 512], PS[pi][:, 0:512]), reads=(PSLOT[pi],), writes=(S[dsl],))
                    elif oc < 16:
                        off = (oc - 8) * SEQ + st * 512
                        pb.use("vector", lambda e, pi=pi, off=off: e.tensor_copy(
                            K_sb[:, off:off + 512], PS[pi][:, 0:512]), reads=(PSLOT[pi],), writes=(S["K_sb"],))
                    else:
                        off = (oc - 16) * 512
                        pb.use("vector", lambda e, pi=pi, off=off: e.tensor_copy(
                            vtmp[:, off:off + 512], PS[pi][:, 0:512]), reads=(PSLOT[pi],), writes=(S["vtmp"],))
                for i in range(4):
                    t = st * 4 + i
                    def tpv(e, i=i):
                        for c in range(8):
                            ins = e.transpose(PT1[:, c * 128:(c + 1) * 128],
                                              vtmp[:, c * 512 + i * 128: c * 512 + i * 128 + 128],
                                              sb_ident[:])
                        return ins
                    pb.use("tensor", tpv, reads=(S["vtmp"], S["consts"]), writes=(S["pt1"],))
                    pb.use("vector", lambda e, t=t: e.tensor_copy(
                        Vt_sb[:, t * 1024:(t + 1) * 1024], PT1.ap()[:, 0:1024]),
                        reads=(S["pt1"],), writes=(S["Vt_sb"],))
                    L = (t + 1) * 128
                    ng = (L + 511) // 512
                    for g in range(ng):
                        w = min(512, L - g * 512)
                        pi = g % 2
                        def smm(e, g=g, w=w, i=i, pi=pi):
                            for kc in range(8):
                                ins = e.matmul(PS[pi][:, 0:w],
                                               qbuf[:, kc * 512 + i * 128: kc * 512 + i * 128 + 128],
                                               K_sb[:, kc * SEQ + g * 512: kc * SEQ + g * 512 + w],
                                               start=(kc == 0), stop=(kc == 7), skip_group_check=True)
                            return ins
                        pb.use("tensor", smm, reads=(S["qbuf"], S["K_sb"]), writes=(PSLOT[pi],))
                        pb.use("vector", lambda e, g=g, w=w, pi=pi: e.tensor_scalar(
                            srow[:, g * 512: g * 512 + w], PS[pi][:, 0:w], 1.0 / 32, None, ALU.mult),
                            reads=(PSLOT[pi],), writes=(S["srow"],))
                        if g == ng - 1:
                            pb.use("vector", lambda e, t=t: e.tensor_tensor(
                                srow[:, t * 128:(t + 1) * 128], srow[:, t * 128:(t + 1) * 128],
                                sb_causal[:], ALU.add),
                                reads=(S["srow"], S["causal"]), writes=(S["srow"],))
                        pb.use("vector", lambda e, g=g, w=w: e.reduce_max(
                            dscm[:, g:g + 1], srow[:, g * 512: g * 512 + w], AX.X),
                            reads=(S["srow"],), writes=(S["dscm"],))
                    pb.use("vector", lambda e, ng=ng: e.reduce_max(
                        dsc[:, 0:1], dscm[:, 0:ng], AX.X, negate=True),
                        reads=(S["dscm"],), writes=(S["dsc"],))
                    tp2 = t % 2
                    for hh in range((L + 1023) // 1024):
                        w2h = min(1024, L - hh * 1024)
                        pb.use("scalar", lambda e, hh=hh, w2h=w2h, tp2=tp2: e.activation(
                            pbf[:, tp2][:, hh * 1024: hh * 1024 + w2h],
                            srow[:, hh * 1024: hh * 1024 + w2h],
                            AF.Exp, bias=dsc[:, 0:1], scale=1.0),
                            reads=(S["srow"], S["dsc"]), writes=(S["pbf%d%d" % (tp2, hh)],))
                    if t > 0:
                        emit_pv(t - 1)
            emit_pv(NT - 1)
            pb.barrier()
            if "s1" in dbg:
                pb.use("scalar", lambda e: e.dma_start(out=dbg_out["s1"], in_=s1_h), dma=True)
                pb.barrier()

        if "E" in phases:
          with ExitStack() as pctx:
            P = pctx.enter_context
            w1_sb = P(nc.sbuf_tensor("w1_sb", [128, 2, 8 * 1408], BF16))
            w3_sb = P(nc.sbuf_tensor("w3_sb", [128, 2, 8 * 1408], BF16))
            w2_sb = P(nc.sbuf_tensor("w2_sb", [128, 11 * 1024], BF16))
            b1c_sb = P(nc.sbuf_tensor("b1c_sb", [128, 11], F32))
            b3c_sb = P(nc.sbuf_tensor("b3c_sb", [128, 11], F32))
            b2m_sb = P(nc.sbuf_tensor("b2m_sb", [9, 1024], BF16))
            rwa = P(nc.sbuf_tensor("rwa", [128, 9], BF16))
            rwt = P(nc.sbuf_tensor("rwt", [9, 128], BF16))
            h_sb = P(nc.sbuf_tensor("h_sb", [128, 11 * 512], BF16))
            prod = P(nc.sbuf_tensor("prod", [128, 2, 512], F32))
            y_half = P(nc.sbuf_tensor("y_half", [128, 8 * 1024], F32))
            s1p = P(nc.sbuf_tensor("s1p", [128, 2, DM], F32))
            x1e = P(nc.sbuf_tensor("x1e", [128, DM], F32))
            slot("yout")
            for nm in ("w10", "w11", "w30", "w31", "w2", "b1c", "b3c", "b2m", "rwa", "rwt",
                       "h_sb", "prod0", "prod1", "s1p0", "s1p1", "x1e",
                       "yh0", "yh1", "yh2", "yh3", "yh4", "yh5", "yh6", "yh7"):
                slot(nm)

            def ld_s1(gi):
                bi = gi % 2
                pb.use("scalar", lambda e, gi=gi, bi=bi: e.dma_start(
                    out=s1p[:, bi], in_=s1_h[gi]),
                    reads=(S["s1_h"],), writes=(S["s1p%d" % bi],), dma=True)

            def ld_x1(gi):
                pb.use("sync", lambda e, gi=gi: e.dma_start(out=x1e[:], in_=x1_h[gi]),
                       reads=(S["x1_h"],), writes=(S["x1e"],), dma=True)
            pb.use("gpsimd", lambda e: e.dma_start(out=b2m_sb[0:8, :], in_=b2flat.rearrange("o (e v) -> (o e) v", v=1024)),
                   writes=(S["b2m"],), dma=True)
            pb.use("gpsimd", lambda e: e.dma_start(out=b2m_sb[8:9, :], in_=sb2flat),
                   writes=(S["b2m"],), dma=True)
            pb.use("vector", lambda e: e.memset(rwa[:, 8:9], 1.0), writes=(S["rwa"],))
            for half in range(2):
                for lt in range(8):
                    tg = half * 8 + lt
                    pb.use("vector", lambda e, tg=tg: e.tensor_copy(
                        rwa[:, 0:8], rw_tok[:, tg * 8:(tg + 1) * 8]),
                        reads=(S["rw_tok"],), writes=(S["rwa"],))
                    pb.use("tensor", lambda e: e.transpose(PT1[0:9, 0:128], rwa[:, 0:9], sb_ident[:]),
                           reads=(S["rwa"], S["consts"]), writes=(S["pt1"],))
                    pb.use("vector", lambda e: e.tensor_copy(rwt[0:9, :], PT1[0:9, 0:128]),
                           reads=(S["pt1"],), writes=(S["rwt"],))
                    def b2mm(e):
                        e.matmul(PS[2][:, 0:512], rwt[0:9, 0:128], b2m_sb[0:9, 0:512],
                                 start=True, stop=True, skip_group_check=True)
                        return e.matmul(PS[3][:, 0:512], rwt[0:9, 0:128], b2m_sb[0:9, 512:1024],
                                        start=True, stop=True, skip_group_check=True)
                    pb.use("tensor", b2mm, reads=(S["rwt"], S["b2m"]), writes=(S["ps2"], S["ps3"]))
                    pb.use("vector", lambda e, lt=lt: e.tensor_copy(
                        y_half[:, lt * 1024: lt * 1024 + 512], PS[2][:, 0:512]),
                        reads=(S["ps2"],), writes=(S["yh%d" % lt],))
                    pb.use("vector", lambda e, lt=lt: e.tensor_copy(
                        y_half[:, lt * 1024 + 512: lt * 1024 + 1024], PS[3][:, 0:512]),
                        reads=(S["ps3"],), writes=(S["yh%d" % lt],))
                for ex in range(9):
                    shared = ex == 8
                    wp = (half * 9 + ex) % 2
                    pb.use("gpsimd", lambda e, ex=ex, wp=wp: e.dma_start(
                        out=w1_sb[:, wp], in_=(csw1 if ex == 8 else cw1[ex])), writes=(S["w1%d" % wp],), dma=True)
                    pb.use("gpsimd", lambda e, ex=ex, wp=wp: e.dma_start(
                        out=w3_sb[:, wp], in_=(csw3 if ex == 8 else cw3[ex])), writes=(S["w3%d" % wp],), dma=True)
                    pb.use("gpsimd", lambda e, ex=ex: e.dma_start(
                        out=w2_sb[:], in_=(csw2 if ex == 8 else cw2[ex])), writes=(S["w2"],), dma=True)
                    pb.use("sync", lambda e, ex=ex: e.dma_start(
                        out=b1c_sb[:], in_=(sb1c if ex == 8 else b1c[ex])), writes=(S["b1c"],), dma=True)
                    pb.use("sync", lambda e, ex=ex: e.dma_start(
                        out=b3c_sb[:], in_=(sb3c if ex == 8 else b3c[ex])), writes=(S["b3c"],), dma=True)
                    if shared:
                        ld_s1(half * 8 + 0)
                        ld_s1(half * 8 + 1)
                        ld_x1(half * 8 + 0)
                    for st2 in range(2):
                        tok0 = half * 1024 + st2 * 512
                        for hc in range(11):
                            pA, pB = (0, 1) if hc % 2 == 0 else (4, 5)
                            pp = hc % 2
                            def mmA(e, hc=hc, tok0=tok0, pA=pA, wp=wp):
                                for kc in range(8):
                                    ins = e.matmul(PS[pA][:, 0:512],
                                                   w1_sb[:, wp][:, kc * 1408 + hc * 128: kc * 1408 + hc * 128 + 128],
                                                   xn1_ap[:, kc, tok0:tok0 + 512],
                                                   start=(kc == 0), stop=(kc == 7), skip_group_check=True)
                                return ins
                            pb.use("tensor", mmA, reads=(S["xn1f%d" % (half * 2 + st2)], S["w1%d" % wp]), writes=(S["ps%d" % pA],))
                            def mmB(e, hc=hc, tok0=tok0, pB=pB, wp=wp):
                                for kc in range(8):
                                    ins = e.matmul(PS[pB][:, 0:512],
                                                   w3_sb[:, wp][:, kc * 1408 + hc * 128: kc * 1408 + hc * 128 + 128],
                                                   xn1_ap[:, kc, tok0:tok0 + 512],
                                                   start=(kc == 0), stop=(kc == 7), skip_group_check=True)
                                return ins
                            pb.use("tensor", mmB, reads=(S["xn1f%d" % (half * 2 + st2)], S["w3%d" % wp]), writes=(S["ps%d" % pB],))
                            pb.use("scalar", lambda e, hc=hc, pB=pB, pp=pp: e.activation(
                                prod[:, pp], PS[pB][:, 0:512], AF.Identity, bias=b3c_sb[:, hc:hc + 1], scale=1.0),
                                reads=(S["ps%d" % pB], S["b3c"]), writes=(S["prod%d" % pp],))
                            pb.use("vector", lambda e, hc=hc, pA=pA, pp=pp: e.scalar_tensor_tensor(
                                prod[:, pp], PS[pA][:, 0:512], b1c_sb[:, hc:hc + 1], prod[:, pp],
                                ALU.add, ALU.mult),
                                reads=(S["ps%d" % pA], S["b1c"], S["prod%d" % pp]), writes=(S["prod%d" % pp],))
                            pb.use("scalar", lambda e, hc=hc, pp=pp: e.activation(
                                h_sb[:, hc * 512:(hc + 1) * 512], prod[:, pp], AF.Silu),
                                reads=(S["prod%d" % pp],), writes=(S["h_sb"],))
                        for i2 in range(4):
                            lt = st2 * 4 + i2
                            tg = half * 8 + lt
                            oA, oB = (2, 3) if i2 % 2 == 0 else (0, 1)
                            def omm(e, i2=i2, oA=oA, oB=oB):
                                for kc2 in range(11):
                                    e.matmul(PS[oA][:, 0:512],
                                             h_sb[:, kc2 * 512 + i2 * 128: kc2 * 512 + i2 * 128 + 128],
                                             w2_sb[:, kc2 * 1024: kc2 * 1024 + 512],
                                             start=(kc2 == 0), stop=(kc2 == 10), skip_group_check=True)
                                    ins = e.matmul(PS[oB][:, 0:512],
                                                   h_sb[:, kc2 * 512 + i2 * 128: kc2 * 512 + i2 * 128 + 128],
                                                   w2_sb[:, kc2 * 1024 + 512: kc2 * 1024 + 1024],
                                                   start=(kc2 == 0), stop=(kc2 == 10), skip_group_check=True)
                                return ins
                            pb.use("tensor", omm, reads=(S["h_sb"], S["w2"]), writes=(S["ps%d" % oA], S["ps%d" % oB]))
                            for v in range(2):
                                pv = oA if v == 0 else oB
                                if shared:
                                    pb.use("vector", lambda e, lt=lt, v=v, pv=pv: e.tensor_tensor(
                                        y_half[:, lt * 1024 + v * 512: lt * 1024 + v * 512 + 512],
                                        y_half[:, lt * 1024 + v * 512: lt * 1024 + v * 512 + 512],
                                        PS[pv][:, 0:512], ALU.add),
                                        reads=(S["ps%d" % pv], S["yh%d" % lt]), writes=(S["yh%d" % lt],))
                                else:
                                    pb.use("vector", lambda e, lt=lt, v=v, tg=tg, ex=ex, pv=pv: e.scalar_tensor_tensor(
                                        y_half[:, lt * 1024 + v * 512: lt * 1024 + v * 512 + 512],
                                        PS[pv][:, 0:512], rw_tok[:, tg * 8 + ex: tg * 8 + ex + 1],
                                        y_half[:, lt * 1024 + v * 512: lt * 1024 + v * 512 + 512],
                                        ALU.mult, ALU.add),
                                        reads=(S["ps%d" % pv], S["rw_tok"], S["yh%d" % lt]),
                                        writes=(S["yh%d" % lt],))
                        if shared:
                            for lt in range(st2 * 4, st2 * 4 + 4):
                                tg = half * 8 + lt
                                bi = lt % 2
                                yv = y_half[:, lt * 1024:(lt + 1) * 1024]
                                pb.use("scalar", lambda e, yv=yv: e.activation(yv, yv, AF.Silu),
                                       reads=(S["yh%d" % lt],), writes=(S["yh%d" % lt],))
                                pb.use("vector", lambda e, yv=yv, bi=bi: e.tensor_tensor(
                                    yv, yv, s1p[:, bi], ALU.mult),
                                    reads=(S["yh%d" % lt], S["s1p%d" % bi]), writes=(S["yh%d" % lt],))
                                if lt + 2 < 8:
                                    ld_s1(tg + 2)
                                pb.use("vector", lambda e, yv=yv: e.tensor_tensor(
                                    yv, yv, x1e[:], ALU.add),
                                    reads=(S["yh%d" % lt], S["x1e"]), writes=(S["yh%d" % lt],))
                                if lt + 1 < 8:
                                    ld_x1(tg + 1)
                                pb.use("sync", lambda e, tg=tg, yv=yv: e.dma_start(out=yout[tg], in_=yv),
                                       reads=(S["yh%d" % lt],), writes=(S["yout"],), dma=True)
            pb.barrier()

        blk = E(nc.Block())
        blk.sync(lambda e: [f(e) for f in pb.ops["sync"]])
        blk.gpsimd(lambda e: [f(e) for f in pb.ops["gpsimd"]])
        blk.vector(lambda e: [f(e) for f in pb.ops["vector"]])
        blk.scalar(lambda e: [f(e) for f in pb.ops["scalar"]])
        blk.tensor(lambda e: [f(e) for f in pb.ops["tensor"]])
    return nc


def kernel(**inputs):
    shared = prep_inputs(inputs)
    x = np.asarray(inputs["L0_attn_x"], dtype=np.float32)
    nc = build_program()
    in_maps = []
    for b in range(8):
        m = dict(shared)
        m["x0"] = np.ascontiguousarray(x[b].reshape(NT, 128, DM))
        in_maps.append(m)
    res = run_bass_kernel_spmd(nc, in_maps, core_ids=list(range(8)))
    out = np.stack([np.asarray(res.results[b]["y"], dtype=np.float32).reshape(SEQ, DM)
                    for b in range(8)], axis=0)
    return out


# version 34
# speedup vs baseline: 1.0553x; 1.0069x over previous
import numpy as np
import ml_dtypes
from contextlib import ExitStack

import concourse.bass as bass
import concourse.mybir as mybir
from concourse.bass_utils import run_bass_kernel_spmd

F32 = mybir.dt.float32
BF16 = mybir.dt.bfloat16
AF = mybir.ActivationFunctionType
ALU = mybir.AluOpType
AX = mybir.AxisListType

SEQ = 2048
DM = 1024
NT = SEQ // 128
EPS_LN = 1e-5
EPS_RMS = 1e-6
ENGINES = ("sync", "gpsimd", "vector", "scalar", "tensor")
CSEM = {"tensor": "pe", "vector": "dv", "scalar": "ac", "gpsimd": "gp"}
QSEM = {"sync": "ld", "gpsimd": "wd", "scalar": "sd"}


class Slot:
    __slots__ = ("name", "ready", "readers", "qsem")

    def __init__(self, name):
        self.name = name
        self.ready = None
        self.readers = []
        self.qsem = None


class PB:
    SERIAL_DMA = False

    def __init__(self, nc):
        self.nc = nc
        self.ops = {e: [] for e in ENGINES}
        self.sem = {}
        self.cnt = {}
        self.hwm = {e: {} for e in ENGINES}
        self.ctx = None

    def new_sem(self, ctx, name):
        self.sem[name] = ctx.enter_context(self.nc.semaphore(name))
        self.cnt[name] = 0

    def slot_qsem(self, s):
        if s.qsem is None:
            name = "q_" + s.name
            self.sem[name] = self.ctx.enter_context(self.nc.semaphore(name))
            self.cnt[name] = 0
            s.qsem = name
        return s.qsem

    def _wait(self, eng, sem, tgt):
        if self.hwm[eng].get(sem, 0) >= tgt:
            return
        self.hwm[eng][sem] = tgt
        h = self.sem[sem]
        self.ops[eng].append(lambda e, h=h, tgt=tgt: e.wait_ge(h, tgt))

    def wait_tok(self, eng, tok, cur_dma=False, raw=False):
        if tok is None:
            return
        sem, val = tok
        if not raw and not cur_dma and sem == CSEM.get(eng):
            return
        self._wait(eng, sem, val)

    def use(self, eng, f, reads=(), writes=(), dma=False, extra_tok=()):
        for s in reads:
            self.wait_tok(eng, s.ready, cur_dma=dma, raw=True)
        for s in writes:
            self.wait_tok(eng, s.ready, cur_dma=dma)
            for t in s.readers:
                self.wait_tok(eng, t, cur_dma=dma)
        for t in extra_tok:
            self.wait_tok(eng, t, cur_dma=dma, raw=True)
        if dma:
            sem = self.slot_qsem(writes[0]) if writes else QSEM[eng]
            by = 16
        else:
            sem = CSEM[eng]
            by = 1
        if dma and PB.SERIAL_DMA:
            self._wait(eng, sem, self.cnt[sem])
        self.cnt[sem] += by
        tok = (sem, self.cnt[sem])
        h = self.sem[sem]

        def run(e, f=f, h=h, by=by):
            ins = f(e)
            ins.then_inc(h, by)
        self.ops[eng].append(run)
        for s in writes:
            s.ready = tok
            s.readers = []
        for s in reads:
            s.readers.append(tok)
        return tok

    def raw(self, eng, f):
        self.ops[eng].append(f)

    def barrier(self):
        for s in list(self.sem):
            if s.startswith("q_") or s in ("ld", "wd", "sd"):
                self._wait("sync", s, self.cnt[s])
        for eng in ENGINES:
            own = CSEM.get(eng)
            if own is not None:
                h2 = self.sem[own]
                tgt2 = self.cnt[own]
                self.ops[eng].append(lambda e, h2=h2, tgt2=tgt2: e.wait_ge(h2, tgt2))
                self.hwm[eng][own] = max(self.hwm[eng].get(own, 0), tgt2)
            self.cnt["bar"] += 1
            h = self.sem["bar"]
            self.ops[eng].append(lambda e, h=h: e.nop().then_inc(h, 1))
        tgt = self.cnt["bar"]
        for eng in ENGINES:
            self._wait(eng, "bar", tgt)


def _pretile_kmajor(w_t, M):
    K = w_t.shape[0]
    return np.ascontiguousarray(
        w_t.reshape(K // 128, 128, M).transpose(1, 0, 2).reshape(128, (K // 128) * M))


def _f32(x):
    return np.ascontiguousarray(np.asarray(x, dtype=np.float32))


def prep_inputs(I):
    t = {}
    t["cw_in"] = _pretile_kmajor(_f32(I["L0_attn_in_proj_w"]).T, 2048)
    for nm, key in (("cw_q", "L0_attn_q_proj_w"), ("cw_k", "L0_attn_k_proj_w"),
                    ("cw_v", "L0_attn_v_proj_w"), ("cw_i", "L0_attn_i_gate_w"),
                    ("cw_f", "L0_attn_f_gate_w"), ("cw_o", "L0_attn_o_gate_w"),
                    ("cw_out", "L0_attn_out_proj_w")):
        t[nm] = _pretile_kmajor(_f32(I[key]).T, 1024)
    for nm, key in (("bi_c", "L0_attn_i_gate_b"), ("bf_c", "L0_attn_f_gate_b"),
                    ("bo_c", "L0_attn_o_gate_b")):
        t[nm] = np.ascontiguousarray(_f32(I[key]).reshape(8, 128).T)
    t["qnw_c"] = np.ascontiguousarray(np.tile(_f32(I["L0_attn_q_norm_w"]), 2).reshape(128, 1))
    t["knw_c"] = np.ascontiguousarray(np.tile(_f32(I["L0_attn_k_norm_w"]), 2).reshape(128, 1))
    t["csel"] = _pretile_kmajor(_f32(I["L0_ffn_expert_sel"]).T, 16)
    keys = _f32(I["L0_ffn_keys"])
    t["ckeys"] = np.ascontiguousarray(keys.reshape(16, 8, 128, 128).transpose(2, 0, 1, 3).reshape(128, 16 * 8 * 128))
    vals = _f32(I["L0_ffn_values"])
    t["cvalues"] = np.ascontiguousarray(vals.transpose(1, 0, 2).reshape(128, 16 * 1024))
    t["cWqkv"] = _pretile_kmajor(_f32(I["L1_attn_W_qkv"]), 3072)
    t["cgw"] = _pretile_kmajor(_f32(I["L1_ffn_gate_w"]).T, 8)
    t["gate_b_bc"] = np.ascontiguousarray(np.tile(_f32(I["L1_ffn_gate_b"]), (128, 1)))
    w1 = _f32(I["L1_ffn_w1"])
    t["cw1"] = np.ascontiguousarray(w1.reshape(8, 8, 128, 1408).transpose(0, 2, 1, 3).reshape(8, 128, 8 * 1408))
    w3 = _f32(I["L1_ffn_w3"])
    t["cw3"] = np.ascontiguousarray(w3.reshape(8, 8, 128, 1408).transpose(0, 2, 1, 3).reshape(8, 128, 8 * 1408))
    w2 = _f32(I["L1_ffn_w2"])
    t["cw2"] = np.ascontiguousarray(w2.reshape(8, 11, 128, 1024).transpose(0, 2, 1, 3).reshape(8, 128, 11 * 1024))
    t["b1c"] = np.ascontiguousarray(_f32(I["L1_ffn_b1"]).reshape(8, 11, 128).transpose(0, 2, 1))
    t["b3c"] = np.ascontiguousarray(_f32(I["L1_ffn_b3"]).reshape(8, 11, 128).transpose(0, 2, 1))
    t["b2flat"] = np.ascontiguousarray(_f32(I["L1_ffn_b2"]).reshape(1, 8 * 1024))
    t["csw1"] = _pretile_kmajor(_f32(I["L1_ffn_sw1"]), 1408)
    t["csw3"] = _pretile_kmajor(_f32(I["L1_ffn_sw3"]), 1408)
    sw2 = _f32(I["L1_ffn_sw2"])
    t["csw2"] = np.ascontiguousarray(sw2.reshape(11, 128, 1024).transpose(1, 0, 2).reshape(128, 11 * 1024))
    t["sb1c"] = np.ascontiguousarray(_f32(I["L1_ffn_sb1"]).reshape(11, 128).T)
    t["sb3c"] = np.ascontiguousarray(_f32(I["L1_ffn_sb3"]).reshape(11, 128).T)
    t["sb2flat"] = np.ascontiguousarray(_f32(I["L1_ffn_sb2"]).reshape(1, 1024))
    t["ident_bf"] = np.eye(128, dtype=ml_dtypes.bfloat16)
    s_i = np.arange(128)[:, None]
    t_i = np.arange(128)[None, :]
    t["mask_at"] = ((s_i <= t_i) & (s_i // 64 == t_i // 64)).astype(ml_dtypes.bfloat16)
    t["causal_neg"] = np.where(t_i > s_i, np.float32(-1e30), np.float32(0.0))
    ind_all = np.zeros((16, 8 * 128), dtype=ml_dtypes.bfloat16)
    bones = np.zeros((128, 8 * 16), dtype=ml_dtypes.bfloat16)
    for kc in range(8):
        for p in range(128):
            h = 2 * kc + (p // 64)
            ind_all[h, kc * 128 + p] = 1
            bones[p, kc * 16 + h] = 1
    t["ind_all"] = ind_all
    t["bones"] = bones
    return t


PHASE_ALL = ("A0", "A1", "A2", "B", "C", "D0", "D", "E")
B_TILES = 32
B_STAGE = 9
B_SUB = 9
B_VAR = 0


def build_program(phases=PHASE_ALL, debug=()):
    nc = bass.Bass("TRN2", target_bir_lowering=False)
    dbg = set(debug)
    phases = set(phases)

    def IN(name, shape, dt=F32):
        return nc.dram_tensor(name, list(shape), dt, kind="ExternalInput").ap()

    x0 = IN("x0", [NT, 128, DM])
    cw_in = IN("cw_in", [128, 8 * 2048])
    cw_q = IN("cw_q", [128, 8 * 1024]); cw_k = IN("cw_k", [128, 8 * 1024])
    cw_v = IN("cw_v", [128, 8 * 1024]); cw_i = IN("cw_i", [128, 8 * 1024])
    cw_f = IN("cw_f", [128, 8 * 1024]); cw_o = IN("cw_o", [128, 8 * 1024])
    cw_out = IN("cw_out", [128, 8 * 1024])
    bi_c = IN("bi_c", [128, 8]); bf_c = IN("bf_c", [128, 8]); bo_c = IN("bo_c", [128, 8])
    qnw_c = IN("qnw_c", [128, 1]); knw_c = IN("knw_c", [128, 1])
    csel = IN("csel", [128, 8 * 16])
    ckeys = IN("ckeys", [128, 16 * 8 * 128])
    cvalues = IN("cvalues", [128, 16 * 1024])
    cWqkv = IN("cWqkv", [128, 8 * 3072])
    cgw = IN("cgw", [128, 8 * 8])
    gate_b_bc = IN("gate_b_bc", [128, 8])
    cw1 = IN("cw1", [8, 128, 8 * 1408]); cw3 = IN("cw3", [8, 128, 8 * 1408])
    cw2 = IN("cw2", [8, 128, 11 * 1024])
    b1c = IN("b1c", [8, 128, 11]); b3c = IN("b3c", [8, 128, 11])
    b2flat = IN("b2flat", [1, 8 * 1024])
    csw1 = IN("csw1", [128, 8 * 1408]); csw3 = IN("csw3", [128, 8 * 1408])
    csw2 = IN("csw2", [128, 11 * 1024])
    sb1c = IN("sb1c", [128, 11]); sb3c = IN("sb3c", [128, 11])
    sb2flat = IN("sb2flat", [1, 1024])
    ident_bf = IN("ident_bf", [128, 128], BF16)
    mask_at = IN("mask_at", [128, 128], BF16)
    causal_neg = IN("causal_neg", [128, 128])
    ind_all = IN("ind_all", [16, 8 * 128], BF16)
    bones = IN("bones", [128, 8 * 16], BF16)

    yout = nc.dram_tensor("y", [NT, 128, DM], F32, kind="ExternalOutput").ap()

    gsig_h = nc.dram_tensor("gsig_h", [8, 128, SEQ], BF16, kind="Internal").ap()
    og_h = nc.dram_tensor("og_h", [8, 128, SEQ], BF16, kind="Internal").ap()
    g0_h = nc.dram_tensor("g0_h", [NT, 128, DM], BF16, kind="Internal").ap()
    hg_h = nc.dram_tensor("hg_h", [8, 128, SEQ], BF16, kind="Internal").ap()
    x1_h = nc.dram_tensor("x1_h", [NT, 128, DM], F32, kind="Internal").ap()
    s1_h = nc.dram_tensor("s1_h", [NT, 128, DM], F32, kind="Internal").ap()

    dbg_specs = {
        "xn": ([128, 8 * SEQ], BF16), "xinner": ([128, 8 * SEQ], BF16),
        "gsig": ([8, 128, SEQ], BF16), "og": ([8, 128, SEQ], BF16),
        "g0": ([NT, 128, DM], BF16), "hg": ([8, 128, SEQ], BF16),
        "x1": ([NT, 128, DM], F32), "s1": ([NT, 128, DM], F32),
        "xn1": ([128, 8 * SEQ], BF16), "rw": ([128, NT * 8], F32),
    }
    dbg_out = {k: nc.dram_tensor("dbg_" + k, list(s[0]), s[1], kind="ExternalOutput").ap()
               for k, s in dbg_specs.items() if k in dbg}

    with ExitStack() as ctx:
        E = ctx.enter_context
        pb = PB(nc)
        pb.ctx = ctx
        for s in ("ld", "wd", "sd", "pe", "dv", "ac", "gp", "bar"):
            pb.new_sem(ctx, s)

        sb_ident = E(nc.sbuf_tensor("sb_ident", [128, 128], BF16))
        sb_ones = E(nc.sbuf_tensor("sb_ones", [128, 512], BF16))
        sb_zeros = E(nc.sbuf_tensor("sb_zeros", [128, 64], F32))
        sb_cst = E(nc.sbuf_tensor("sb_cst", [128, 3], F32))
        xn_stk = ExitStack()
        xin_stk = ExitStack()
        a1w_stk = ExitStack()
        a2w_stk = ExitStack()
        x_inner = xin_stk.enter_context(nc.sbuf_tensor("x_inner", [128, 8 * SEQ], BF16, side="right"))
        xn_fm = xn_stk.enter_context(nc.sbuf_tensor("xn_fm", [128, 8 * SEQ], BF16, side="right"))
        keys_sb = a2w_stk.enter_context(nc.sbuf_tensor("keys_sb", [128, 16 * 8 * 128], BF16, side="right"))
        vals_sb = a2w_stk.enter_context(nc.sbuf_tensor("vals_sb", [128, 16 * 1024], BF16, side="right"))
        sel_sb = a2w_stk.enter_context(nc.sbuf_tensor("sel_sb", [128, 8 * 16], BF16, side="right"))
        w_in_sb = a1w_stk.enter_context(nc.sbuf_tensor("w_in_sb", [128, 8 * 2048], BF16, side="right"))
        w_o_sb = a1w_stk.enter_context(nc.sbuf_tensor("w_o_sb", [128, 8 * 1024], BF16, side="right"))
        rw_tok = E(nc.sbuf_tensor("rw_tok", [128, NT * 8], F32))

        PS = [E(nc.psum_tensor("psb%d" % i, [128, 512], F32)) for i in range(6)]
        PT1 = E(nc.psum_tensor("pt1", [128, 1024], BF16))
        PT2 = E(nc.psum_tensor("pt2", [128, 1024], BF16))

        S = {}
        def slot(name):
            S[name] = Slot(name)
            return S[name]
        for nm in ("consts", "xnf0", "xnf1", "xnf2", "xnf3", "x_inner", "rw_tok",
                   "ps0", "ps1", "ps2", "ps3", "ps4", "ps5", "pt1", "pt2",
                   "gsig_h", "og_h", "g0_h", "hg_h", "x1_h", "s1_h"):
            slot(nm)
        PSLOT = [S["ps%d" % i] for i in range(6)]

        xn_ap = xn_fm.ap().rearrange("p (c s) -> p c s", s=SEQ)
        xin_ap = x_inner.ap().rearrange("p (c s) -> p c s", s=SEQ)

        for dst, src in ((sb_ident, ident_bf),):
            pb.use("sync", lambda e, dst=dst, src=src: e.dma_start(out=dst[:], in_=src),
                   writes=(S["consts"],), dma=True)
        pb.use("vector", lambda e: e.memset(sb_ones[:], 1.0), writes=(S["consts"],))
        pb.use("vector", lambda e: e.memset(sb_zeros[:], 0.0), writes=(S["consts"],))
        pb.use("vector", lambda e: e.memset(sb_cst[:, 0:1], EPS_LN), writes=(S["consts"],))
        pb.use("vector", lambda e: e.memset(sb_cst[:, 1:2], EPS_RMS), writes=(S["consts"],))
        pb.use("vector", lambda e: e.memset(sb_cst[:, 2:3], 1.0), writes=(S["consts"],))
        sl_win = slot("w_in"); sl_wo = slot("w_o")
        sl_keys = slot("keys"); sl_vals = slot("vals"); sl_selw = slot("selw")
        if "A1" in phases:
            pb.use("gpsimd", lambda e: e.dma_start(out=w_in_sb[:], in_=cw_in), writes=(sl_win,), dma=True)
            pb.use("gpsimd", lambda e: e.dma_start(out=w_o_sb[:], in_=cw_o), writes=(sl_wo,), dma=True)
        if "A2" in phases:
            pb.use("gpsimd", lambda e: e.dma_start(out=keys_sb[:], in_=ckeys), writes=(sl_keys,), dma=True)
            pb.use("gpsimd", lambda e: e.dma_start(out=vals_sb[:], in_=cvalues), writes=(sl_vals,), dma=True)
            pb.use("gpsimd", lambda e: e.dma_start(out=sel_sb[:], in_=csel), writes=(sl_selw,), dma=True)

        a01_stk = ExitStack()
        if "A0" in phases:
            A0P = a01_stk.enter_context
            xt4 = A0P(nc.sbuf_tensor("xt4", [128, 4, DM], F32))
            sqd = A0P(nc.sbuf_tensor("sqd", [128, DM], F32))
            a0_xn = A0P(nc.sbuf_tensor("a0_xn", [128, 2, DM], BF16))
            a0_st = A0P(nc.sbuf_tensor("a0_st", [128, 6, 16], F32))
            sl_x4 = [slot("xt4_%d" % j) for j in range(4)]
            sl_sqd = slot("a0_sqd")
            sl_xn = [slot("a0_xn0"), slot("a0_xn1")]
            sl_sA = slot("a0_sA"); sl_sB = slot("a0_sB")
            sl_nm = slot("a0_nm"); sl_rs = slot("a0_rs")
            sl_t4 = slot("a0_t4"); sl_t5 = slot("a0_t5")
            sA = a0_st.ap()[:, 0]; sB = a0_st.ap()[:, 1]
            nm = a0_st.ap()[:, 2]; rs = a0_st.ap()[:, 3]
            q4 = a0_st.ap()[:, 4]; q5 = a0_st.ap()[:, 5]

            def emit_a0_group(g):
                g4 = slice(g * 4, g * 4 + 4)
                for j in range(4):
                    t = g * 4 + j
                    pb.use("sync" if j % 2 == 0 else "scalar",
                           lambda e, t=t, j=j: e.dma_start(out=xt4[:, j], in_=x0[t]),
                           writes=(sl_x4[j],), dma=True)
                    pb.use("vector", lambda e, j=j, t=t, sA=sA: e.reduce_sum(
                        sA[:, t:t + 1], xt4[:, j][:, :], AX.X, negate=True),
                        reads=(sl_x4[j],), writes=(sl_sA,))
                    pb.use("scalar", lambda e, j=j, t=t, sB=sB: e.activation(
                        sqd[:], xt4[:, j][:, :], AF.Square, accum_out=sB[:, t:t + 1]),
                        reads=(sl_x4[j], sl_sqd), writes=(sl_sqd, sl_sB))
                pb.use("vector", lambda e, g4=g4, nm=nm, sA=sA: e.tensor_scalar_mul(
                    nm[:, g4], sA[:, g4], 1.0 / DM), reads=(sl_sA,), writes=(sl_nm,))
                pb.use("vector", lambda e, g4=g4, nm=nm, q4=q4: e.tensor_tensor(
                    q4[:, g4], nm[:, g4], nm[:, g4], ALU.mult), reads=(sl_nm,), writes=(sl_t4,))
                pb.use("vector", lambda e, g4=g4, sB=sB, q4=q4, q5=q5: e.scalar_tensor_tensor(
                    q5[:, g4], sB[:, g4], 1.0 / DM, q4[:, g4], ALU.mult, ALU.subtract),
                    reads=(sl_sB, sl_t4), writes=(sl_t5,))
                pb.use("scalar", lambda e, g4=g4, q4=q4, q5=q5: e.activation(
                    q4[:, g4], q5[:, g4], AF.Ln, bias=sb_cst[:, 0:1], scale=1.0),
                    reads=(sl_t5, S["consts"]), writes=(sl_t4,))
                pb.use("scalar", lambda e, g4=g4, rs=rs, q4=q4: e.activation(
                    rs[:, g4], q4[:, g4], AF.Exp, bias=0.0, scale=-0.5),
                    reads=(sl_t4,), writes=(sl_rs,))
                for j in range(4):
                    t = g * 4 + j
                    p = j % 2
                    PT = PT1 if p == 0 else PT2
                    pb.use("vector", lambda e, j=j, p=p, t=t, nm=nm, rs=rs: e.tensor_scalar(
                        a0_xn[:, p][:, :], xt4[:, j][:, :], nm[:, t:t + 1], rs[:, t:t + 1],
                        ALU.add, ALU.mult),
                        reads=(sl_x4[j], sl_nm, sl_rs), writes=(sl_xn[p],))
                    def tp(e, p=p, PT=PT):
                        for c in range(8):
                            ins = e.transpose(PT[:, c * 128:(c + 1) * 128],
                                              a0_xn[:, p][:, c * 128:(c + 1) * 128], sb_ident[:])
                        return ins
                    pb.use("tensor", tp, reads=(sl_xn[p], S["consts"]), writes=(S["pt%d" % (p + 1)],))
                    pb.use("vector", lambda e, t=t, PT=PT: e.tensor_copy(
                        xn_ap[:, :, t * 128:(t + 1) * 128],
                        PT.ap()[:, 0:1024].rearrange("p (c s) -> p c s", s=128)),
                        reads=(S["pt%d" % (p + 1)],), writes=(S["xnf%d" % (t // 4)],))
            if "A1" not in phases:
                for g in range(4):
                    emit_a0_group(g)
            if "xn" in dbg and "A1" not in phases:
                pb.barrier()
                pb.use("scalar", lambda e: e.dma_start(out=dbg_out["xn"], in_=xn_fm[:]),
                       reads=(S["xnf0"], S["xnf1"], S["xnf2"], S["xnf3"]), dma=True)
                pb.barrier()

        if "A1" in phases:
            P = a01_stk.enter_context
            a1_bo = P(nc.sbuf_tensor("a1_bo", [128, 9], F32))
            a1_ev = P(nc.sbuf_tensor("a1_ev", [128, 2, 512], BF16))
            a1_sg = P(nc.sbuf_tensor("a1_sg", [128, 2, 512], BF16))
            sl_bo = slot("a1_bo")
            sl_ev = [slot("a1_ev0"), slot("a1_ev1")]
            sl_sg = [slot("a1_sg0"), slot("a1_sg1")]
            pb.use("sync", lambda e: e.dma_start(out=a1_bo[:, 0:8], in_=bo_c), writes=(sl_bo,), dma=True)
            pb.use("vector", lambda e: e.tensor_scalar_mul(a1_bo[:, 0:8], a1_bo[:, 0:8], 0.5),
                   reads=(sl_bo,), writes=(sl_bo,))

            for st in range(4):
                emit_a0_group(st)
                for oc in range(24):
                    pi = oc % 2
                    bank, sbank = PS[pi], PSLOT[pi]
                    def mms(e, st=st, oc=oc, bank=bank):
                        for kc in range(8):
                            if oc < 16:
                                lhs = w_in_sb[:, kc * 2048 + oc * 128: kc * 2048 + oc * 128 + 128]
                                rhs = xn_ap[:, kc, st * 512:(st + 1) * 512]
                            else:
                                o2 = oc - 16
                                lhs = w_o_sb[:, kc * 1024 + o2 * 128: kc * 1024 + o2 * 128 + 128]
                                rhs = xin_ap[:, kc, st * 512:(st + 1) * 512]
                            ins = e.matmul(bank[:, 0:512], lhs, rhs,
                                           start=(kc == 0), stop=(kc == 7), skip_group_check=True)
                        return ins
                    pb.use("tensor", mms,
                           reads=(S["xnf%d" % st], sl_win) if oc < 16 else (S["x_inner"], sl_wo),
                           writes=(sbank,))
                    if oc < 8:
                        pb.use("scalar", lambda e, oc=oc, st=st, bank=bank: e.activation(
                            xin_ap[:, oc, st * 512:(st + 1) * 512], bank[:, 0:512],
                            AF.Identity, bias=0.0),
                            reads=(sbank,), writes=(S["x_inner"],))
                    else:
                        if oc < 16:
                            c2, dsth = oc - 8, gsig_h
                            bias_ap = None
                        else:
                            c2, dsth = oc - 16, og_h
                            bias_ap = a1_bo
                        ep = oc % 2
                        def tanh_op(e, bank=bank, bias_ap=bias_ap, c2=c2, ep=ep):
                            b = bias_ap[:, c2:c2 + 1] if bias_ap is not None else 0.0
                            return e.activation(a1_ev[:, ep][:, :], bank[:, 0:512], AF.Tanh,
                                                bias=b, scale=0.5)
                        pb.use("scalar", tanh_op, reads=(sbank,) + ((sl_bo,) if bias_ap is not None else ()),
                               writes=(sl_ev[ep],))
                        pb.use("vector", lambda e, ep=ep: e.tensor_scalar(
                            a1_sg[:, ep][:, :], a1_ev[:, ep][:, :], 0.5, 0.5, ALU.mult, ALU.add),
                            reads=(sl_ev[ep],), writes=(sl_sg[ep],))
                        pb.use("scalar", lambda e, dsth=dsth, c2=c2, st=st, ep=ep: e.dma_start(
                            out=dsth[c2, :, st * 512:(st + 1) * 512], in_=a1_sg[:, ep][:, :]),
                            reads=(sl_sg[ep],), writes=(S["gsig_h" if dsth is gsig_h else "og_h"],),
                            dma=True)
            pb.barrier()
            if "xinner" in dbg:
                pb.use("scalar", lambda e: e.dma_start(out=dbg_out["xinner"], in_=x_inner[:]),
                       reads=(S["x_inner"],), dma=True)
            if "gsig" in dbg:
                pb.use("scalar", lambda e: e.dma_start(out=dbg_out["gsig"], in_=gsig_h), dma=True)
            if "og" in dbg:
                pb.use("scalar", lambda e: e.dma_start(out=dbg_out["og"], in_=og_h), dma=True)
            if "xn" in dbg:
                pb.use("scalar", lambda e: e.dma_start(out=dbg_out["xn"], in_=xn_fm[:]),
                       reads=(S["xnf0"], S["xnf1"], S["xnf2"], S["xnf3"]), dma=True)
            pb.barrier()
            a1w_stk.close()
        a01_stk.close()

        if "A2" in phases:
          with ExitStack() as pctx:
            P = pctx.enter_context
            selt = P(nc.sbuf_tensor("selt", [16, 512], F32))
            selbf = P(nc.sbuf_tensor("selbf", [16, 512], BF16))
            selflat = P(nc.sbuf_tensor("selflat", [1, 16 * 512], BF16))
            rbuf = P(nc.sbuf_tensor("rbuf", [128, 16 * 512], BF16))
            rp = P(nc.sbuf_tensor("rp", [128, 2, 128], BF16))
            a2_t = P(nc.sbuf_tensor("a2_t", [128, 2, 512], F32))
            a2_g = P(nc.sbuf_tensor("a2_g", [128, DM], BF16))
            sl_selt = slot("selt"); sl_selbf = slot("selbf"); sl_selflat = slot("selflat")
            sl_rbuf = slot("rbuf")
            sl_rp = [slot("rp0"), slot("rp1")]
            sl_a2t = [slot("a2_t0"), slot("a2_t1")]
            sl_a2g = slot("a2_g")

            for st in range(4):
                def selmm(e, st=st):
                    for kc in range(8):
                        ins = e.matmul(PS[4][0:16, 0:512], sel_sb[:, kc * 16:(kc + 1) * 16],
                                       xn_ap[:, kc, st * 512:(st + 1) * 512],
                                       start=(kc == 0), stop=(kc == 7), skip_group_check=True)
                    return ins
                pb.use("tensor", selmm, reads=(S["xnf%d" % st], sl_selw), writes=(S["ps4"],))
                pb.use("scalar", lambda e: e.activation(selt[0:16, :], PS[4][0:16, 0:512],
                                                        AF.Tanh, bias=0.0, scale=0.5),
                       reads=(S["ps4"],), writes=(sl_selt,))
                pb.use("vector", lambda e: e.tensor_scalar(selbf[0:16, :], selt[0:16, :],
                                                           0.5, 0.5, ALU.mult, ALU.add),
                       reads=(sl_selt,), writes=(sl_selbf,))
                pb.use("gpsimd", lambda e: e.dma_start(out=selflat[0:1, 0:8192], in_=selbf[0:16, 0:512]),
                       reads=(sl_selbf,), writes=(sl_selflat,), dma=True)
                for ex in range(16):
                    pi = ex % 2
                    def scmm(e, st=st, ex=ex, pi=pi):
                        for kc in range(8):
                            ins = e.matmul(PS[pi][:, 0:512],
                                           keys_sb[:, ex * 1024 + kc * 128: ex * 1024 + kc * 128 + 128],
                                           xn_ap[:, kc, st * 512:(st + 1) * 512],
                                           start=(kc == 0), stop=(kc == 7), skip_group_check=True)
                        return ins
                    pb.use("tensor", scmm, reads=(S["xnf%d" % st], sl_keys), writes=(PSLOT[pi],))
                    pb.use("tensor", lambda e, ex=ex, pi=pi: e.matmul(
                        PS[2 + pi][:, 0:512], sb_ones[0:1, 0:128],
                        selflat[0:1, ex * 512: ex * 512 + 512],
                        start=True, stop=True, skip_group_check=True),
                        reads=(sl_selflat, S["consts"]), writes=(S["ps%d" % (2 + pi)],))
                    pb.use("scalar", lambda e, ex=ex, pi=pi: e.activation(
                        rbuf[:, ex * 512:(ex + 1) * 512], PS[pi][:, 0:512], AF.Relu),
                        reads=(PSLOT[pi],), writes=(sl_rbuf,))
                    pb.use("vector", lambda e, ex=ex, pi=pi: e.tensor_tensor(
                        rbuf[:, ex * 512:(ex + 1) * 512], rbuf[:, ex * 512:(ex + 1) * 512],
                        PS[2 + pi][:, 0:512], ALU.mult),
                        reads=(sl_rbuf, S["ps%d" % (2 + pi)]), writes=(sl_rbuf,))
                for i in range(4):
                    tg = st * 4 + i
                    pA, pB2 = (4, 5) if i % 2 == 0 else (2, 3)
                    def vmm_all(e, i=i, pA=pA, pB2=pB2):
                        for ex in range(16):
                            lh = rbuf[:, ex * 512 + i * 128: ex * 512 + i * 128 + 128]
                            e.matmul(PS[pA][:, 0:512], lh, vals_sb[:, ex * 1024: ex * 1024 + 512],
                                     start=(ex == 0), stop=(ex == 15), skip_group_check=True)
                            ins = e.matmul(PS[pB2][:, 0:512], lh,
                                           vals_sb[:, ex * 1024 + 512: ex * 1024 + 1024],
                                           start=(ex == 0), stop=(ex == 15), skip_group_check=True)
                        return ins
                    pb.use("tensor", vmm_all, reads=(sl_rbuf, sl_vals),
                           writes=(S["ps%d" % pA], S["ps%d" % pB2]))
                    for v, pv in enumerate((pA, pB2)):
                        pb.use("scalar", lambda e, v=v, pv=pv: e.activation(
                            a2_t[:, v], PS[pv][:, 0:512], AF.Tanh, bias=0.0, scale=0.5),
                            reads=(S["ps%d" % pv],), writes=(sl_a2t[v],))
                        pb.use("vector", lambda e, v=v: e.tensor_scalar(
                            a2_t[:, v], a2_t[:, v], 0.5, 0.5, ALU.mult, ALU.add),
                            reads=(sl_a2t[v],), writes=(sl_a2t[v],))
                        pb.use("vector", lambda e, v=v, pv=pv: e.tensor_tensor(
                            a2_g[:, v * 512:(v + 1) * 512], a2_t[:, v], PS[pv][:, 0:512], ALU.mult),
                            reads=(sl_a2t[v], S["ps%d" % pv]), writes=(sl_a2g,))
                    pb.use("scalar", lambda e, tg=tg: e.dma_start(out=g0_h[tg], in_=a2_g[:]),
                           reads=(sl_a2g,), writes=(S["g0_h"],), dma=True)
            pb.barrier()
            if "g0" in dbg:
                pb.use("scalar", lambda e: e.dma_start(out=dbg_out["g0"], in_=g0_h), dma=True)
                pb.barrier()
          a2w_stk.close()

        a1w_stk.close()
        a2w_stk.close()
        xn_stk.close()

        if "B" in phases:
          with ExitStack() as pctx:
            P = pctx.enter_context
            sb_maskat = P(nc.sbuf_tensor("sb_maskat", [128, 128], BF16))
            sb_maskat4 = P(nc.sbuf_tensor("sb_maskat4", [128, 512], BF16))
            sb_ind = P(nc.sbuf_tensor("sb_ind", [16, 8 * 128], BF16))
            sb_bones = P(nc.sbuf_tensor("sb_bones", [128, 8 * 16], BF16))
            slot("bconst")
            for dst, src in ((sb_maskat, mask_at), (sb_ind, ind_all), (sb_bones, bones)):
                pb.use("sync", lambda e, dst=dst, src=src: e.dma_start(out=dst[:], in_=src),
                       writes=(S["bconst"],), dma=True)
            wq_sb = P(nc.sbuf_tensor("wq_sb", [128, 8 * 1024], BF16))
            wk_sb = P(nc.sbuf_tensor("wk_sb", [128, 8 * 1024], BF16))
            wv_sb = P(nc.sbuf_tensor("wv_sb", [128, 8 * 1024], BF16))
            wi_sb = P(nc.sbuf_tensor("wi_sb", [128, 8 * 1024], BF16))
            wf_sb = P(nc.sbuf_tensor("wf_sb", [128, 8 * 1024], BF16))
            bcol = P(nc.sbuf_tensor("bcol", [128, 16], F32))
            nwc = P(nc.sbuf_tensor("nwc", [128, 2], F32))
            qhat = P(nc.sbuf_tensor("qhat", [128, 8 * 256], BF16))
            khat = P(nc.sbuf_tensor("khat", [128, 8 * 256], BF16))
            vbf = P(nc.sbuf_tensor("vbf", [128, 8 * 256], BF16))
            ibf = P(nc.sbuf_tensor("ibf", [128, 8 * 256], BF16))
            qraw = P(nc.sbuf_tensor("qraw", [128, 8 * 256], BF16))
            rstdb = P(nc.sbuf_tensor("rstdb", [16, 256], BF16))
            rstdf = P(nc.sbuf_tensor("rstdf", [16, 256], F32))
            fpre = P(nc.sbuf_tensor("fpre", [128, 8 * 256], F32))
            fb1 = P(nc.sbuf_tensor("fb1", [128, 8 * 256], F32))
            lfb = P(nc.sbuf_tensor("lfb", [128, 8 * 256], F32))
            cumf = P(nc.sbuf_tensor("cumf", [128, 8 * 128], F32))
            eqb = P(nc.sbuf_tensor("eqb", [128, 8 * 128], BF16))
            enb = P(nc.sbuf_tensor("enb", [128, 8 * 128], BF16))
            qtl = P(nc.sbuf_tensor("qtl", [128, 8 * 128], BF16))
            qtlz = P(nc.sbuf_tensor("qtlz", [128, 2, 8 * 128], BF16))
            ktl = P(nc.sbuf_tensor("ktl", [128, 8 * 128], BF16))
            ktok = P(nc.sbuf_tensor("ktok", [128, 8 * 128], BF16))
            vaug = P(nc.sbuf_tensor("vaug", [128, 16 * 65], BF16))
            asb = P(nc.sbuf_tensor("asb", [128, 16 * 128], BF16))
            call = P(nc.sbuf_tensor("call", [128, 8 * 65], F32))
            cbfz = P(nc.sbuf_tensor("cbfz", [128, 2, 8 * 65], BF16))
            fend = P(nc.sbuf_tensor("fend", [128, 2, 8], F32))
            dens = P(nc.sbuf_tensor("dens", [128, 16], F32))
            rden = P(nc.sbuf_tensor("rden", [128, 16], F32))
            htok = P(nc.sbuf_tensor("htok", [128, 1024], BF16))
            ogs = P(nc.sbuf_tensor("ogs", [128, 8 * 128], BF16))
            for nm in ("wq", "wk", "wv", "wi", "wf", "bcol", "nwc", "qhat", "khat",
                       "vbf", "ibf", "qraw", "sqb", "rstdb", "rstdf", "fpre", "fb1", "lfb",
                       "cumf", "eqb", "enb", "qtl", "qtlz0", "qtlz1", "ktl", "ktok", "vaug", "asb",
                       "call", "cbfz", "fend", "dens", "rden", "htok", "ogs", "hgf"):
                slot(nm)
            for w, src, sn in ((wf_sb, cw_f, "wf"), (wq_sb, cw_q, "wq"), (wk_sb, cw_k, "wk"),
                               (wv_sb, cw_v, "wv"), (wi_sb, cw_i, "wi")):
                pb.use("gpsimd", lambda e, w=w, src=src: e.dma_start(out=w[:], in_=src),
                       writes=(S[sn],), dma=True)
            pb.use("sync", lambda e: e.dma_start(out=bcol[:, 0:8], in_=bi_c), writes=(S["bcol"],), dma=True)
            pb.use("sync", lambda e: e.dma_start(out=bcol[:, 8:16], in_=bf_c), writes=(S["bcol"],), dma=True)
            pb.use("sync", lambda e: e.dma_start(out=nwc[:, 0:1], in_=qnw_c), writes=(S["nwc"],), dma=True)
            pb.use("sync", lambda e: e.dma_start(out=nwc[:, 1:2], in_=knw_c), writes=(S["nwc"],), dma=True)
            pb.use("vector", lambda e: e.memset(vaug[:], 1.0), writes=(S["vaug"],))
            def mk_mask4(e):
                for c in range(4):
                    ins = e.tensor_copy(sb_maskat4[:, c * 128:(c + 1) * 128], sb_maskat[:])
                return ins
            pb.use("vector", mk_mask4, reads=(S["bconst"],), writes=(S["bconst"],))
            pb.use("vector", lambda e: e.memset(qtlz[:], 0.0), writes=(S["qtlz0"], S["qtlz1"]))
            pb.use("vector", lambda e: e.memset(call[:], 0.0), writes=(S["call"],))
            pb.use("vector", lambda e: e.memset(cbfz[:], 0.0), writes=(S["cbfz"],))

            qhat3 = qhat.ap().rearrange("p (c s) -> p c s", s=256)
            khat3 = khat.ap().rearrange("p (c s) -> p c s", s=256)
            vbf3 = vbf.ap().rearrange("p (c s) -> p c s", s=256)
            ibf3 = ibf.ap().rearrange("p (c s) -> p c s", s=256)
            c65 = call.ap().rearrange("p (b c) -> p b c", c=65)

            vaug65 = vaug.ap().rearrange("p (b c) -> p b c", c=65)
            eq3 = eqb.ap().rearrange("p (c s) -> p c s", s=128)
            HOUT = [PS[0], PS[1], PS[4], PS[5]]
            HSL = [S["ps0"], S["ps1"], S["ps4"], S["ps5"]]
            AT_ORDER = [0, 2, 4, 6, 8, 10, 12, 14, 1, 3, 5, 7, 9, 11, 13, 15]
            AT_POS = [AT_ORDER.index(h) for h in range(16)]

            def proj_norm(wsb, wsl, st, dst, dstsl, nwslice):
                for oc in range(8):
                    pi = oc % 2
                    def mm(e, oc=oc, st=st, wsb=wsb, pi=pi):
                        for kc in range(8):
                            ins = e.matmul(PS[pi][:, 0:256],
                                           wsb[:, kc * 1024 + oc * 128: kc * 1024 + oc * 128 + 128],
                                           xin_ap[:, kc, st * 256:(st + 1) * 256],
                                           start=(kc == 0), stop=(kc == 7), skip_group_check=True)
                        return ins
                    pb.use("tensor", mm, reads=(S["x_inner"], wsl), writes=(PSLOT[pi],))
                    pb.use("vector", lambda e, oc=oc, pi=pi: e.tensor_copy(
                        qraw[:, oc * 256:(oc + 1) * 256], PS[pi][:, 0:256]),
                        reads=(PSLOT[pi],), writes=(S["qraw"],))
                pb.use("vector", lambda e: e.tensor_tensor(asb[:], qraw[:], qraw[:], ALU.mult),
                       reads=(S["qraw"],), writes=(S["asb"],))
                def statmm(e):
                    for kc in range(8):
                        ins = e.matmul(PS[2][0:16, 0:256], sb_bones[:, kc * 16:(kc + 1) * 16],
                                       asb[:, kc * 256:(kc + 1) * 256],
                                       start=(kc == 0), stop=(kc == 7), skip_group_check=True)
                    return ins
                pb.use("tensor", statmm, reads=(S["asb"], S["bconst"]), writes=(S["ps2"],))
                pb.use("scalar", lambda e: e.activation(rstdf[0:16, 0:256], PS[2][0:16, 0:256],
                                                        AF.Ln, bias=sb_cst[0:16, 1:2], scale=1.0 / 64),
                       reads=(S["ps2"], S["consts"]), writes=(S["rstdf"],))
                pb.use("scalar", lambda e: e.activation(rstdb[0:16, 0:256], rstdf[0:16, 0:256],
                                                        AF.Exp, bias=0.0, scale=-0.5),
                       reads=(S["rstdf"],), writes=(S["rstdb"],))
                for oc in range(8):
                    pi = oc % 2
                    pb.use("tensor", lambda e, oc=oc, pi=pi: e.matmul(
                        PS[2 + pi][:, 0:256], sb_ind[:, oc * 128:(oc + 1) * 128],
                        rstdb[0:16, 0:256], start=True, stop=True, skip_group_check=True),
                        reads=(S["rstdb"], S["bconst"]), writes=(S["ps%d" % (2 + pi)],))
                    pb.use("vector", lambda e, oc=oc, pi=pi, dst=dst, nwslice=nwslice: e.scalar_tensor_tensor(
                        dst[:, oc * 256:(oc + 1) * 256], qraw[:, oc * 256:(oc + 1) * 256],
                        nwslice, PS[2 + pi][:, 0:256], ALU.mult, ALU.mult),
                        reads=(S["qraw"], S["nwc"], S["ps%d" % (2 + pi)]), writes=(dstsl,))

            for st in range(8):
                for oc in range(8):
                    pi = 2 + oc % 2
                    def mm(e, oc=oc, st=st, pi=pi):
                        for kc in range(8):
                            ins = e.matmul(PS[pi][:, 0:256],
                                           wf_sb[:, kc * 1024 + oc * 128: kc * 1024 + oc * 128 + 128],
                                           xin_ap[:, kc, st * 256:(st + 1) * 256],
                                           start=(kc == 0), stop=(kc == 7), skip_group_check=True)
                        return ins
                    pb.use("tensor", mm, reads=(S["x_inner"], S["wf"]), writes=(S["ps%d" % pi],))
                    pb.use("scalar", lambda e, oc=oc, pi=pi: e.activation(
                        fpre[:, oc * 256:(oc + 1) * 256], PS[pi][:, 0:256],
                        AF.Identity, bias=bcol[:, 8 + oc:9 + oc], scale=1.0),
                        reads=(S["ps%d" % pi], S["bcol"]), writes=(S["fpre"],))
                pb.use("scalar", lambda e: e.activation(fb1[:], fpre[:], AF.Abs),
                       reads=(S["fpre"],), writes=(S["fb1"],))
                pb.use("scalar", lambda e: e.activation(lfb[:], fb1[:], AF.Exp, bias=0.0, scale=-1.0),
                       reads=(S["fb1"],), writes=(S["lfb"],))
                pb.use("scalar", lambda e: e.activation(fb1[:], lfb[:], AF.Ln, bias=sb_cst[:, 2:3], scale=1.0),
                       reads=(S["lfb"], S["consts"]), writes=(S["fb1"],))
                pb.use("vector", lambda e: e.tensor_scalar(lfb[:], fpre[:], 0.0, None, ALU.min),
                       reads=(S["fpre"],), writes=(S["lfb"],))
                pb.use("vector", lambda e: e.tensor_tensor(lfb[:], lfb[:], fb1[:], ALU.subtract),
                       reads=(S["lfb"], S["fb1"]), writes=(S["lfb"],))

                def emit_prep(i, st=st):
                    if st * 2 + i >= B_TILES:
                        return
                    def scans(e, i=i):
                        for kc in range(8):
                            for cc in range(2):
                                ins = e.tensor_tensor_scan(
                                    cumf[:, kc * 128 + cc * 64: kc * 128 + cc * 64 + 64],
                                    lfb[:, kc * 256 + i * 128 + cc * 64: kc * 256 + i * 128 + cc * 64 + 64],
                                    sb_zeros[:, 0:64], 0.0, ALU.add, ALU.add)
                        return ins
                    pb.use("vector", scans, reads=(S["lfb"], S["consts"]), writes=(S["cumf"],))
                    if B_STAGE < 1:
                        return
                    def eewave(e):
                        e.activation(eqb[:], cumf[:], AF.Exp, bias=0.0, scale=1.0)
                        return e.activation(enb[:], cumf[:], AF.Exp, bias=0.0, scale=-1.0)
                    pb.use("scalar", eewave, reads=(S["cumf"],), writes=(S["eqb"], S["enb"]))

                emit_prep(0)
                proj_norm(wq_sb, S["wq"], st, qhat, S["qhat"], nwc[:, 0:1])
                proj_norm(wk_sb, S["wk"], st, khat, S["khat"], nwc[:, 1:2])
                for which, wsb, wsl in (("v", wv_sb, S["wv"]), ("i", wi_sb, S["wi"])):
                    for oc in range(8):
                        pi = oc % 2
                        def mm(e, oc=oc, st=st, wsb=wsb, pi=pi):
                            for kc in range(8):
                                ins = e.matmul(PS[pi][:, 0:256],
                                               wsb[:, kc * 1024 + oc * 128: kc * 1024 + oc * 128 + 128],
                                               xin_ap[:, kc, st * 256:(st + 1) * 256],
                                               start=(kc == 0), stop=(kc == 7), skip_group_check=True)
                            return ins
                        pb.use("tensor", mm, reads=(S["x_inner"], wsl), writes=(PSLOT[pi],))
                        if which == "v":
                            pb.use("vector", lambda e, oc=oc, pi=pi: e.tensor_copy(
                                vbf[:, oc * 256:(oc + 1) * 256], PS[pi][:, 0:256]),
                                reads=(PSLOT[pi],), writes=(S["vbf"],))
                        else:
                            pb.use("scalar", lambda e, oc=oc, pi=pi: e.activation(
                                ibf[:, oc * 256:(oc + 1) * 256], PS[pi][:, 0:256],
                                AF.Exp, bias=bcol[:, oc:oc + 1], scale=1.0),
                                reads=(PSLOT[pi], S["bcol"]), writes=(S["ibf"],))

                for i in range(2):
                    T0 = st * 256 + i * 128
                    if st * 2 + i >= B_TILES:
                        continue
                    if B_STAGE < 1:
                        continue
                    def qkwave_a(e, i=i):
                        e.tensor_tensor(qtl.ap().rearrange("p (c s) -> p c s", s=128),
                                        qhat3[:, :, i * 128:(i + 1) * 128], eq3, ALU.mult)
                        return e.tensor_tensor(ktl.ap().rearrange("p (c s) -> p c s", s=128),
                                               khat3[:, :, i * 128:(i + 1) * 128],
                                               ibf3[:, :, i * 128:(i + 1) * 128], ALU.mult)
                    pb.use("vector", qkwave_a,
                           reads=(S["qhat"], S["eqb"], S["khat"], S["ibf"]),
                           writes=(S["qtl"], S["ktl"]))
                    def qkwave_b(e):
                        for cc2 in range(2):
                            e.tensor_copy(
                                qtlz.ap()[:, cc2].rearrange("p (c s) -> p c s", s=128)[:, :, cc2 * 64:(cc2 + 1) * 64],
                                qtl.ap().rearrange("p (c s) -> p c s", s=128)[:, :, cc2 * 64:(cc2 + 1) * 64])
                        for cc3 in range(2):
                            e.tensor_copy(
                                fend.ap()[:, cc3].rearrange("p (b c) -> p b c", c=1),
                                eq3[:, :, cc3 * 64 + 63: cc3 * 64 + 64])
                        return e.tensor_tensor(
                            ktl.ap().rearrange("p (c s) -> p c s", s=128),
                            ktl.ap().rearrange("p (c s) -> p c s", s=128),
                            enb.ap().rearrange("p (c s) -> p c s", s=128), ALU.mult)
                    pb.use("vector", qkwave_b,
                           reads=(S["qtl"], S["eqb"], S["ktl"], S["enb"]),
                           writes=(S["qtlz0"], S["qtlz1"], S["ktl"], S["fend"]))
                    if i == 0:
                        emit_prep(1)
                    if B_STAGE < 2:
                        continue
                    def tpk(e):
                        for c in range(8):
                            ins = e.transpose(PT1[:, c * 128:(c + 1) * 128],
                                              ktl[:, c * 128:(c + 1) * 128], sb_ident[:])
                        return ins
                    pb.use("tensor", tpk, reads=(S["ktl"], S["consts"]), writes=(S["pt1"],))
                    pb.use("vector", lambda e: e.tensor_copy(ktok[:], PT1.ap()[:, 0:1024]),
                           reads=(S["pt1"],), writes=(S["ktok"],))
                    def tpv(e, i=i):
                        for c in range(8):
                            ins = e.transpose(PT2[:, c * 128:(c + 1) * 128],
                                              vbf[:, c * 256 + i * 128: c * 256 + i * 128 + 128],
                                              sb_ident[:])
                        return ins
                    pb.use("tensor", tpv, reads=(S["vbf"], S["consts"]), writes=(S["pt2"],))
                    pb.use("vector", lambda e: e.tensor_copy(
                        vaug65[:, :, 0:64],
                        PT2.ap()[:, 0:1024].rearrange("p (b c) -> p b c", c=64)),
                        reads=(S["pt2"],), writes=(S["vaug"],))
                    if B_STAGE < 3:
                        continue
                    for g in range(4):
                        pg = 2 + g % 2
                        def atmm(e, g=g, pg=pg):
                            for j in range(4):
                                h = AT_ORDER[g * 4 + j]
                                rows = slice(64 * (h % 2), 64 * (h % 2) + 64)
                                cols = slice((h // 2) * 128, (h // 2) * 128 + 128)
                                ins = e.matmul(PS[pg][:, j * 128:(j + 1) * 128],
                                               ktl[rows, cols], qtl[rows, cols],
                                               start=(j == 0), stop=(j == 3),
                                               skip_group_check=True)
                            return ins
                        pb.use("tensor", atmm, reads=(S["ktl"], S["qtl"]),
                               writes=(S["ps%d" % pg],))
                        pb.use("vector", lambda e, g=g, pg=pg: e.tensor_tensor(
                            asb[:, g * 512:(g + 1) * 512], PS[pg][:, 0:512],
                            sb_maskat4[:], ALU.mult),
                            reads=(S["ps%d" % pg], S["bconst"]), writes=(S["asb"],))
                    if B_STAGE < 4:
                        continue
                    for bk in range(4):
                        def intmm(e, bk=bk):
                            for bl in range(4):
                                h = bk * 4 + bl
                                ins = e.matmul(HOUT[bk][:, bl * 128: bl * 128 + 65],
                                               asb[:, AT_POS[h] * 128:AT_POS[h] * 128 + 128],
                                               vaug[:, h * 65:(h + 1) * 65],
                                               start=(bl == 0), stop=False, skip_group_check=True)
                            return ins
                        pb.use("tensor", intmm, reads=(S["asb"], S["vaug"]), writes=(HSL[bk],))
                    if B_STAGE < 5:
                        continue
                    for cc in range(2 if B_SUB >= 4 else 1):
                        def intermm(e, cc=cc):
                            for h in range(16):
                                bk, bl = h // 4, h % 4
                                hcols = slice((h // 2) * 128, (h // 2) * 128 + 128)
                                ins = e.matmul(HOUT[bk][:, bl * 128: bl * 128 + 65],
                                               qtlz[:, cc][:, hcols],
                                               cbfz[:, h % 2][:, (h // 2) * 65:(h // 2) * 65 + 65],
                                               start=False, stop=(cc == 1), skip_group_check=True)
                            return ins
                        pb.use("tensor", intermm, reads=(S["qtlz%d" % cc], S["cbfz"]),
                               writes=(HSL[0], HSL[1], HSL[2], HSL[3]))
                        if B_SUB >= 2:
                            for par in range(2):
                                def dmm(e, cc=cc, par=par):
                                    for h2 in range(8):
                                        h = h2 * 2 + par
                                        rows = slice(64 * par, 64 * par + 64)
                                        e.matmul(PS[2][rows, (h // 2) * 64: (h // 2) * 64 + 64],
                                                 ktok[cc * 64:(cc + 1) * 64, h * 64:(h + 1) * 64],
                                                 vaug[cc * 64:(cc + 1) * 64, h * 65: h * 65 + 64],
                                                 start=True, stop=True, skip_group_check=True)
                                        ins = e.matmul(PS[3][rows, (h // 2) + 8 * cc: (h // 2) + 8 * cc + 1],
                                                       ktok[cc * 64:(cc + 1) * 64, h * 64:(h + 1) * 64],
                                                       vaug[cc * 64:(cc + 1) * 64, h * 65 + 64: h * 65 + 65],
                                                       start=True, stop=True, skip_group_check=True)
                                    return ins
                                pb.use("tensor", dmm, reads=(S["ktok"], S["vaug"]),
                                       writes=(S["ps2"], S["ps3"]))
                        if B_SUB < 3:
                            continue
                        def supd1(e, cc=cc):
                            e.tensor_tensor(c65[:, :, 0:64], c65[:, :, 0:64],
                                            PS[2].ap()[:, 0:512].rearrange("p (b c) -> p b c", c=64),
                                            ALU.add)
                            return e.tensor_tensor(
                                c65[:, :, 64:65], c65[:, :, 64:65],
                                PS[3].ap()[:, 8 * cc: 8 * cc + 8].rearrange("p (b c) -> p b c", c=1),
                                ALU.add)
                        pb.use("vector", supd1, reads=(S["ps2"], S["ps3"], S["call"]),
                               writes=(S["call"],))
                        def supd2(e, cc=cc):
                            for b in range(8):
                                ins = e.tensor_scalar(call[:, b * 65:(b + 1) * 65],
                                                      call[:, b * 65:(b + 1) * 65],
                                                      fend[:, cc][:, b:b + 1], None, ALU.mult)
                            return ins
                        pb.use("vector", supd2, reads=(S["call"], S["fend"]),
                               writes=(S["call"],))
                        def supd3(e):
                            e.tensor_copy(cbfz[:, 0][0:64, :], call[0:64, :])
                            return e.tensor_copy(cbfz[:, 1][64:128, :], call[64:128, :])
                        pb.use("vector", supd3, reads=(S["call"],), writes=(S["cbfz"],))
                    if B_STAGE < 6:
                        continue
                    def fdens(e):
                        for bk in range(4):
                            ins = e.tensor_scalar(
                                dens.ap()[:, bk * 4:(bk + 1) * 4].rearrange("p (b c) -> p b c", c=1),
                                HOUT[bk].ap()[:, 0:512].rearrange("p (b c) -> p b c", c=128)[:, :, 64:65],
                                1.0, None, ALU.max)
                        return ins
                    pb.use("vector", fdens, reads=(HSL[0], HSL[1], HSL[2], HSL[3]),
                           writes=(S["dens"],))
                    pb.use("vector", lambda e: e.reciprocal(rden[:], dens[:]),
                           reads=(S["dens"],), writes=(S["rden"],))
                    def fhtok(e):
                        for h in range(16):
                            bk, bl = h // 4, h % 4
                            ins = e.tensor_scalar(
                                htok[:, h * 64:(h + 1) * 64], HOUT[bk][:, bl * 128: bl * 128 + 64],
                                rden[:, h:h + 1], None, ALU.mult)
                        return ins
                    pb.use("vector", fhtok, reads=(HSL[0], HSL[1], HSL[2], HSL[3], S["rden"]),
                           writes=(S["htok"],))
                    if B_STAGE < 7:
                        continue
                    def tph(e):
                        for c in range(8):
                            ins = e.transpose(PT1[:, c * 128:(c + 1) * 128],
                                              htok[:, c * 128:(c + 1) * 128], sb_ident[:])
                        return ins
                    pb.use("tensor", tph, reads=(S["htok"], S["consts"]), writes=(S["pt1"],))
                    pb.use("sync", lambda e, T0=T0: e.dma_start(
                        out=ogs[:], in_=og_h[:, :, T0:T0 + 128].rearrange("o p f -> p o f")),
                        reads=(S["og_h"],), writes=(S["ogs"],), dma=True)
                    pb.use("vector", lambda e: e.tensor_tensor(ogs[:], PT1.ap()[:, 0:1024], ogs[:], ALU.mult),
                           reads=(S["pt1"], S["ogs"]), writes=(S["ogs"],))
                    pb.use("scalar", lambda e, T0=T0: e.dma_start(
                        out=hg_h[:, :, T0:T0 + 128].rearrange("o p f -> p o f"), in_=ogs[:]),
                        reads=(S["ogs"],), writes=(S["hg_h"],), dma=True)
            pb.barrier()
            if "hg" in dbg:
                pb.use("scalar", lambda e: e.dma_start(out=dbg_out["hg"], in_=hg_h), dma=True)
                pb.barrier()
        xin_stk.close()

        if "D0" in phases:
            xn1_fm = E(nc.sbuf_tensor("xn1_fm", [128, 8 * SEQ], BF16))
            xn1_ap = xn1_fm.ap().rearrange("p (c s) -> p c s", s=SEQ)
            for _n in ("xn1f0", "xn1f1", "xn1f2", "xn1f3"):
                slot(_n)
        if "C" in phases:
          with ExitStack() as pctx:
            P = pctx.enter_context
            wout_sb = P(nc.sbuf_tensor("wout_sb", [128, 8 * 1024], BF16))
            slot("wout")
            pb.use("gpsimd", lambda e: e.dma_start(out=wout_sb[:], in_=cw_out),
                   writes=(S["wout"],), dma=True)
            if "D0" in phases:
                gw_sb = P(nc.sbuf_tensor("gw_sb", [128, 8 * 8], BF16))
                gb_sb = P(nc.sbuf_tensor("gb_sb", [128, 8], F32))
                d0_xn = P(nc.sbuf_tensor("d0_xn", [128, 2, DM], BF16))
                d0_sq = P(nc.sbuf_tensor("d0_sq", [128, 2, DM], F32))
                d0_st = P(nc.sbuf_tensor("d0_st", [128, 6, 16], F32))
                d0_r = P(nc.sbuf_tensor("d0_r", [128, 4, 32], F32))
                for nm in ("gw", "gb", "d0_xn0", "d0_xn1", "d0_sq", "d0_r",
                           "d0_sA", "d0_sB", "d0_nm", "d0_rs", "d0_t4", "d0_t5"):
                    slot(nm)
                pb.use("gpsimd", lambda e: e.dma_start(out=gw_sb[:], in_=cgw), writes=(S["gw"],), dma=True)
                pb.use("sync", lambda e: e.dma_start(out=gb_sb[:], in_=gate_b_bc), writes=(S["gb"],), dma=True)
            hg_ld = P(nc.sbuf_tensor("hg_ld", [128, 8 * 512], BF16))
            gs_ld = P(nc.sbuf_tensor("gs_ld", [128, 8 * 512], BF16))
            hgm = P(nc.sbuf_tensor("hgm", [128, 8 * 512], BF16))
            s_bf = P(nc.sbuf_tensor("s_bf", [128, 8 * 512], BF16))
            g0_ld = P(nc.sbuf_tensor("g0_ld", [128, 2, DM], BF16))
            x0_ld = P(nc.sbuf_tensor("x0_ld", [128, 2, DM], F32))
            sgf = P(nc.sbuf_tensor("sgf", [128, DM], F32))
            x1b = P(nc.sbuf_tensor("x1b", [128, 2, DM], F32))
            for nm in ("hg_ld", "gs_ld", "hgm", "s_bf", "g0_ld0", "g0_ld1",
                       "x0_ld0", "x0_ld1", "sgf", "x1b0", "x1b1"):
                slot(nm)
            for st in range(4):
                pb.use("sync", lambda e, st=st: e.dma_start(
                    out=hg_ld.ap().rearrange("p (c s) -> p c s", s=512),
                    in_=hg_h[:, :, st * 512:(st + 1) * 512].rearrange("o p f -> p o f")),
                    reads=(S["hg_h"],), writes=(S["hg_ld"],), dma=True)
                pb.use("sync", lambda e, st=st: e.dma_start(
                    out=gs_ld.ap().rearrange("p (c s) -> p c s", s=512),
                    in_=gsig_h[:, :, st * 512:(st + 1) * 512].rearrange("o p f -> p o f")),
                    reads=(S["gsig_h"],), writes=(S["gs_ld"],), dma=True)
                pb.use("vector", lambda e: e.tensor_tensor(hgm[:], hg_ld[:], gs_ld[:], ALU.mult),
                       reads=(S["hg_ld"], S["gs_ld"]), writes=(S["hgm"],))
                for oc in range(8):
                    pi = oc % 2
                    def mm(e, oc=oc, pi=pi):
                        for kc in range(8):
                            ins = e.matmul(PS[pi][:, 0:512],
                                           wout_sb[:, kc * 1024 + oc * 128: kc * 1024 + oc * 128 + 128],
                                           hgm[:, kc * 512:(kc + 1) * 512],
                                           start=(kc == 0), stop=(kc == 7), skip_group_check=True)
                        return ins
                    pb.use("tensor", mm, reads=(S["hgm"], S["wout"]), writes=(PSLOT[pi],))
                    pb.use("vector", lambda e, oc=oc, pi=pi: e.tensor_copy(
                        s_bf[:, oc * 512:(oc + 1) * 512], PS[pi][:, 0:512]),
                        reads=(PSLOT[pi],), writes=(S["s_bf"],))
                for i in range(4):
                    tg = st * 4 + i
                    p2 = i % 2
                    def tps(e, i=i):
                        for c in range(8):
                            ins = e.transpose(PT1[:, c * 128:(c + 1) * 128],
                                              s_bf[:, c * 512 + i * 128: c * 512 + i * 128 + 128],
                                              sb_ident[:])
                        return ins
                    pb.use("tensor", tps, reads=(S["s_bf"], S["consts"]), writes=(S["pt1"],))
                    pb.use("sync", lambda e, tg=tg, p2=p2: e.dma_start(out=g0_ld[:, p2], in_=g0_h[tg]),
                           reads=(S["g0_h"],), writes=(S["g0_ld%d" % p2],), dma=True)
                    pb.use("sync", lambda e, tg=tg, p2=p2: e.dma_start(out=x0_ld[:, p2], in_=x0[tg]),
                           writes=(S["x0_ld%d" % p2],), dma=True)
                    pb.use("vector", lambda e, p2=p2: e.tensor_tensor(
                        sgf[:], PT1.ap()[:, 0:1024], g0_ld[:, p2], ALU.mult),
                        reads=(S["pt1"], S["g0_ld%d" % p2]), writes=(S["sgf"],))
                    pb.use("vector", lambda e, p2=p2: e.tensor_tensor(
                        x1b[:, p2], sgf[:], x0_ld[:, p2], ALU.add),
                        reads=(S["sgf"], S["x0_ld%d" % p2]), writes=(S["x1b%d" % p2],))
                    pb.use("scalar", lambda e, tg=tg, p2=p2: e.dma_start(out=x1_h[tg], in_=x1b[:, p2]),
                           reads=(S["x1b%d" % p2],), writes=(S["x1_h"],), dma=True)
                    if "D0" not in phases:
                        continue
                    t = tg
                    sAd = d0_st.ap()[:, 0]; sBd = d0_st.ap()[:, 1]
                    pb.use("vector", lambda e, t=t, p2=p2, sAd=sAd: e.reduce_sum(
                        sAd[:, t:t + 1], x1b[:, p2][:, :], AX.X, negate=True),
                        reads=(S["x1b%d" % p2],), writes=(S["d0_sA"],))
                    pb.use("scalar", lambda e, t=t, p2=p2, sBd=sBd: e.activation(
                        d0_sq[:, p2][:, :], x1b[:, p2][:, :], AF.Square, accum_out=sBd[:, t:t + 1]),
                        reads=(S["x1b%d" % p2], S["d0_sq"]), writes=(S["d0_sq"], S["d0_sB"]))
            if "D0" in phases:
                sAd = d0_st.ap()[:, 0]; sBd = d0_st.ap()[:, 1]
                nm1 = d0_st.ap()[:, 2]; rs1 = d0_st.ap()[:, 3]
                t4d = d0_st.ap()[:, 4]; t5d = d0_st.ap()[:, 5]
                pb.use("vector", lambda e, nm1=nm1, sAd=sAd: e.tensor_scalar_mul(
                    nm1[:], sAd[:], 1.0 / DM), reads=(S["d0_sA"],), writes=(S["d0_nm"],))
                pb.use("vector", lambda e, nm1=nm1, t4d=t4d: e.tensor_tensor(
                    t4d[:], nm1[:], nm1[:], ALU.mult), reads=(S["d0_nm"],), writes=(S["d0_t4"],))
                pb.use("vector", lambda e, sBd=sBd, t4d=t4d, t5d=t5d: e.scalar_tensor_tensor(
                    t5d[:], sBd[:], 1.0 / DM, t4d[:], ALU.mult, ALU.subtract),
                    reads=(S["d0_sB"], S["d0_t4"]), writes=(S["d0_t5"],))
                pb.use("scalar", lambda e, t4d=t4d, t5d=t5d: e.activation(
                    t4d[:], t5d[:], AF.Ln, bias=sb_cst[:, 0:1], scale=1.0),
                    reads=(S["d0_t5"], S["consts"]), writes=(S["d0_t4"],))
                pb.use("scalar", lambda e, rs1=rs1, t4d=t4d: e.activation(
                    rs1[:], t4d[:], AF.Exp, bias=0.0, scale=-0.5),
                    reads=(S["d0_t4"],), writes=(S["d0_rs"],))
                for st in range(4):
                    for i in range(4):
                        t = st * 4 + i
                        p = t % 2
                        PT = PT1 if p == 0 else PT2
                        pb.use("sync", lambda e, t=t, p=p: e.dma_start(
                            out=x0_ld[:, p], in_=x1_h[t]),
                            reads=(S["x1_h"],), writes=(S["x0_ld%d" % p],), dma=True)
                        pb.use("vector", lambda e, t=t, p=p, nm1=nm1, rs1=rs1: e.tensor_scalar(
                            d0_xn[:, p][:, :], x0_ld[:, p][:, :], nm1[:, t:t + 1], rs1[:, t:t + 1],
                            ALU.add, ALU.mult),
                            reads=(S["x0_ld%d" % p], S["d0_nm"], S["d0_rs"]), writes=(S["d0_xn%d" % p],))
                        def tp(e, p=p, PT=PT):
                            for c in range(8):
                                ins = e.transpose(PT[:, c * 128:(c + 1) * 128],
                                                  d0_xn[:, p][:, c * 128:(c + 1) * 128], sb_ident[:])
                            return ins
                        pb.use("tensor", tp, reads=(S["d0_xn%d" % p], S["consts"]),
                               writes=(S["pt%d" % (p + 1)],))
                        pb.use("vector", lambda e, t=t, PT=PT: e.tensor_copy(
                            xn1_ap[:, :, t * 128:(t + 1) * 128],
                            PT.ap()[:, 0:1024].rearrange("p (c s) -> p c s", s=128)),
                            reads=(S["pt%d" % (p + 1)],), writes=(S["xn1f%d" % st],))
                        def rmm(e, t=t, i=i):
                            for kc in range(8):
                                ins = e.matmul(PS[2 + i][:, 0:8], xn1_ap[:, kc, t * 128:(t + 1) * 128],
                                               gw_sb[:, kc * 8:(kc + 1) * 8],
                                               start=(kc == 0), stop=(kc == 7), skip_group_check=True)
                            return ins
                        pb.use("tensor", rmm, reads=(S["xn1f%d" % st], S["gw"]),
                               writes=(S["ps%d" % (2 + i)],))
                    def u1(e, st=st):
                        for i in range(4):
                            ins = e.reduce_max(d0_r[:, i][:, 8:9], PS[2 + i][:, 0:8], AX.X, negate=True)
                        return ins
                    pb.use("vector", u1, reads=(S["ps2"], S["ps3"], S["ps4"], S["ps5"]),
                           writes=(S["d0_r"],))
                    def u2(e, st=st):
                        for i in range(4):
                            ins = e.activation(d0_r[:, i][:, 0:8], PS[2 + i][:, 0:8], AF.Exp,
                                               bias=d0_r[:, i][:, 8:9], scale=1.0)
                        return ins
                    pb.use("scalar", u2, reads=(S["ps2"], S["ps3"], S["ps4"], S["ps5"], S["d0_r"]),
                           writes=(S["d0_r"],))
                    R = [d0_r[:, i] for i in range(4)]
                    steps = [
                        lambda e, r: e.reduce_sum(r[:, 9:10], r[:, 0:8], AX.X),
                        lambda e, r: e.reciprocal(r[:, 10:11], r[:, 9:10]),
                        lambda e, r: e.scalar_tensor_tensor(r[:, 0:8], r[:, 0:8], r[:, 10:11],
                                                            gb_sb[:], ALU.mult, ALU.add),
                        lambda e, r: e.reduce_max(r[:, 11:12], r[:, 0:8], AX.X),
                        lambda e, r: e.tensor_scalar(r[:, 12:20], r[:, 0:8], r[:, 11:12],
                                                     None, ALU.is_ge),
                        lambda e, r: e.scalar_tensor_tensor(r[:, 21:29], r[:, 12:20], -1e30,
                                                            r[:, 0:8], ALU.mult, ALU.add),
                        lambda e, r: e.reduce_max(r[:, 30:31], r[:, 21:29], AX.X),
                        lambda e, r: e.tensor_scalar(r[:, 12:20], r[:, 0:8], r[:, 30:31],
                                                     None, ALU.is_ge),
                        lambda e, r: e.tensor_tensor(r[:, 0:8], r[:, 0:8], gb_sb[:], ALU.subtract),
                    ]
                    for stp in steps:
                        def wave(e, stp=stp):
                            for r in R:
                                ins = stp(e, r)
                            return ins
                        pb.use("vector", wave, reads=(S["d0_r"], S["gb"]), writes=(S["d0_r"],))
                    def wfin(e, st=st):
                        for i, r in enumerate(R):
                            ins = e.tensor_tensor(rw_tok[:, (st * 4 + i) * 8:(st * 4 + i + 1) * 8],
                                                  r[:, 0:8], r[:, 12:20], ALU.mult)
                        return ins
                    pb.use("vector", wfin, reads=(S["d0_r"],), writes=(S["rw_tok"],))
            pb.barrier()
            if "xn1" in dbg:
                pb.use("scalar", lambda e: e.dma_start(out=dbg_out["xn1"], in_=xn1_fm[:]),
                       reads=(S["xn1f0"], S["xn1f1"], S["xn1f2"], S["xn1f3"]), dma=True)
            if "rw" in dbg:
                pb.use("scalar", lambda e: e.dma_start(out=dbg_out["rw"], in_=rw_tok[:]),
                       reads=(S["rw_tok"],), dma=True)
            if "x1" in dbg:
                pb.use("scalar", lambda e: e.dma_start(out=dbg_out["x1"], in_=x1_h), dma=True)
                pb.barrier()

        if "D" in phases:
          with ExitStack() as pctx:
            P = pctx.enter_context
            wqkv_sb = P(nc.sbuf_tensor("wqkv_sb", [128, 8 * 3072], BF16))
            sb_causal = P(nc.sbuf_tensor("sb_causal", [128, 128], F32))
            slot("causal")
            pb.use("sync", lambda e: e.dma_start(out=sb_causal[:], in_=causal_neg),
                   writes=(S["causal"],), dma=True)
            K_sb = P(nc.sbuf_tensor("K_sb", [128, 8 * SEQ], BF16))
            Vt_sb = P(nc.sbuf_tensor("Vt_sb", [128, NT * 1024], BF16))
            qbuf = P(nc.sbuf_tensor("qbuf", [128, 8 * 512], BF16))
            vtmp = P(nc.sbuf_tensor("vtmp", [128, 8 * 512], BF16))
            srow = P(nc.sbuf_tensor("srow", [128, SEQ], F32))
            pbf = P(nc.sbuf_tensor("pbf", [128, 2, SEQ], BF16))
            pts = P(nc.sbuf_tensor("pts", [128, 2, 1024], BF16))
            dsc = P(nc.sbuf_tensor("dsc", [128, 2], F32))
            dscm = P(nc.sbuf_tensor("dscm", [128, 4], F32))
            s1b = P(nc.sbuf_tensor("s1b", [128, DM], F32))
            for nm in ("wqkv", "K_sb", "Vt_sb", "qbuf", "vtmp", "srow",
                       "pbf00", "pbf01", "pbf10", "pbf11",
                       "pts0", "pts1", "dsc", "dscm", "s1b"):
                slot(nm)
            pb.use("gpsimd", lambda e: e.dma_start(out=wqkv_sb[:], in_=cWqkv), writes=(S["wqkv"],), dma=True)

            def emit_pv(t):
                tp2 = t % 2
                for gr in range((t + 8) // 8):
                    j0 = gr * 8
                    jn = min(8, t + 1 - j0)
                    pj = gr % 2
                    PT = PT1 if pj == 0 else PT2
                    def tpg(e, j0=j0, jn=jn, PT=PT, tp2=tp2):
                        for jj in range(jn):
                            ins = e.transpose(PT[:, jj * 128:(jj + 1) * 128],
                                              pbf[:, tp2][:, (j0 + jj) * 128:(j0 + jj + 1) * 128],
                                              sb_ident[:])
                        return ins
                    pb.use("tensor", tpg, reads=(S["pbf%d%d" % (tp2, gr)], S["consts"]),
                           writes=(S["pt%d" % (pj + 1)],))
                    pb.use("vector", lambda e, pj=pj, jn=jn, PT=PT: e.tensor_copy(
                        pts[:, pj][:, 0:jn * 128], PT[:, 0:jn * 128]),
                        reads=(S["pt%d" % (pj + 1)],), writes=(S["pts%d" % pj],))
                    def pvmm(e, j0=j0, jn=jn, pj=pj, t=t):
                        for jj in range(jn):
                            j = j0 + jj
                            lh = pts[:, pj][:, jj * 128:(jj + 1) * 128]
                            e.matmul(PS[2][:, 0:512], lh, Vt_sb[:, j * 1024: j * 1024 + 512],
                                     start=(j == 0), stop=(j == t), skip_group_check=True)
                            e.matmul(PS[3][:, 0:512], lh, Vt_sb[:, j * 1024 + 512: j * 1024 + 1024],
                                     start=(j == 0), stop=(j == t), skip_group_check=True)
                            ins = e.matmul(PS[4][:, 0:1], lh, sb_ones[:, 0:1],
                                           start=(j == 0), stop=(j == t), skip_group_check=True)
                        return ins
                    pb.use("tensor", pvmm, reads=(S["pts%d" % pj], S["Vt_sb"], S["consts"]),
                           writes=(S["ps2"], S["ps3"], S["ps4"]))
                pb.use("vector", lambda e: e.reciprocal(dsc[:, 1:2], PS[4][:, 0:1]),
                       reads=(S["ps4"],), writes=(S["dsc"],))
                pb.use("vector", lambda e: e.tensor_scalar(
                    s1b[:, 0:512], PS[2][:, 0:512], dsc[:, 1:2], None, ALU.mult),
                    reads=(S["ps2"], S["dsc"]), writes=(S["s1b"],))
                pb.use("vector", lambda e: e.tensor_scalar(
                    s1b[:, 512:1024], PS[3][:, 0:512], dsc[:, 1:2], None, ALU.mult),
                    reads=(S["ps3"], S["dsc"]), writes=(S["s1b"],))
                pb.use("scalar", lambda e, t=t: e.dma_start(out=s1_h[t], in_=s1b[:]),
                       reads=(S["s1b"],), writes=(S["s1_h"],), dma=True)

            for st in range(4):
                for oc in range(24):
                    pi = oc % 2
                    def mm(e, oc=oc, st=st, pi=pi):
                        for kc in range(8):
                            ins = e.matmul(PS[pi][:, 0:512],
                                           wqkv_sb[:, kc * 3072 + oc * 128: kc * 3072 + oc * 128 + 128],
                                           xn1_ap[:, kc, st * 512:(st + 1) * 512],
                                           start=(kc == 0), stop=(kc == 7), skip_group_check=True)
                        return ins
                    pb.use("tensor", mm, reads=(S["xn1f%d" % st], S["wqkv"]), writes=(PSLOT[pi],))
                    if oc < 8:
                        dstt, dsl, off = qbuf, "qbuf", oc * 512
                        pb.use("vector", lambda e, pi=pi, off=off, dstt=dstt: e.tensor_copy(
                            dstt[:, off:off + 512], PS[pi][:, 0:512]), reads=(PSLOT[pi],), writes=(S[dsl],))
                    elif oc < 16:
                        off = (oc - 8) * SEQ + st * 512
                        pb.use("vector", lambda e, pi=pi, off=off: e.tensor_copy(
                            K_sb[:, off:off + 512], PS[pi][:, 0:512]), reads=(PSLOT[pi],), writes=(S["K_sb"],))
                    else:
                        off = (oc - 16) * 512
                        pb.use("vector", lambda e, pi=pi, off=off: e.tensor_copy(
                            vtmp[:, off:off + 512], PS[pi][:, 0:512]), reads=(PSLOT[pi],), writes=(S["vtmp"],))
                for i in range(4):
                    t = st * 4 + i
                    def tpv(e, i=i):
                        for c in range(8):
                            ins = e.transpose(PT1[:, c * 128:(c + 1) * 128],
                                              vtmp[:, c * 512 + i * 128: c * 512 + i * 128 + 128],
                                              sb_ident[:])
                        return ins
                    pb.use("tensor", tpv, reads=(S["vtmp"], S["consts"]), writes=(S["pt1"],))
                    pb.use("vector", lambda e, t=t: e.tensor_copy(
                        Vt_sb[:, t * 1024:(t + 1) * 1024], PT1.ap()[:, 0:1024]),
                        reads=(S["pt1"],), writes=(S["Vt_sb"],))
                    L = (t + 1) * 128
                    ng = (L + 511) // 512
                    for g in range(ng):
                        w = min(512, L - g * 512)
                        pi = g % 2
                        def smm(e, g=g, w=w, i=i, pi=pi):
                            for kc in range(8):
                                ins = e.matmul(PS[pi][:, 0:w],
                                               qbuf[:, kc * 512 + i * 128: kc * 512 + i * 128 + 128],
                                               K_sb[:, kc * SEQ + g * 512: kc * SEQ + g * 512 + w],
                                               start=(kc == 0), stop=(kc == 7), skip_group_check=True)
                            return ins
                        pb.use("tensor", smm, reads=(S["qbuf"], S["K_sb"]), writes=(PSLOT[pi],))
                        pb.use("vector", lambda e, g=g, w=w, pi=pi: e.tensor_scalar(
                            srow[:, g * 512: g * 512 + w], PS[pi][:, 0:w], 1.0 / 32, None, ALU.mult),
                            reads=(PSLOT[pi],), writes=(S["srow"],))
                        if g == ng - 1:
                            pb.use("vector", lambda e, t=t: e.tensor_tensor(
                                srow[:, t * 128:(t + 1) * 128], srow[:, t * 128:(t + 1) * 128],
                                sb_causal[:], ALU.add),
                                reads=(S["srow"], S["causal"]), writes=(S["srow"],))
                        pb.use("vector", lambda e, g=g, w=w: e.reduce_max(
                            dscm[:, g:g + 1], srow[:, g * 512: g * 512 + w], AX.X),
                            reads=(S["srow"],), writes=(S["dscm"],))
                    pb.use("vector", lambda e, ng=ng: e.reduce_max(
                        dsc[:, 0:1], dscm[:, 0:ng], AX.X, negate=True),
                        reads=(S["dscm"],), writes=(S["dsc"],))
                    tp2 = t % 2
                    for hh in range((L + 1023) // 1024):
                        w2h = min(1024, L - hh * 1024)
                        pb.use("scalar", lambda e, hh=hh, w2h=w2h, tp2=tp2: e.activation(
                            pbf[:, tp2][:, hh * 1024: hh * 1024 + w2h],
                            srow[:, hh * 1024: hh * 1024 + w2h],
                            AF.Exp, bias=dsc[:, 0:1], scale=1.0),
                            reads=(S["srow"], S["dsc"]), writes=(S["pbf%d%d" % (tp2, hh)],))
                    if t > 0:
                        emit_pv(t - 1)
            emit_pv(NT - 1)
            pb.barrier()
            if "s1" in dbg:
                pb.use("scalar", lambda e: e.dma_start(out=dbg_out["s1"], in_=s1_h), dma=True)
                pb.barrier()

        if "E" in phases:
          with ExitStack() as pctx:
            P = pctx.enter_context
            w1_sb = P(nc.sbuf_tensor("w1_sb", [128, 2, 8 * 1408], BF16))
            w3_sb = P(nc.sbuf_tensor("w3_sb", [128, 2, 8 * 1408], BF16))
            w2_sb = P(nc.sbuf_tensor("w2_sb", [128, 11 * 1024], BF16))
            b1c_sb = P(nc.sbuf_tensor("b1c_sb", [128, 11], F32))
            b3c_sb = P(nc.sbuf_tensor("b3c_sb", [128, 11], F32))
            b2m_sb = P(nc.sbuf_tensor("b2m_sb", [9, 1024], BF16))
            rwa = P(nc.sbuf_tensor("rwa", [128, 9], BF16))
            rwt = P(nc.sbuf_tensor("rwt", [9, 128], BF16))
            h_sb = P(nc.sbuf_tensor("h_sb", [128, 11 * 512], BF16))
            prod = P(nc.sbuf_tensor("prod", [128, 2, 512], F32))
            y_half = P(nc.sbuf_tensor("y_half", [128, 8 * 1024], F32))
            s1p = P(nc.sbuf_tensor("s1p", [128, 2, DM], F32))
            x1e = P(nc.sbuf_tensor("x1e", [128, DM], F32))
            slot("yout")
            for nm in ("w10", "w11", "w30", "w31", "w2", "b1c", "b3c", "b2m", "rwa", "rwt",
                       "h_sb", "prod0", "prod1", "s1p0", "s1p1", "x1e",
                       "yh0", "yh1", "yh2", "yh3", "yh4", "yh5", "yh6", "yh7"):
                slot(nm)

            def ld_s1(gi):
                bi = gi % 2
                pb.use("scalar", lambda e, gi=gi, bi=bi: e.dma_start(
                    out=s1p[:, bi], in_=s1_h[gi]),
                    reads=(S["s1_h"],), writes=(S["s1p%d" % bi],), dma=True)

            def ld_x1(gi):
                pb.use("sync", lambda e, gi=gi: e.dma_start(out=x1e[:], in_=x1_h[gi]),
                       reads=(S["x1_h"],), writes=(S["x1e"],), dma=True)
            pb.use("gpsimd", lambda e: e.dma_start(out=b2m_sb[0:8, :], in_=b2flat.rearrange("o (e v) -> (o e) v", v=1024)),
                   writes=(S["b2m"],), dma=True)
            pb.use("gpsimd", lambda e: e.dma_start(out=b2m_sb[8:9, :], in_=sb2flat),
                   writes=(S["b2m"],), dma=True)
            pb.use("vector", lambda e: e.memset(rwa[:, 8:9], 1.0), writes=(S["rwa"],))
            for half in range(2):
                for lt in range(8):
                    tg = half * 8 + lt
                    pb.use("vector", lambda e, tg=tg: e.tensor_copy(
                        rwa[:, 0:8], rw_tok[:, tg * 8:(tg + 1) * 8]),
                        reads=(S["rw_tok"],), writes=(S["rwa"],))
                    pb.use("tensor", lambda e: e.transpose(PT1[0:9, 0:128], rwa[:, 0:9], sb_ident[:]),
                           reads=(S["rwa"], S["consts"]), writes=(S["pt1"],))
                    pb.use("vector", lambda e: e.tensor_copy(rwt[0:9, :], PT1[0:9, 0:128]),
                           reads=(S["pt1"],), writes=(S["rwt"],))
                    def b2mm(e):
                        e.matmul(PS[2][:, 0:512], rwt[0:9, 0:128], b2m_sb[0:9, 0:512],
                                 start=True, stop=True, skip_group_check=True)
                        return e.matmul(PS[3][:, 0:512], rwt[0:9, 0:128], b2m_sb[0:9, 512:1024],
                                        start=True, stop=True, skip_group_check=True)
                    pb.use("tensor", b2mm, reads=(S["rwt"], S["b2m"]), writes=(S["ps2"], S["ps3"]))
                    pb.use("vector", lambda e, lt=lt: e.tensor_copy(
                        y_half[:, lt * 1024: lt * 1024 + 512], PS[2][:, 0:512]),
                        reads=(S["ps2"],), writes=(S["yh%d" % lt],))
                    pb.use("vector", lambda e, lt=lt: e.tensor_copy(
                        y_half[:, lt * 1024 + 512: lt * 1024 + 1024], PS[3][:, 0:512]),
                        reads=(S["ps3"],), writes=(S["yh%d" % lt],))
                for ex in range(9):
                    shared = ex == 8
                    wp = (half * 9 + ex) % 2
                    pb.use("gpsimd", lambda e, ex=ex, wp=wp: e.dma_start(
                        out=w1_sb[:, wp], in_=(csw1 if ex == 8 else cw1[ex])), writes=(S["w1%d" % wp],), dma=True)
                    pb.use("gpsimd", lambda e, ex=ex, wp=wp: e.dma_start(
                        out=w3_sb[:, wp], in_=(csw3 if ex == 8 else cw3[ex])), writes=(S["w3%d" % wp],), dma=True)
                    pb.use("gpsimd", lambda e, ex=ex: e.dma_start(
                        out=w2_sb[:], in_=(csw2 if ex == 8 else cw2[ex])), writes=(S["w2"],), dma=True)
                    pb.use("sync", lambda e, ex=ex: e.dma_start(
                        out=b1c_sb[:], in_=(sb1c if ex == 8 else b1c[ex])), writes=(S["b1c"],), dma=True)
                    pb.use("sync", lambda e, ex=ex: e.dma_start(
                        out=b3c_sb[:], in_=(sb3c if ex == 8 else b3c[ex])), writes=(S["b3c"],), dma=True)
                    if shared:
                        ld_s1(half * 8 + 0)
                        ld_s1(half * 8 + 1)
                        ld_x1(half * 8 + 0)
                    for st2 in range(2):
                        tok0 = half * 1024 + st2 * 512
                        for hc in range(11):
                            pA, pB = (0, 1) if hc % 2 == 0 else (4, 5)
                            pp = hc % 2
                            def mmA(e, hc=hc, tok0=tok0, pA=pA, wp=wp):
                                for kc in range(8):
                                    ins = e.matmul(PS[pA][:, 0:512],
                                                   w1_sb[:, wp][:, kc * 1408 + hc * 128: kc * 1408 + hc * 128 + 128],
                                                   xn1_ap[:, kc, tok0:tok0 + 512],
                                                   start=(kc == 0), stop=(kc == 7), skip_group_check=True)
                                return ins
                            pb.use("tensor", mmA, reads=(S["xn1f%d" % (half * 2 + st2)], S["w1%d" % wp]), writes=(S["ps%d" % pA],))
                            def mmB(e, hc=hc, tok0=tok0, pB=pB, wp=wp):
                                for kc in range(8):
                                    ins = e.matmul(PS[pB][:, 0:512],
                                                   w3_sb[:, wp][:, kc * 1408 + hc * 128: kc * 1408 + hc * 128 + 128],
                                                   xn1_ap[:, kc, tok0:tok0 + 512],
                                                   start=(kc == 0), stop=(kc == 7), skip_group_check=True)
                                return ins
                            pb.use("tensor", mmB, reads=(S["xn1f%d" % (half * 2 + st2)], S["w3%d" % wp]), writes=(S["ps%d" % pB],))
                            pb.use("scalar", lambda e, hc=hc, pB=pB, pp=pp: e.activation(
                                prod[:, pp], PS[pB][:, 0:512], AF.Identity, bias=b3c_sb[:, hc:hc + 1], scale=1.0),
                                reads=(S["ps%d" % pB], S["b3c"]), writes=(S["prod%d" % pp],))
                            pb.use("vector", lambda e, hc=hc, pA=pA, pp=pp: e.scalar_tensor_tensor(
                                prod[:, pp], PS[pA][:, 0:512], b1c_sb[:, hc:hc + 1], prod[:, pp],
                                ALU.add, ALU.mult),
                                reads=(S["ps%d" % pA], S["b1c"], S["prod%d" % pp]), writes=(S["prod%d" % pp],))
                            pb.use("scalar", lambda e, hc=hc, pp=pp: e.activation(
                                h_sb[:, hc * 512:(hc + 1) * 512], prod[:, pp], AF.Silu),
                                reads=(S["prod%d" % pp],), writes=(S["h_sb"],))
                        for i2 in range(4):
                            lt = st2 * 4 + i2
                            tg = half * 8 + lt
                            oA, oB = (2, 3) if i2 % 2 == 0 else (0, 1)
                            def omm(e, i2=i2, oA=oA, oB=oB):
                                for kc2 in range(11):
                                    e.matmul(PS[oA][:, 0:512],
                                             h_sb[:, kc2 * 512 + i2 * 128: kc2 * 512 + i2 * 128 + 128],
                                             w2_sb[:, kc2 * 1024: kc2 * 1024 + 512],
                                             start=(kc2 == 0), stop=(kc2 == 10), skip_group_check=True)
                                    ins = e.matmul(PS[oB][:, 0:512],
                                                   h_sb[:, kc2 * 512 + i2 * 128: kc2 * 512 + i2 * 128 + 128],
                                                   w2_sb[:, kc2 * 1024 + 512: kc2 * 1024 + 1024],
                                                   start=(kc2 == 0), stop=(kc2 == 10), skip_group_check=True)
                                return ins
                            pb.use("tensor", omm, reads=(S["h_sb"], S["w2"]), writes=(S["ps%d" % oA], S["ps%d" % oB]))
                            for v in range(2):
                                pv = oA if v == 0 else oB
                                if shared:
                                    pb.use("vector", lambda e, lt=lt, v=v, pv=pv: e.tensor_tensor(
                                        y_half[:, lt * 1024 + v * 512: lt * 1024 + v * 512 + 512],
                                        y_half[:, lt * 1024 + v * 512: lt * 1024 + v * 512 + 512],
                                        PS[pv][:, 0:512], ALU.add),
                                        reads=(S["ps%d" % pv], S["yh%d" % lt]), writes=(S["yh%d" % lt],))
                                else:
                                    pb.use("vector", lambda e, lt=lt, v=v, tg=tg, ex=ex, pv=pv: e.scalar_tensor_tensor(
                                        y_half[:, lt * 1024 + v * 512: lt * 1024 + v * 512 + 512],
                                        PS[pv][:, 0:512], rw_tok[:, tg * 8 + ex: tg * 8 + ex + 1],
                                        y_half[:, lt * 1024 + v * 512: lt * 1024 + v * 512 + 512],
                                        ALU.mult, ALU.add),
                                        reads=(S["ps%d" % pv], S["rw_tok"], S["yh%d" % lt]),
                                        writes=(S["yh%d" % lt],))
                        if shared:
                            for lt in range(st2 * 4, st2 * 4 + 4):
                                tg = half * 8 + lt
                                bi = lt % 2
                                yv = y_half[:, lt * 1024:(lt + 1) * 1024]
                                pb.use("scalar", lambda e, yv=yv: e.activation(yv, yv, AF.Silu),
                                       reads=(S["yh%d" % lt],), writes=(S["yh%d" % lt],))
                                pb.use("vector", lambda e, yv=yv, bi=bi: e.tensor_tensor(
                                    yv, yv, s1p[:, bi], ALU.mult),
                                    reads=(S["yh%d" % lt], S["s1p%d" % bi]), writes=(S["yh%d" % lt],))
                                if lt + 2 < 8:
                                    ld_s1(tg + 2)
                                pb.use("vector", lambda e, yv=yv: e.tensor_tensor(
                                    yv, yv, x1e[:], ALU.add),
                                    reads=(S["yh%d" % lt], S["x1e"]), writes=(S["yh%d" % lt],))
                                if lt + 1 < 8:
                                    ld_x1(tg + 1)
                                pb.use("sync", lambda e, tg=tg, yv=yv: e.dma_start(out=yout[tg], in_=yv),
                                       reads=(S["yh%d" % lt],), writes=(S["yout"],), dma=True)
            pb.barrier()

        blk = E(nc.Block())
        blk.sync(lambda e: [f(e) for f in pb.ops["sync"]])
        blk.gpsimd(lambda e: [f(e) for f in pb.ops["gpsimd"]])
        blk.vector(lambda e: [f(e) for f in pb.ops["vector"]])
        blk.scalar(lambda e: [f(e) for f in pb.ops["scalar"]])
        blk.tensor(lambda e: [f(e) for f in pb.ops["tensor"]])
    return nc


def kernel(**inputs):
    shared = prep_inputs(inputs)
    x = np.asarray(inputs["L0_attn_x"], dtype=np.float32)
    nc = build_program()
    in_maps = []
    for b in range(8):
        m = dict(shared)
        m["x0"] = np.ascontiguousarray(x[b].reshape(NT, 128, DM))
        in_maps.append(m)
    res = run_bass_kernel_spmd(nc, in_maps, core_ids=list(range(8)))
    out = np.stack([np.asarray(res.results[b]["y"], dtype=np.float32).reshape(SEQ, DM)
                    for b in range(8)], axis=0)
    return out


# version 35
# speedup vs baseline: 1.0579x; 1.0025x over previous
import numpy as np
import ml_dtypes
from contextlib import ExitStack

import concourse.bass as bass
import concourse.mybir as mybir
from concourse.bass_utils import run_bass_kernel_spmd

F32 = mybir.dt.float32
BF16 = mybir.dt.bfloat16
AF = mybir.ActivationFunctionType
ALU = mybir.AluOpType
AX = mybir.AxisListType

SEQ = 2048
DM = 1024
NT = SEQ // 128
EPS_LN = 1e-5
EPS_RMS = 1e-6
ENGINES = ("sync", "gpsimd", "vector", "scalar", "tensor")
CSEM = {"tensor": "pe", "vector": "dv", "scalar": "ac", "gpsimd": "gp"}
QSEM = {"sync": "ld", "gpsimd": "wd", "scalar": "sd"}


class Slot:
    __slots__ = ("name", "ready", "readers", "qsem")

    def __init__(self, name):
        self.name = name
        self.ready = None
        self.readers = []
        self.qsem = None


class PB:
    SERIAL_DMA = False

    def __init__(self, nc):
        self.nc = nc
        self.ops = {e: [] for e in ENGINES}
        self.sem = {}
        self.cnt = {}
        self.hwm = {e: {} for e in ENGINES}
        self.ctx = None

    def new_sem(self, ctx, name):
        self.sem[name] = ctx.enter_context(self.nc.semaphore(name))
        self.cnt[name] = 0

    def slot_qsem(self, s):
        if s.qsem is None:
            name = "q_" + s.name
            self.sem[name] = self.ctx.enter_context(self.nc.semaphore(name))
            self.cnt[name] = 0
            s.qsem = name
        return s.qsem

    def _wait(self, eng, sem, tgt):
        if self.hwm[eng].get(sem, 0) >= tgt:
            return
        self.hwm[eng][sem] = tgt
        h = self.sem[sem]
        self.ops[eng].append(lambda e, h=h, tgt=tgt: e.wait_ge(h, tgt))

    def wait_tok(self, eng, tok, cur_dma=False, raw=False):
        if tok is None:
            return
        sem, val = tok
        if not raw and not cur_dma and sem == CSEM.get(eng):
            return
        self._wait(eng, sem, val)

    def use(self, eng, f, reads=(), writes=(), dma=False, extra_tok=()):
        for s in reads:
            self.wait_tok(eng, s.ready, cur_dma=dma, raw=True)
        for s in writes:
            self.wait_tok(eng, s.ready, cur_dma=dma)
            for t in s.readers:
                self.wait_tok(eng, t, cur_dma=dma)
        for t in extra_tok:
            self.wait_tok(eng, t, cur_dma=dma, raw=True)
        if dma:
            sem = self.slot_qsem(writes[0]) if writes else QSEM[eng]
            by = 16
        else:
            sem = CSEM[eng]
            by = 1
        if dma and PB.SERIAL_DMA:
            self._wait(eng, sem, self.cnt[sem])
        self.cnt[sem] += by
        tok = (sem, self.cnt[sem])
        h = self.sem[sem]

        def run(e, f=f, h=h, by=by):
            ins = f(e)
            ins.then_inc(h, by)
        self.ops[eng].append(run)
        for s in writes:
            s.ready = tok
            s.readers = []
        for s in reads:
            s.readers.append(tok)
        return tok

    def raw(self, eng, f):
        self.ops[eng].append(f)

    def barrier(self):
        for s in list(self.sem):
            if s.startswith("q_") or s in ("ld", "wd", "sd"):
                self._wait("sync", s, self.cnt[s])
        for eng in ENGINES:
            own = CSEM.get(eng)
            if own is not None:
                h2 = self.sem[own]
                tgt2 = self.cnt[own]
                self.ops[eng].append(lambda e, h2=h2, tgt2=tgt2: e.wait_ge(h2, tgt2))
                self.hwm[eng][own] = max(self.hwm[eng].get(own, 0), tgt2)
            self.cnt["bar"] += 1
            h = self.sem["bar"]
            self.ops[eng].append(lambda e, h=h: e.nop().then_inc(h, 1))
        tgt = self.cnt["bar"]
        for eng in ENGINES:
            self._wait(eng, "bar", tgt)


def _pretile_kmajor(w_t, M):
    K = w_t.shape[0]
    return np.ascontiguousarray(
        w_t.reshape(K // 128, 128, M).transpose(1, 0, 2).reshape(128, (K // 128) * M))


def _f32(x):
    return np.ascontiguousarray(np.asarray(x, dtype=np.float32))


def prep_inputs(I):
    t = {}
    t["cw_in"] = _pretile_kmajor(_f32(I["L0_attn_in_proj_w"]).T, 2048)
    for nm, key in (("cw_q", "L0_attn_q_proj_w"), ("cw_k", "L0_attn_k_proj_w"),
                    ("cw_v", "L0_attn_v_proj_w"), ("cw_i", "L0_attn_i_gate_w"),
                    ("cw_f", "L0_attn_f_gate_w"), ("cw_o", "L0_attn_o_gate_w"),
                    ("cw_out", "L0_attn_out_proj_w")):
        t[nm] = _pretile_kmajor(_f32(I[key]).T, 1024)
    for nm, key in (("bi_c", "L0_attn_i_gate_b"), ("bf_c", "L0_attn_f_gate_b"),
                    ("bo_c", "L0_attn_o_gate_b")):
        t[nm] = np.ascontiguousarray(_f32(I[key]).reshape(8, 128).T)
    t["qnw_c"] = np.ascontiguousarray(np.tile(_f32(I["L0_attn_q_norm_w"]), 2).reshape(128, 1))
    t["knw_c"] = np.ascontiguousarray(np.tile(_f32(I["L0_attn_k_norm_w"]), 2).reshape(128, 1))
    t["csel"] = _pretile_kmajor(_f32(I["L0_ffn_expert_sel"]).T, 16)
    keys = _f32(I["L0_ffn_keys"])
    t["ckeys"] = np.ascontiguousarray(keys.reshape(16, 8, 128, 128).transpose(2, 0, 1, 3).reshape(128, 16 * 8 * 128))
    vals = _f32(I["L0_ffn_values"])
    t["cvalues"] = np.ascontiguousarray(vals.transpose(1, 0, 2).reshape(128, 16 * 1024))
    t["cWqkv"] = _pretile_kmajor(_f32(I["L1_attn_W_qkv"]), 3072)
    t["cgw"] = _pretile_kmajor(_f32(I["L1_ffn_gate_w"]).T, 8)
    t["gate_b_bc"] = np.ascontiguousarray(np.tile(_f32(I["L1_ffn_gate_b"]), (128, 1)))
    w1 = _f32(I["L1_ffn_w1"])
    t["cw1"] = np.ascontiguousarray(w1.reshape(8, 8, 128, 1408).transpose(0, 2, 1, 3).reshape(8, 128, 8 * 1408))
    w3 = _f32(I["L1_ffn_w3"])
    t["cw3"] = np.ascontiguousarray(w3.reshape(8, 8, 128, 1408).transpose(0, 2, 1, 3).reshape(8, 128, 8 * 1408))
    w2 = _f32(I["L1_ffn_w2"])
    t["cw2"] = np.ascontiguousarray(w2.reshape(8, 11, 128, 1024).transpose(0, 2, 1, 3).reshape(8, 128, 11 * 1024))
    t["b1c"] = np.ascontiguousarray(_f32(I["L1_ffn_b1"]).reshape(8, 11, 128).transpose(0, 2, 1))
    t["b3c"] = np.ascontiguousarray(_f32(I["L1_ffn_b3"]).reshape(8, 11, 128).transpose(0, 2, 1))
    t["b2flat"] = np.ascontiguousarray(_f32(I["L1_ffn_b2"]).reshape(1, 8 * 1024))
    t["csw1"] = _pretile_kmajor(_f32(I["L1_ffn_sw1"]), 1408)
    t["csw3"] = _pretile_kmajor(_f32(I["L1_ffn_sw3"]), 1408)
    sw2 = _f32(I["L1_ffn_sw2"])
    t["csw2"] = np.ascontiguousarray(sw2.reshape(11, 128, 1024).transpose(1, 0, 2).reshape(128, 11 * 1024))
    t["sb1c"] = np.ascontiguousarray(_f32(I["L1_ffn_sb1"]).reshape(11, 128).T)
    t["sb3c"] = np.ascontiguousarray(_f32(I["L1_ffn_sb3"]).reshape(11, 128).T)
    t["sb2flat"] = np.ascontiguousarray(_f32(I["L1_ffn_sb2"]).reshape(1, 1024))
    t["ident_bf"] = np.eye(128, dtype=ml_dtypes.bfloat16)
    s_i = np.arange(128)[:, None]
    t_i = np.arange(128)[None, :]
    t["mask_at"] = ((s_i <= t_i) & (s_i // 64 == t_i // 64)).astype(ml_dtypes.bfloat16)
    t["causal_neg"] = np.where(t_i > s_i, np.float32(-1e30), np.float32(0.0))
    ind_all = np.zeros((16, 8 * 128), dtype=ml_dtypes.bfloat16)
    bones = np.zeros((128, 8 * 16), dtype=ml_dtypes.bfloat16)
    for kc in range(8):
        for p in range(128):
            h = 2 * kc + (p // 64)
            ind_all[h, kc * 128 + p] = 1
            bones[p, kc * 16 + h] = 1
    t["ind_all"] = ind_all
    t["bones"] = bones
    return t


PHASE_ALL = ("A0", "A1", "A2", "B", "C", "D0", "D", "E")
B_TILES = 32
B_STAGE = 9
B_SUB = 9
B_VAR = 0


def build_program(phases=PHASE_ALL, debug=()):
    nc = bass.Bass("TRN2", target_bir_lowering=False)
    dbg = set(debug)
    phases = set(phases)

    def IN(name, shape, dt=F32):
        return nc.dram_tensor(name, list(shape), dt, kind="ExternalInput").ap()

    x0 = IN("x0", [NT, 128, DM])
    cw_in = IN("cw_in", [128, 8 * 2048])
    cw_q = IN("cw_q", [128, 8 * 1024]); cw_k = IN("cw_k", [128, 8 * 1024])
    cw_v = IN("cw_v", [128, 8 * 1024]); cw_i = IN("cw_i", [128, 8 * 1024])
    cw_f = IN("cw_f", [128, 8 * 1024]); cw_o = IN("cw_o", [128, 8 * 1024])
    cw_out = IN("cw_out", [128, 8 * 1024])
    bi_c = IN("bi_c", [128, 8]); bf_c = IN("bf_c", [128, 8]); bo_c = IN("bo_c", [128, 8])
    qnw_c = IN("qnw_c", [128, 1]); knw_c = IN("knw_c", [128, 1])
    csel = IN("csel", [128, 8 * 16])
    ckeys = IN("ckeys", [128, 16 * 8 * 128])
    cvalues = IN("cvalues", [128, 16 * 1024])
    cWqkv = IN("cWqkv", [128, 8 * 3072])
    cgw = IN("cgw", [128, 8 * 8])
    gate_b_bc = IN("gate_b_bc", [128, 8])
    cw1 = IN("cw1", [8, 128, 8 * 1408]); cw3 = IN("cw3", [8, 128, 8 * 1408])
    cw2 = IN("cw2", [8, 128, 11 * 1024])
    b1c = IN("b1c", [8, 128, 11]); b3c = IN("b3c", [8, 128, 11])
    b2flat = IN("b2flat", [1, 8 * 1024])
    csw1 = IN("csw1", [128, 8 * 1408]); csw3 = IN("csw3", [128, 8 * 1408])
    csw2 = IN("csw2", [128, 11 * 1024])
    sb1c = IN("sb1c", [128, 11]); sb3c = IN("sb3c", [128, 11])
    sb2flat = IN("sb2flat", [1, 1024])
    ident_bf = IN("ident_bf", [128, 128], BF16)
    mask_at = IN("mask_at", [128, 128], BF16)
    causal_neg = IN("causal_neg", [128, 128])
    ind_all = IN("ind_all", [16, 8 * 128], BF16)
    bones = IN("bones", [128, 8 * 16], BF16)

    yout = nc.dram_tensor("y", [NT, 128, DM], F32, kind="ExternalOutput").ap()

    gsig_h = nc.dram_tensor("gsig_h", [8, 128, SEQ], BF16, kind="Internal").ap()
    og_h = nc.dram_tensor("og_h", [8, 128, SEQ], BF16, kind="Internal").ap()
    g0_h = nc.dram_tensor("g0_h", [NT, 128, DM], BF16, kind="Internal").ap()
    hg_h = nc.dram_tensor("hg_h", [8, 128, SEQ], BF16, kind="Internal").ap()
    x1_h = nc.dram_tensor("x1_h", [NT, 128, DM], F32, kind="Internal").ap()
    s1_h = nc.dram_tensor("s1_h", [NT, 128, DM], F32, kind="Internal").ap()

    dbg_specs = {
        "xn": ([128, 8 * SEQ], BF16), "xinner": ([128, 8 * SEQ], BF16),
        "gsig": ([8, 128, SEQ], BF16), "og": ([8, 128, SEQ], BF16),
        "g0": ([NT, 128, DM], BF16), "hg": ([8, 128, SEQ], BF16),
        "x1": ([NT, 128, DM], F32), "s1": ([NT, 128, DM], F32),
        "xn1": ([128, 8 * SEQ], BF16), "rw": ([128, NT * 8], F32),
    }
    dbg_out = {k: nc.dram_tensor("dbg_" + k, list(s[0]), s[1], kind="ExternalOutput").ap()
               for k, s in dbg_specs.items() if k in dbg}

    with ExitStack() as ctx:
        E = ctx.enter_context
        pb = PB(nc)
        pb.ctx = ctx
        for s in ("ld", "wd", "sd", "pe", "dv", "ac", "gp", "bar"):
            pb.new_sem(ctx, s)

        sb_ident = E(nc.sbuf_tensor("sb_ident", [128, 128], BF16))
        sb_ones = E(nc.sbuf_tensor("sb_ones", [128, 512], BF16))
        sb_zeros = E(nc.sbuf_tensor("sb_zeros", [128, 64], F32))
        sb_cst = E(nc.sbuf_tensor("sb_cst", [128, 3], F32))
        xn_stk = ExitStack()
        xin_stk = ExitStack()
        a1w_stk = ExitStack()
        a2w_stk = ExitStack()
        x_inner = xin_stk.enter_context(nc.sbuf_tensor("x_inner", [128, 8 * SEQ], BF16, side="right"))
        xn_fm = xn_stk.enter_context(nc.sbuf_tensor("xn_fm", [128, 8 * SEQ], BF16, side="right"))
        keys_sb = a2w_stk.enter_context(nc.sbuf_tensor("keys_sb", [128, 16 * 8 * 128], BF16, side="right"))
        vals_sb = a2w_stk.enter_context(nc.sbuf_tensor("vals_sb", [128, 16 * 1024], BF16, side="right"))
        sel_sb = a2w_stk.enter_context(nc.sbuf_tensor("sel_sb", [128, 8 * 16], BF16, side="right"))
        w_in_sb = a1w_stk.enter_context(nc.sbuf_tensor("w_in_sb", [128, 8 * 2048], BF16, side="right"))
        w_o_sb = a1w_stk.enter_context(nc.sbuf_tensor("w_o_sb", [128, 8 * 1024], BF16, side="right"))
        rw_tok = E(nc.sbuf_tensor("rw_tok", [128, NT * 8], F32))

        PS = [E(nc.psum_tensor("psb%d" % i, [128, 512], F32)) for i in range(6)]
        PT1 = E(nc.psum_tensor("pt1", [128, 1024], BF16))
        PT2 = E(nc.psum_tensor("pt2", [128, 1024], BF16))

        S = {}
        def slot(name):
            S[name] = Slot(name)
            return S[name]
        for nm in ("consts", "xnf0", "xnf1", "xnf2", "xnf3", "x_inner", "rw_tok",
                   "ps0", "ps1", "ps2", "ps3", "ps4", "ps5", "pt1", "pt2",
                   "gsig_h", "og_h", "g0_h", "hg_h", "x1_h", "s1_h"):
            slot(nm)
        PSLOT = [S["ps%d" % i] for i in range(6)]

        xn_ap = xn_fm.ap().rearrange("p (c s) -> p c s", s=SEQ)
        xin_ap = x_inner.ap().rearrange("p (c s) -> p c s", s=SEQ)

        for dst, src in ((sb_ident, ident_bf),):
            pb.use("sync", lambda e, dst=dst, src=src: e.dma_start(out=dst[:], in_=src),
                   writes=(S["consts"],), dma=True)
        pb.use("vector", lambda e: e.memset(sb_ones[:], 1.0), writes=(S["consts"],))
        pb.use("vector", lambda e: e.memset(sb_zeros[:], 0.0), writes=(S["consts"],))
        pb.use("vector", lambda e: e.memset(sb_cst[:, 0:1], EPS_LN), writes=(S["consts"],))
        pb.use("vector", lambda e: e.memset(sb_cst[:, 1:2], EPS_RMS), writes=(S["consts"],))
        pb.use("vector", lambda e: e.memset(sb_cst[:, 2:3], 1.0), writes=(S["consts"],))
        sl_win = slot("w_in"); sl_wo = slot("w_o")
        sl_keys = slot("keys"); sl_vals = slot("vals"); sl_selw = slot("selw")
        if "A1" in phases:
            pb.use("gpsimd", lambda e: e.dma_start(out=w_in_sb[:], in_=cw_in), writes=(sl_win,), dma=True)
            pb.use("gpsimd", lambda e: e.dma_start(out=w_o_sb[:], in_=cw_o), writes=(sl_wo,), dma=True)
        if "A2" in phases:
            pb.use("gpsimd", lambda e: e.dma_start(out=keys_sb[:], in_=ckeys), writes=(sl_keys,), dma=True)
            pb.use("gpsimd", lambda e: e.dma_start(out=vals_sb[:], in_=cvalues), writes=(sl_vals,), dma=True)
            pb.use("gpsimd", lambda e: e.dma_start(out=sel_sb[:], in_=csel), writes=(sl_selw,), dma=True)

        a01_stk = ExitStack()
        if "A0" in phases:
            A0P = a01_stk.enter_context
            xt4 = A0P(nc.sbuf_tensor("xt4", [128, 4, DM], F32))
            sqd = A0P(nc.sbuf_tensor("sqd", [128, DM], F32))
            a0_xn = A0P(nc.sbuf_tensor("a0_xn", [128, 2, DM], BF16))
            a0_st = A0P(nc.sbuf_tensor("a0_st", [128, 6, 16], F32))
            sl_x4 = [slot("xt4_%d" % j) for j in range(4)]
            sl_sqd = slot("a0_sqd")
            sl_xn = [slot("a0_xn0"), slot("a0_xn1")]
            sl_sA = slot("a0_sA"); sl_sB = slot("a0_sB")
            sl_nm = slot("a0_nm"); sl_rs = slot("a0_rs")
            sl_t4 = slot("a0_t4"); sl_t5 = slot("a0_t5")
            sA = a0_st.ap()[:, 0]; sB = a0_st.ap()[:, 1]
            nm = a0_st.ap()[:, 2]; rs = a0_st.ap()[:, 3]
            q4 = a0_st.ap()[:, 4]; q5 = a0_st.ap()[:, 5]

            def emit_a0_load(g):
                for j in range(4):
                    t = g * 4 + j
                    pb.use("sync" if j % 2 == 0 else "scalar",
                           lambda e, t=t, j=j: e.dma_start(out=xt4[:, j], in_=x0[t]),
                           writes=(sl_x4[j],), dma=True)
                    pb.use("vector", lambda e, j=j, t=t, sA=sA: e.reduce_sum(
                        sA[:, t:t + 1], xt4[:, j][:, :], AX.X, negate=True),
                        reads=(sl_x4[j],), writes=(sl_sA,))
                    pb.use("scalar", lambda e, j=j, t=t, sB=sB: e.activation(
                        sqd[:], xt4[:, j][:, :], AF.Square, accum_out=sB[:, t:t + 1]),
                        reads=(sl_x4[j], sl_sqd), writes=(sl_sqd, sl_sB))

            def emit_a0_group(g):
                g4 = slice(g * 4, g * 4 + 4)
                pb.use("vector", lambda e, g4=g4, nm=nm, sA=sA: e.tensor_scalar_mul(
                    nm[:, g4], sA[:, g4], 1.0 / DM), reads=(sl_sA,), writes=(sl_nm,))
                pb.use("vector", lambda e, g4=g4, nm=nm, q4=q4: e.tensor_tensor(
                    q4[:, g4], nm[:, g4], nm[:, g4], ALU.mult), reads=(sl_nm,), writes=(sl_t4,))
                pb.use("vector", lambda e, g4=g4, sB=sB, q4=q4, q5=q5: e.scalar_tensor_tensor(
                    q5[:, g4], sB[:, g4], 1.0 / DM, q4[:, g4], ALU.mult, ALU.subtract),
                    reads=(sl_sB, sl_t4), writes=(sl_t5,))
                pb.use("scalar", lambda e, g4=g4, q4=q4, q5=q5: e.activation(
                    q4[:, g4], q5[:, g4], AF.Ln, bias=sb_cst[:, 0:1], scale=1.0),
                    reads=(sl_t5, S["consts"]), writes=(sl_t4,))
                pb.use("scalar", lambda e, g4=g4, rs=rs, q4=q4: e.activation(
                    rs[:, g4], q4[:, g4], AF.Exp, bias=0.0, scale=-0.5),
                    reads=(sl_t4,), writes=(sl_rs,))
                for j in range(4):
                    t = g * 4 + j
                    p = j % 2
                    PT = PT1 if p == 0 else PT2
                    pb.use("vector", lambda e, j=j, p=p, t=t, nm=nm, rs=rs: e.tensor_scalar(
                        a0_xn[:, p][:, :], xt4[:, j][:, :], nm[:, t:t + 1], rs[:, t:t + 1],
                        ALU.add, ALU.mult),
                        reads=(sl_x4[j], sl_nm, sl_rs), writes=(sl_xn[p],))
                    def tp(e, p=p, PT=PT):
                        for c in range(8):
                            ins = e.transpose(PT[:, c * 128:(c + 1) * 128],
                                              a0_xn[:, p][:, c * 128:(c + 1) * 128], sb_ident[:])
                        return ins
                    pb.use("tensor", tp, reads=(sl_xn[p], S["consts"]), writes=(S["pt%d" % (p + 1)],))
                    pb.use("vector", lambda e, t=t, PT=PT: e.tensor_copy(
                        xn_ap[:, :, t * 128:(t + 1) * 128],
                        PT.ap()[:, 0:1024].rearrange("p (c s) -> p c s", s=128)),
                        reads=(S["pt%d" % (p + 1)],), writes=(S["xnf%d" % (t // 4)],))
            if "A1" not in phases:
                for g in range(4):
                    emit_a0_load(g)
                    emit_a0_group(g)
            if "xn" in dbg and "A1" not in phases:
                pb.barrier()
                pb.use("scalar", lambda e: e.dma_start(out=dbg_out["xn"], in_=xn_fm[:]),
                       reads=(S["xnf0"], S["xnf1"], S["xnf2"], S["xnf3"]), dma=True)
                pb.barrier()

        if "A1" in phases:
            P = a01_stk.enter_context
            a1_bo = P(nc.sbuf_tensor("a1_bo", [128, 9], F32))
            a1_ev = P(nc.sbuf_tensor("a1_ev", [128, 2, 512], BF16))
            a1_sg = P(nc.sbuf_tensor("a1_sg", [128, 2, 512], BF16))
            sl_bo = slot("a1_bo")
            sl_ev = [slot("a1_ev0"), slot("a1_ev1")]
            sl_sg = [slot("a1_sg0"), slot("a1_sg1")]
            pb.use("sync", lambda e: e.dma_start(out=a1_bo[:, 0:8], in_=bo_c), writes=(sl_bo,), dma=True)
            pb.use("vector", lambda e: e.tensor_scalar_mul(a1_bo[:, 0:8], a1_bo[:, 0:8], 0.5),
                   reads=(sl_bo,), writes=(sl_bo,))

            for st in range(4):
                if st == 0:
                    emit_a0_load(0)
                emit_a0_group(st)
                for oc in range(24):
                    pi = oc % 2
                    bank, sbank = PS[pi], PSLOT[pi]
                    def mms(e, st=st, oc=oc, bank=bank):
                        for kc in range(8):
                            if oc < 16:
                                lhs = w_in_sb[:, kc * 2048 + oc * 128: kc * 2048 + oc * 128 + 128]
                                rhs = xn_ap[:, kc, st * 512:(st + 1) * 512]
                            else:
                                o2 = oc - 16
                                lhs = w_o_sb[:, kc * 1024 + o2 * 128: kc * 1024 + o2 * 128 + 128]
                                rhs = xin_ap[:, kc, st * 512:(st + 1) * 512]
                            ins = e.matmul(bank[:, 0:512], lhs, rhs,
                                           start=(kc == 0), stop=(kc == 7), skip_group_check=True)
                        return ins
                    pb.use("tensor", mms,
                           reads=(S["xnf%d" % st], sl_win) if oc < 16 else (S["x_inner"], sl_wo),
                           writes=(sbank,))
                    if oc < 8:
                        pb.use("scalar", lambda e, oc=oc, st=st, bank=bank: e.activation(
                            xin_ap[:, oc, st * 512:(st + 1) * 512], bank[:, 0:512],
                            AF.Identity, bias=0.0),
                            reads=(sbank,), writes=(S["x_inner"],))
                    else:
                        if oc < 16:
                            c2, dsth = oc - 8, gsig_h
                            bias_ap = None
                        else:
                            c2, dsth = oc - 16, og_h
                            bias_ap = a1_bo
                        ep = oc % 2
                        def tanh_op(e, bank=bank, bias_ap=bias_ap, c2=c2, ep=ep):
                            b = bias_ap[:, c2:c2 + 1] if bias_ap is not None else 0.0
                            return e.activation(a1_ev[:, ep][:, :], bank[:, 0:512], AF.Tanh,
                                                bias=b, scale=0.5)
                        pb.use("scalar", tanh_op, reads=(sbank,) + ((sl_bo,) if bias_ap is not None else ()),
                               writes=(sl_ev[ep],))
                        pb.use("vector", lambda e, ep=ep: e.tensor_scalar(
                            a1_sg[:, ep][:, :], a1_ev[:, ep][:, :], 0.5, 0.5, ALU.mult, ALU.add),
                            reads=(sl_ev[ep],), writes=(sl_sg[ep],))
                        pb.use("scalar", lambda e, dsth=dsth, c2=c2, st=st, ep=ep: e.dma_start(
                            out=dsth[c2, :, st * 512:(st + 1) * 512], in_=a1_sg[:, ep][:, :]),
                            reads=(sl_sg[ep],), writes=(S["gsig_h" if dsth is gsig_h else "og_h"],),
                            dma=True)
                    if oc == 11 and st < 3:
                        emit_a0_load(st + 1)
            pb.barrier()
            if "xinner" in dbg:
                pb.use("scalar", lambda e: e.dma_start(out=dbg_out["xinner"], in_=x_inner[:]),
                       reads=(S["x_inner"],), dma=True)
            if "gsig" in dbg:
                pb.use("scalar", lambda e: e.dma_start(out=dbg_out["gsig"], in_=gsig_h), dma=True)
            if "og" in dbg:
                pb.use("scalar", lambda e: e.dma_start(out=dbg_out["og"], in_=og_h), dma=True)
            if "xn" in dbg:
                pb.use("scalar", lambda e: e.dma_start(out=dbg_out["xn"], in_=xn_fm[:]),
                       reads=(S["xnf0"], S["xnf1"], S["xnf2"], S["xnf3"]), dma=True)
            pb.barrier()
            a1w_stk.close()
        a01_stk.close()

        if "A2" in phases:
          with ExitStack() as pctx:
            P = pctx.enter_context
            selt = P(nc.sbuf_tensor("selt", [16, 512], F32))
            selbf = P(nc.sbuf_tensor("selbf", [16, 512], BF16))
            selflat = P(nc.sbuf_tensor("selflat", [1, 16 * 512], BF16))
            rbuf = P(nc.sbuf_tensor("rbuf", [128, 16 * 512], BF16))
            rp = P(nc.sbuf_tensor("rp", [128, 2, 128], BF16))
            a2_t = P(nc.sbuf_tensor("a2_t", [128, 2, 512], F32))
            a2_g = P(nc.sbuf_tensor("a2_g", [128, DM], BF16))
            sl_selt = slot("selt"); sl_selbf = slot("selbf"); sl_selflat = slot("selflat")
            sl_rbuf = slot("rbuf")
            sl_rp = [slot("rp0"), slot("rp1")]
            sl_a2t = [slot("a2_t0"), slot("a2_t1")]
            sl_a2g = slot("a2_g")

            for st in range(4):
                def selmm(e, st=st):
                    for kc in range(8):
                        ins = e.matmul(PS[4][0:16, 0:512], sel_sb[:, kc * 16:(kc + 1) * 16],
                                       xn_ap[:, kc, st * 512:(st + 1) * 512],
                                       start=(kc == 0), stop=(kc == 7), skip_group_check=True)
                    return ins
                pb.use("tensor", selmm, reads=(S["xnf%d" % st], sl_selw), writes=(S["ps4"],))
                pb.use("scalar", lambda e: e.activation(selt[0:16, :], PS[4][0:16, 0:512],
                                                        AF.Tanh, bias=0.0, scale=0.5),
                       reads=(S["ps4"],), writes=(sl_selt,))
                pb.use("vector", lambda e: e.tensor_scalar(selbf[0:16, :], selt[0:16, :],
                                                           0.5, 0.5, ALU.mult, ALU.add),
                       reads=(sl_selt,), writes=(sl_selbf,))
                pb.use("gpsimd", lambda e: e.dma_start(out=selflat[0:1, 0:8192], in_=selbf[0:16, 0:512]),
                       reads=(sl_selbf,), writes=(sl_selflat,), dma=True)
                for ex in range(16):
                    pi = ex % 2
                    def scmm(e, st=st, ex=ex, pi=pi):
                        for kc in range(8):
                            ins = e.matmul(PS[pi][:, 0:512],
                                           keys_sb[:, ex * 1024 + kc * 128: ex * 1024 + kc * 128 + 128],
                                           xn_ap[:, kc, st * 512:(st + 1) * 512],
                                           start=(kc == 0), stop=(kc == 7), skip_group_check=True)
                        return ins
                    pb.use("tensor", scmm, reads=(S["xnf%d" % st], sl_keys), writes=(PSLOT[pi],))
                    pb.use("tensor", lambda e, ex=ex, pi=pi: e.matmul(
                        PS[2 + pi][:, 0:512], sb_ones[0:1, 0:128],
                        selflat[0:1, ex * 512: ex * 512 + 512],
                        start=True, stop=True, skip_group_check=True),
                        reads=(sl_selflat, S["consts"]), writes=(S["ps%d" % (2 + pi)],))
                    pb.use("scalar", lambda e, ex=ex, pi=pi: e.activation(
                        rbuf[:, ex * 512:(ex + 1) * 512], PS[pi][:, 0:512], AF.Relu),
                        reads=(PSLOT[pi],), writes=(sl_rbuf,))
                    pb.use("vector", lambda e, ex=ex, pi=pi: e.tensor_tensor(
                        rbuf[:, ex * 512:(ex + 1) * 512], rbuf[:, ex * 512:(ex + 1) * 512],
                        PS[2 + pi][:, 0:512], ALU.mult),
                        reads=(sl_rbuf, S["ps%d" % (2 + pi)]), writes=(sl_rbuf,))
                for i in range(4):
                    tg = st * 4 + i
                    pA, pB2 = (4, 5) if i % 2 == 0 else (2, 3)
                    def vmm_all(e, i=i, pA=pA, pB2=pB2):
                        for ex in range(16):
                            lh = rbuf[:, ex * 512 + i * 128: ex * 512 + i * 128 + 128]
                            e.matmul(PS[pA][:, 0:512], lh, vals_sb[:, ex * 1024: ex * 1024 + 512],
                                     start=(ex == 0), stop=(ex == 15), skip_group_check=True)
                            ins = e.matmul(PS[pB2][:, 0:512], lh,
                                           vals_sb[:, ex * 1024 + 512: ex * 1024 + 1024],
                                           start=(ex == 0), stop=(ex == 15), skip_group_check=True)
                        return ins
                    pb.use("tensor", vmm_all, reads=(sl_rbuf, sl_vals),
                           writes=(S["ps%d" % pA], S["ps%d" % pB2]))
                    for v, pv in enumerate((pA, pB2)):
                        pb.use("scalar", lambda e, v=v, pv=pv: e.activation(
                            a2_t[:, v], PS[pv][:, 0:512], AF.Tanh, bias=0.0, scale=0.5),
                            reads=(S["ps%d" % pv],), writes=(sl_a2t[v],))
                        pb.use("vector", lambda e, v=v: e.tensor_scalar(
                            a2_t[:, v], a2_t[:, v], 0.5, 0.5, ALU.mult, ALU.add),
                            reads=(sl_a2t[v],), writes=(sl_a2t[v],))
                        pb.use("vector", lambda e, v=v, pv=pv: e.tensor_tensor(
                            a2_g[:, v * 512:(v + 1) * 512], a2_t[:, v], PS[pv][:, 0:512], ALU.mult),
                            reads=(sl_a2t[v], S["ps%d" % pv]), writes=(sl_a2g,))
                    pb.use("scalar", lambda e, tg=tg: e.dma_start(out=g0_h[tg], in_=a2_g[:]),
                           reads=(sl_a2g,), writes=(S["g0_h"],), dma=True)
            pb.barrier()
            if "g0" in dbg:
                pb.use("scalar", lambda e: e.dma_start(out=dbg_out["g0"], in_=g0_h), dma=True)
                pb.barrier()
          a2w_stk.close()

        a1w_stk.close()
        a2w_stk.close()
        xn_stk.close()

        if "B" in phases:
          with ExitStack() as pctx:
            P = pctx.enter_context
            sb_maskat = P(nc.sbuf_tensor("sb_maskat", [128, 128], BF16))
            sb_maskat4 = P(nc.sbuf_tensor("sb_maskat4", [128, 512], BF16))
            sb_ind = P(nc.sbuf_tensor("sb_ind", [16, 8 * 128], BF16))
            sb_bones = P(nc.sbuf_tensor("sb_bones", [128, 8 * 16], BF16))
            slot("bconst")
            for dst, src in ((sb_maskat, mask_at), (sb_ind, ind_all), (sb_bones, bones)):
                pb.use("sync", lambda e, dst=dst, src=src: e.dma_start(out=dst[:], in_=src),
                       writes=(S["bconst"],), dma=True)
            wq_sb = P(nc.sbuf_tensor("wq_sb", [128, 8 * 1024], BF16))
            wk_sb = P(nc.sbuf_tensor("wk_sb", [128, 8 * 1024], BF16))
            wv_sb = P(nc.sbuf_tensor("wv_sb", [128, 8 * 1024], BF16))
            wi_sb = P(nc.sbuf_tensor("wi_sb", [128, 8 * 1024], BF16))
            wf_sb = P(nc.sbuf_tensor("wf_sb", [128, 8 * 1024], BF16))
            bcol = P(nc.sbuf_tensor("bcol", [128, 16], F32))
            nwc = P(nc.sbuf_tensor("nwc", [128, 2], F32))
            qhat = P(nc.sbuf_tensor("qhat", [128, 8 * 256], BF16))
            khat = P(nc.sbuf_tensor("khat", [128, 8 * 256], BF16))
            vbf = P(nc.sbuf_tensor("vbf", [128, 8 * 256], BF16))
            ibf = P(nc.sbuf_tensor("ibf", [128, 8 * 256], BF16))
            qraw = P(nc.sbuf_tensor("qraw", [128, 8 * 256], BF16))
            rstdb = P(nc.sbuf_tensor("rstdb", [16, 256], BF16))
            rstdf = P(nc.sbuf_tensor("rstdf", [16, 256], F32))
            fpre = P(nc.sbuf_tensor("fpre", [128, 8 * 256], F32))
            fb1 = P(nc.sbuf_tensor("fb1", [128, 8 * 256], F32))
            lfb = P(nc.sbuf_tensor("lfb", [128, 8 * 256], F32))
            cumf = P(nc.sbuf_tensor("cumf", [128, 8 * 128], F32))
            eqb = P(nc.sbuf_tensor("eqb", [128, 8 * 128], BF16))
            enb = P(nc.sbuf_tensor("enb", [128, 8 * 128], BF16))
            qtl = P(nc.sbuf_tensor("qtl", [128, 8 * 128], BF16))
            qtlz = P(nc.sbuf_tensor("qtlz", [128, 2, 8 * 128], BF16))
            ktl = P(nc.sbuf_tensor("ktl", [128, 8 * 128], BF16))
            ktok = P(nc.sbuf_tensor("ktok", [128, 8 * 128], BF16))
            vaug = P(nc.sbuf_tensor("vaug", [128, 16 * 65], BF16))
            asb = P(nc.sbuf_tensor("asb", [128, 16 * 128], BF16))
            call = P(nc.sbuf_tensor("call", [128, 8 * 65], F32))
            cbfz = P(nc.sbuf_tensor("cbfz", [128, 2, 8 * 65], BF16))
            fend = P(nc.sbuf_tensor("fend", [128, 2, 8], F32))
            dens = P(nc.sbuf_tensor("dens", [128, 16], F32))
            rden = P(nc.sbuf_tensor("rden", [128, 16], F32))
            htok = P(nc.sbuf_tensor("htok", [128, 1024], BF16))
            ogs = P(nc.sbuf_tensor("ogs", [128, 8 * 128], BF16))
            for nm in ("wq", "wk", "wv", "wi", "wf", "bcol", "nwc", "qhat", "khat",
                       "vbf", "ibf", "qraw", "sqb", "rstdb", "rstdf", "fpre", "fb1", "lfb",
                       "cumf", "eqb", "enb", "qtl", "qtlz0", "qtlz1", "ktl", "ktok", "vaug", "asb",
                       "call", "cbfz", "fend", "dens", "rden", "htok", "ogs", "hgf"):
                slot(nm)
            for w, src, sn in ((wf_sb, cw_f, "wf"), (wq_sb, cw_q, "wq"), (wk_sb, cw_k, "wk"),
                               (wv_sb, cw_v, "wv"), (wi_sb, cw_i, "wi")):
                pb.use("gpsimd", lambda e, w=w, src=src: e.dma_start(out=w[:], in_=src),
                       writes=(S[sn],), dma=True)
            pb.use("sync", lambda e: e.dma_start(out=bcol[:, 0:8], in_=bi_c), writes=(S["bcol"],), dma=True)
            pb.use("sync", lambda e: e.dma_start(out=bcol[:, 8:16], in_=bf_c), writes=(S["bcol"],), dma=True)
            pb.use("sync", lambda e: e.dma_start(out=nwc[:, 0:1], in_=qnw_c), writes=(S["nwc"],), dma=True)
            pb.use("sync", lambda e: e.dma_start(out=nwc[:, 1:2], in_=knw_c), writes=(S["nwc"],), dma=True)
            pb.use("vector", lambda e: e.memset(vaug[:], 1.0), writes=(S["vaug"],))
            def mk_mask4(e):
                for c in range(4):
                    ins = e.tensor_copy(sb_maskat4[:, c * 128:(c + 1) * 128], sb_maskat[:])
                return ins
            pb.use("vector", mk_mask4, reads=(S["bconst"],), writes=(S["bconst"],))
            pb.use("vector", lambda e: e.memset(qtlz[:], 0.0), writes=(S["qtlz0"], S["qtlz1"]))
            pb.use("vector", lambda e: e.memset(call[:], 0.0), writes=(S["call"],))
            pb.use("vector", lambda e: e.memset(cbfz[:], 0.0), writes=(S["cbfz"],))

            qhat3 = qhat.ap().rearrange("p (c s) -> p c s", s=256)
            khat3 = khat.ap().rearrange("p (c s) -> p c s", s=256)
            vbf3 = vbf.ap().rearrange("p (c s) -> p c s", s=256)
            ibf3 = ibf.ap().rearrange("p (c s) -> p c s", s=256)
            c65 = call.ap().rearrange("p (b c) -> p b c", c=65)

            vaug65 = vaug.ap().rearrange("p (b c) -> p b c", c=65)
            eq3 = eqb.ap().rearrange("p (c s) -> p c s", s=128)
            HOUT = [PS[0], PS[1], PS[4], PS[5]]
            HSL = [S["ps0"], S["ps1"], S["ps4"], S["ps5"]]
            AT_ORDER = [0, 2, 4, 6, 8, 10, 12, 14, 1, 3, 5, 7, 9, 11, 13, 15]
            AT_POS = [AT_ORDER.index(h) for h in range(16)]

            def proj_norm(wsb, wsl, st, dst, dstsl, nwslice):
                for oc in range(8):
                    pi = oc % 2
                    def mm(e, oc=oc, st=st, wsb=wsb, pi=pi):
                        for kc in range(8):
                            ins = e.matmul(PS[pi][:, 0:256],
                                           wsb[:, kc * 1024 + oc * 128: kc * 1024 + oc * 128 + 128],
                                           xin_ap[:, kc, st * 256:(st + 1) * 256],
                                           start=(kc == 0), stop=(kc == 7), skip_group_check=True)
                        return ins
                    pb.use("tensor", mm, reads=(S["x_inner"], wsl), writes=(PSLOT[pi],))
                    pb.use("vector", lambda e, oc=oc, pi=pi: e.tensor_copy(
                        qraw[:, oc * 256:(oc + 1) * 256], PS[pi][:, 0:256]),
                        reads=(PSLOT[pi],), writes=(S["qraw"],))
                pb.use("vector", lambda e: e.tensor_tensor(asb[:], qraw[:], qraw[:], ALU.mult),
                       reads=(S["qraw"],), writes=(S["asb"],))
                def statmm(e):
                    for kc in range(8):
                        ins = e.matmul(PS[2][0:16, 0:256], sb_bones[:, kc * 16:(kc + 1) * 16],
                                       asb[:, kc * 256:(kc + 1) * 256],
                                       start=(kc == 0), stop=(kc == 7), skip_group_check=True)
                    return ins
                pb.use("tensor", statmm, reads=(S["asb"], S["bconst"]), writes=(S["ps2"],))
                pb.use("scalar", lambda e: e.activation(rstdf[0:16, 0:256], PS[2][0:16, 0:256],
                                                        AF.Ln, bias=sb_cst[0:16, 1:2], scale=1.0 / 64),
                       reads=(S["ps2"], S["consts"]), writes=(S["rstdf"],))
                pb.use("scalar", lambda e: e.activation(rstdb[0:16, 0:256], rstdf[0:16, 0:256],
                                                        AF.Exp, bias=0.0, scale=-0.5),
                       reads=(S["rstdf"],), writes=(S["rstdb"],))
                for oc in range(8):
                    pi = oc % 2
                    pb.use("tensor", lambda e, oc=oc, pi=pi: e.matmul(
                        PS[2 + pi][:, 0:256], sb_ind[:, oc * 128:(oc + 1) * 128],
                        rstdb[0:16, 0:256], start=True, stop=True, skip_group_check=True),
                        reads=(S["rstdb"], S["bconst"]), writes=(S["ps%d" % (2 + pi)],))
                    pb.use("vector", lambda e, oc=oc, pi=pi, dst=dst, nwslice=nwslice: e.scalar_tensor_tensor(
                        dst[:, oc * 256:(oc + 1) * 256], qraw[:, oc * 256:(oc + 1) * 256],
                        nwslice, PS[2 + pi][:, 0:256], ALU.mult, ALU.mult),
                        reads=(S["qraw"], S["nwc"], S["ps%d" % (2 + pi)]), writes=(dstsl,))

            for st in range(8):
                for oc in range(8):
                    pi = 2 + oc % 2
                    def mm(e, oc=oc, st=st, pi=pi):
                        for kc in range(8):
                            ins = e.matmul(PS[pi][:, 0:256],
                                           wf_sb[:, kc * 1024 + oc * 128: kc * 1024 + oc * 128 + 128],
                                           xin_ap[:, kc, st * 256:(st + 1) * 256],
                                           start=(kc == 0), stop=(kc == 7), skip_group_check=True)
                        return ins
                    pb.use("tensor", mm, reads=(S["x_inner"], S["wf"]), writes=(S["ps%d" % pi],))
                    pb.use("scalar", lambda e, oc=oc, pi=pi: e.activation(
                        fpre[:, oc * 256:(oc + 1) * 256], PS[pi][:, 0:256],
                        AF.Identity, bias=bcol[:, 8 + oc:9 + oc], scale=1.0),
                        reads=(S["ps%d" % pi], S["bcol"]), writes=(S["fpre"],))
                pb.use("scalar", lambda e: e.activation(fb1[:], fpre[:], AF.Abs),
                       reads=(S["fpre"],), writes=(S["fb1"],))
                pb.use("scalar", lambda e: e.activation(lfb[:], fb1[:], AF.Exp, bias=0.0, scale=-1.0),
                       reads=(S["fb1"],), writes=(S["lfb"],))
                pb.use("scalar", lambda e: e.activation(fb1[:], lfb[:], AF.Ln, bias=sb_cst[:, 2:3], scale=1.0),
                       reads=(S["lfb"], S["consts"]), writes=(S["fb1"],))
                pb.use("vector", lambda e: e.tensor_scalar(lfb[:], fpre[:], 0.0, None, ALU.min),
                       reads=(S["fpre"],), writes=(S["lfb"],))
                pb.use("vector", lambda e: e.tensor_tensor(lfb[:], lfb[:], fb1[:], ALU.subtract),
                       reads=(S["lfb"], S["fb1"]), writes=(S["lfb"],))

                def emit_prep(i, st=st):
                    if st * 2 + i >= B_TILES:
                        return
                    def scans(e, i=i):
                        for kc in range(8):
                            for cc in range(2):
                                ins = e.tensor_tensor_scan(
                                    cumf[:, kc * 128 + cc * 64: kc * 128 + cc * 64 + 64],
                                    lfb[:, kc * 256 + i * 128 + cc * 64: kc * 256 + i * 128 + cc * 64 + 64],
                                    sb_zeros[:, 0:64], 0.0, ALU.add, ALU.add)
                        return ins
                    pb.use("vector", scans, reads=(S["lfb"], S["consts"]), writes=(S["cumf"],))
                    if B_STAGE < 1:
                        return
                    def eewave(e):
                        e.activation(eqb[:], cumf[:], AF.Exp, bias=0.0, scale=1.0)
                        return e.activation(enb[:], cumf[:], AF.Exp, bias=0.0, scale=-1.0)
                    pb.use("scalar", eewave, reads=(S["cumf"],), writes=(S["eqb"], S["enb"]))

                emit_prep(0)
                proj_norm(wq_sb, S["wq"], st, qhat, S["qhat"], nwc[:, 0:1])
                proj_norm(wk_sb, S["wk"], st, khat, S["khat"], nwc[:, 1:2])
                for which, wsb, wsl in (("v", wv_sb, S["wv"]), ("i", wi_sb, S["wi"])):
                    for oc in range(8):
                        pi = oc % 2
                        def mm(e, oc=oc, st=st, wsb=wsb, pi=pi):
                            for kc in range(8):
                                ins = e.matmul(PS[pi][:, 0:256],
                                               wsb[:, kc * 1024 + oc * 128: kc * 1024 + oc * 128 + 128],
                                               xin_ap[:, kc, st * 256:(st + 1) * 256],
                                               start=(kc == 0), stop=(kc == 7), skip_group_check=True)
                            return ins
                        pb.use("tensor", mm, reads=(S["x_inner"], wsl), writes=(PSLOT[pi],))
                        if which == "v":
                            pb.use("vector", lambda e, oc=oc, pi=pi: e.tensor_copy(
                                vbf[:, oc * 256:(oc + 1) * 256], PS[pi][:, 0:256]),
                                reads=(PSLOT[pi],), writes=(S["vbf"],))
                        else:
                            pb.use("scalar", lambda e, oc=oc, pi=pi: e.activation(
                                ibf[:, oc * 256:(oc + 1) * 256], PS[pi][:, 0:256],
                                AF.Exp, bias=bcol[:, oc:oc + 1], scale=1.0),
                                reads=(PSLOT[pi], S["bcol"]), writes=(S["ibf"],))

                for i in range(2):
                    T0 = st * 256 + i * 128
                    if st * 2 + i >= B_TILES:
                        continue
                    if B_STAGE < 1:
                        continue
                    def qkwave_a(e, i=i):
                        e.tensor_tensor(qtl.ap().rearrange("p (c s) -> p c s", s=128),
                                        qhat3[:, :, i * 128:(i + 1) * 128], eq3, ALU.mult)
                        return e.tensor_tensor(ktl.ap().rearrange("p (c s) -> p c s", s=128),
                                               khat3[:, :, i * 128:(i + 1) * 128],
                                               ibf3[:, :, i * 128:(i + 1) * 128], ALU.mult)
                    pb.use("vector", qkwave_a,
                           reads=(S["qhat"], S["eqb"], S["khat"], S["ibf"]),
                           writes=(S["qtl"], S["ktl"]))
                    def qkwave_b(e):
                        for cc2 in range(2):
                            e.tensor_copy(
                                qtlz.ap()[:, cc2].rearrange("p (c s) -> p c s", s=128)[:, :, cc2 * 64:(cc2 + 1) * 64],
                                qtl.ap().rearrange("p (c s) -> p c s", s=128)[:, :, cc2 * 64:(cc2 + 1) * 64])
                        for cc3 in range(2):
                            e.tensor_copy(
                                fend.ap()[:, cc3].rearrange("p (b c) -> p b c", c=1),
                                eq3[:, :, cc3 * 64 + 63: cc3 * 64 + 64])
                        return e.tensor_tensor(
                            ktl.ap().rearrange("p (c s) -> p c s", s=128),
                            ktl.ap().rearrange("p (c s) -> p c s", s=128),
                            enb.ap().rearrange("p (c s) -> p c s", s=128), ALU.mult)
                    pb.use("vector", qkwave_b,
                           reads=(S["qtl"], S["eqb"], S["ktl"], S["enb"]),
                           writes=(S["qtlz0"], S["qtlz1"], S["ktl"], S["fend"]))
                    if i == 0:
                        emit_prep(1)
                    if B_STAGE < 2:
                        continue
                    def tpk(e):
                        for c in range(8):
                            ins = e.transpose(PT1[:, c * 128:(c + 1) * 128],
                                              ktl[:, c * 128:(c + 1) * 128], sb_ident[:])
                        return ins
                    pb.use("tensor", tpk, reads=(S["ktl"], S["consts"]), writes=(S["pt1"],))
                    pb.use("vector", lambda e: e.tensor_copy(ktok[:], PT1.ap()[:, 0:1024]),
                           reads=(S["pt1"],), writes=(S["ktok"],))
                    def tpv(e, i=i):
                        for c in range(8):
                            ins = e.transpose(PT2[:, c * 128:(c + 1) * 128],
                                              vbf[:, c * 256 + i * 128: c * 256 + i * 128 + 128],
                                              sb_ident[:])
                        return ins
                    pb.use("tensor", tpv, reads=(S["vbf"], S["consts"]), writes=(S["pt2"],))
                    pb.use("vector", lambda e: e.tensor_copy(
                        vaug65[:, :, 0:64],
                        PT2.ap()[:, 0:1024].rearrange("p (b c) -> p b c", c=64)),
                        reads=(S["pt2"],), writes=(S["vaug"],))
                    if B_STAGE < 3:
                        continue
                    for g in range(4):
                        pg = 2 + g % 2
                        def atmm(e, g=g, pg=pg):
                            for j in range(4):
                                h = AT_ORDER[g * 4 + j]
                                rows = slice(64 * (h % 2), 64 * (h % 2) + 64)
                                cols = slice((h // 2) * 128, (h // 2) * 128 + 128)
                                ins = e.matmul(PS[pg][:, j * 128:(j + 1) * 128],
                                               ktl[rows, cols], qtl[rows, cols],
                                               start=(j == 0), stop=(j == 3),
                                               skip_group_check=True)
                            return ins
                        pb.use("tensor", atmm, reads=(S["ktl"], S["qtl"]),
                               writes=(S["ps%d" % pg],))
                        pb.use("vector", lambda e, g=g, pg=pg: e.tensor_tensor(
                            asb[:, g * 512:(g + 1) * 512], PS[pg][:, 0:512],
                            sb_maskat4[:], ALU.mult),
                            reads=(S["ps%d" % pg], S["bconst"]), writes=(S["asb"],))
                    if B_STAGE < 4:
                        continue
                    for bk in range(4):
                        def intmm(e, bk=bk):
                            for bl in range(4):
                                h = bk * 4 + bl
                                ins = e.matmul(HOUT[bk][:, bl * 128: bl * 128 + 65],
                                               asb[:, AT_POS[h] * 128:AT_POS[h] * 128 + 128],
                                               vaug[:, h * 65:(h + 1) * 65],
                                               start=(bl == 0), stop=False, skip_group_check=True)
                            return ins
                        pb.use("tensor", intmm, reads=(S["asb"], S["vaug"]), writes=(HSL[bk],))
                    if B_STAGE < 5:
                        continue
                    for cc in range(2 if B_SUB >= 4 else 1):
                        def intermm(e, cc=cc):
                            for h in range(16):
                                bk, bl = h // 4, h % 4
                                hcols = slice((h // 2) * 128, (h // 2) * 128 + 128)
                                ins = e.matmul(HOUT[bk][:, bl * 128: bl * 128 + 65],
                                               qtlz[:, cc][:, hcols],
                                               cbfz[:, h % 2][:, (h // 2) * 65:(h // 2) * 65 + 65],
                                               start=False, stop=(cc == 1), skip_group_check=True)
                            return ins
                        pb.use("tensor", intermm, reads=(S["qtlz%d" % cc], S["cbfz"]),
                               writes=(HSL[0], HSL[1], HSL[2], HSL[3]))
                        if B_SUB >= 2:
                            for par in range(2):
                                def dmm(e, cc=cc, par=par):
                                    for h2 in range(8):
                                        h = h2 * 2 + par
                                        rows = slice(64 * par, 64 * par + 64)
                                        e.matmul(PS[2][rows, (h // 2) * 64: (h // 2) * 64 + 64],
                                                 ktok[cc * 64:(cc + 1) * 64, h * 64:(h + 1) * 64],
                                                 vaug[cc * 64:(cc + 1) * 64, h * 65: h * 65 + 64],
                                                 start=True, stop=True, skip_group_check=True)
                                        ins = e.matmul(PS[3][rows, (h // 2) + 8 * cc: (h // 2) + 8 * cc + 1],
                                                       ktok[cc * 64:(cc + 1) * 64, h * 64:(h + 1) * 64],
                                                       vaug[cc * 64:(cc + 1) * 64, h * 65 + 64: h * 65 + 65],
                                                       start=True, stop=True, skip_group_check=True)
                                    return ins
                                pb.use("tensor", dmm, reads=(S["ktok"], S["vaug"]),
                                       writes=(S["ps2"], S["ps3"]))
                        if B_SUB < 3:
                            continue
                        def supd1(e, cc=cc):
                            e.tensor_tensor(c65[:, :, 0:64], c65[:, :, 0:64],
                                            PS[2].ap()[:, 0:512].rearrange("p (b c) -> p b c", c=64),
                                            ALU.add)
                            return e.tensor_tensor(
                                c65[:, :, 64:65], c65[:, :, 64:65],
                                PS[3].ap()[:, 8 * cc: 8 * cc + 8].rearrange("p (b c) -> p b c", c=1),
                                ALU.add)
                        pb.use("vector", supd1, reads=(S["ps2"], S["ps3"], S["call"]),
                               writes=(S["call"],))
                        def supd2(e, cc=cc):
                            for b in range(8):
                                ins = e.tensor_scalar(call[:, b * 65:(b + 1) * 65],
                                                      call[:, b * 65:(b + 1) * 65],
                                                      fend[:, cc][:, b:b + 1], None, ALU.mult)
                            return ins
                        pb.use("vector", supd2, reads=(S["call"], S["fend"]),
                               writes=(S["call"],))
                        def supd3(e):
                            e.tensor_copy(cbfz[:, 0][0:64, :], call[0:64, :])
                            return e.tensor_copy(cbfz[:, 1][64:128, :], call[64:128, :])
                        pb.use("vector", supd3, reads=(S["call"],), writes=(S["cbfz"],))
                    if B_STAGE < 6:
                        continue
                    def fdens(e):
                        for bk in range(4):
                            ins = e.tensor_scalar(
                                dens.ap()[:, bk * 4:(bk + 1) * 4].rearrange("p (b c) -> p b c", c=1),
                                HOUT[bk].ap()[:, 0:512].rearrange("p (b c) -> p b c", c=128)[:, :, 64:65],
                                1.0, None, ALU.max)
                        return ins
                    pb.use("vector", fdens, reads=(HSL[0], HSL[1], HSL[2], HSL[3]),
                           writes=(S["dens"],))
                    pb.use("vector", lambda e: e.reciprocal(rden[:], dens[:]),
                           reads=(S["dens"],), writes=(S["rden"],))
                    def fhtok(e):
                        for h in range(16):
                            bk, bl = h // 4, h % 4
                            ins = e.tensor_scalar(
                                htok[:, h * 64:(h + 1) * 64], HOUT[bk][:, bl * 128: bl * 128 + 64],
                                rden[:, h:h + 1], None, ALU.mult)
                        return ins
                    pb.use("vector", fhtok, reads=(HSL[0], HSL[1], HSL[2], HSL[3], S["rden"]),
                           writes=(S["htok"],))
                    if B_STAGE < 7:
                        continue
                    def tph(e):
                        for c in range(8):
                            ins = e.transpose(PT1[:, c * 128:(c + 1) * 128],
                                              htok[:, c * 128:(c + 1) * 128], sb_ident[:])
                        return ins
                    pb.use("tensor", tph, reads=(S["htok"], S["consts"]), writes=(S["pt1"],))
                    pb.use("sync", lambda e, T0=T0: e.dma_start(
                        out=ogs[:], in_=og_h[:, :, T0:T0 + 128].rearrange("o p f -> p o f")),
                        reads=(S["og_h"],), writes=(S["ogs"],), dma=True)
                    pb.use("vector", lambda e: e.tensor_tensor(ogs[:], PT1.ap()[:, 0:1024], ogs[:], ALU.mult),
                           reads=(S["pt1"], S["ogs"]), writes=(S["ogs"],))
                    pb.use("scalar", lambda e, T0=T0: e.dma_start(
                        out=hg_h[:, :, T0:T0 + 128].rearrange("o p f -> p o f"), in_=ogs[:]),
                        reads=(S["ogs"],), writes=(S["hg_h"],), dma=True)
            pb.barrier()
            if "hg" in dbg:
                pb.use("scalar", lambda e: e.dma_start(out=dbg_out["hg"], in_=hg_h), dma=True)
                pb.barrier()
        xin_stk.close()

        if "D0" in phases:
            xn1_fm = E(nc.sbuf_tensor("xn1_fm", [128, 8 * SEQ], BF16))
            xn1_ap = xn1_fm.ap().rearrange("p (c s) -> p c s", s=SEQ)
            for _n in ("xn1f0", "xn1f1", "xn1f2", "xn1f3"):
                slot(_n)
        if "C" in phases:
          with ExitStack() as pctx:
            P = pctx.enter_context
            wout_sb = P(nc.sbuf_tensor("wout_sb", [128, 8 * 1024], BF16))
            slot("wout")
            pb.use("gpsimd", lambda e: e.dma_start(out=wout_sb[:], in_=cw_out),
                   writes=(S["wout"],), dma=True)
            if "D0" in phases:
                gw_sb = P(nc.sbuf_tensor("gw_sb", [128, 8 * 8], BF16))
                gb_sb = P(nc.sbuf_tensor("gb_sb", [128, 8], F32))
                d0_xn = P(nc.sbuf_tensor("d0_xn", [128, 2, DM], BF16))
                d0_sq = P(nc.sbuf_tensor("d0_sq", [128, 2, DM], F32))
                d0_st = P(nc.sbuf_tensor("d0_st", [128, 6, 16], F32))
                d0_r = P(nc.sbuf_tensor("d0_r", [128, 4, 32], F32))
                for nm in ("gw", "gb", "d0_xn0", "d0_xn1", "d0_sq", "d0_r",
                           "d0_sA", "d0_sB", "d0_nm", "d0_rs", "d0_t4", "d0_t5"):
                    slot(nm)
                pb.use("gpsimd", lambda e: e.dma_start(out=gw_sb[:], in_=cgw), writes=(S["gw"],), dma=True)
                pb.use("sync", lambda e: e.dma_start(out=gb_sb[:], in_=gate_b_bc), writes=(S["gb"],), dma=True)
            hg_ld = P(nc.sbuf_tensor("hg_ld", [128, 8 * 512], BF16))
            gs_ld = P(nc.sbuf_tensor("gs_ld", [128, 8 * 512], BF16))
            hgm = P(nc.sbuf_tensor("hgm", [128, 8 * 512], BF16))
            s_bf = P(nc.sbuf_tensor("s_bf", [128, 8 * 512], BF16))
            g0_ld = P(nc.sbuf_tensor("g0_ld", [128, 2, DM], BF16))
            x0_ld = P(nc.sbuf_tensor("x0_ld", [128, 2, DM], F32))
            sgf = P(nc.sbuf_tensor("sgf", [128, DM], F32))
            x1b = P(nc.sbuf_tensor("x1b", [128, 2, DM], F32))
            for nm in ("hg_ld", "gs_ld", "hgm", "s_bf", "g0_ld0", "g0_ld1",
                       "x0_ld0", "x0_ld1", "sgf", "x1b0", "x1b1"):
                slot(nm)
            for st in range(4):
                pb.use("sync", lambda e, st=st: e.dma_start(
                    out=hg_ld.ap().rearrange("p (c s) -> p c s", s=512),
                    in_=hg_h[:, :, st * 512:(st + 1) * 512].rearrange("o p f -> p o f")),
                    reads=(S["hg_h"],), writes=(S["hg_ld"],), dma=True)
                pb.use("sync", lambda e, st=st: e.dma_start(
                    out=gs_ld.ap().rearrange("p (c s) -> p c s", s=512),
                    in_=gsig_h[:, :, st * 512:(st + 1) * 512].rearrange("o p f -> p o f")),
                    reads=(S["gsig_h"],), writes=(S["gs_ld"],), dma=True)
                pb.use("vector", lambda e: e.tensor_tensor(hgm[:], hg_ld[:], gs_ld[:], ALU.mult),
                       reads=(S["hg_ld"], S["gs_ld"]), writes=(S["hgm"],))
                for oc in range(8):
                    pi = oc % 2
                    def mm(e, oc=oc, pi=pi):
                        for kc in range(8):
                            ins = e.matmul(PS[pi][:, 0:512],
                                           wout_sb[:, kc * 1024 + oc * 128: kc * 1024 + oc * 128 + 128],
                                           hgm[:, kc * 512:(kc + 1) * 512],
                                           start=(kc == 0), stop=(kc == 7), skip_group_check=True)
                        return ins
                    pb.use("tensor", mm, reads=(S["hgm"], S["wout"]), writes=(PSLOT[pi],))
                    pb.use("vector", lambda e, oc=oc, pi=pi: e.tensor_copy(
                        s_bf[:, oc * 512:(oc + 1) * 512], PS[pi][:, 0:512]),
                        reads=(PSLOT[pi],), writes=(S["s_bf"],))
                for i in range(4):
                    tg = st * 4 + i
                    p2 = i % 2
                    def tps(e, i=i):
                        for c in range(8):
                            ins = e.transpose(PT1[:, c * 128:(c + 1) * 128],
                                              s_bf[:, c * 512 + i * 128: c * 512 + i * 128 + 128],
                                              sb_ident[:])
                        return ins
                    pb.use("tensor", tps, reads=(S["s_bf"], S["consts"]), writes=(S["pt1"],))
                    pb.use("sync", lambda e, tg=tg, p2=p2: e.dma_start(out=g0_ld[:, p2], in_=g0_h[tg]),
                           reads=(S["g0_h"],), writes=(S["g0_ld%d" % p2],), dma=True)
                    pb.use("sync", lambda e, tg=tg, p2=p2: e.dma_start(out=x0_ld[:, p2], in_=x0[tg]),
                           writes=(S["x0_ld%d" % p2],), dma=True)
                    pb.use("vector", lambda e, p2=p2: e.tensor_tensor(
                        sgf[:], PT1.ap()[:, 0:1024], g0_ld[:, p2], ALU.mult),
                        reads=(S["pt1"], S["g0_ld%d" % p2]), writes=(S["sgf"],))
                    pb.use("vector", lambda e, p2=p2: e.tensor_tensor(
                        x1b[:, p2], sgf[:], x0_ld[:, p2], ALU.add),
                        reads=(S["sgf"], S["x0_ld%d" % p2]), writes=(S["x1b%d" % p2],))
                    pb.use("scalar", lambda e, tg=tg, p2=p2: e.dma_start(out=x1_h[tg], in_=x1b[:, p2]),
                           reads=(S["x1b%d" % p2],), writes=(S["x1_h"],), dma=True)
                    if "D0" not in phases:
                        continue
                    t = tg
                    sAd = d0_st.ap()[:, 0]; sBd = d0_st.ap()[:, 1]
                    pb.use("vector", lambda e, t=t, p2=p2, sAd=sAd: e.reduce_sum(
                        sAd[:, t:t + 1], x1b[:, p2][:, :], AX.X, negate=True),
                        reads=(S["x1b%d" % p2],), writes=(S["d0_sA"],))
                    pb.use("scalar", lambda e, t=t, p2=p2, sBd=sBd: e.activation(
                        d0_sq[:, p2][:, :], x1b[:, p2][:, :], AF.Square, accum_out=sBd[:, t:t + 1]),
                        reads=(S["x1b%d" % p2], S["d0_sq"]), writes=(S["d0_sq"], S["d0_sB"]))
            if "D0" in phases:
                sAd = d0_st.ap()[:, 0]; sBd = d0_st.ap()[:, 1]
                nm1 = d0_st.ap()[:, 2]; rs1 = d0_st.ap()[:, 3]
                t4d = d0_st.ap()[:, 4]; t5d = d0_st.ap()[:, 5]
                pb.use("vector", lambda e, nm1=nm1, sAd=sAd: e.tensor_scalar_mul(
                    nm1[:], sAd[:], 1.0 / DM), reads=(S["d0_sA"],), writes=(S["d0_nm"],))
                pb.use("vector", lambda e, nm1=nm1, t4d=t4d: e.tensor_tensor(
                    t4d[:], nm1[:], nm1[:], ALU.mult), reads=(S["d0_nm"],), writes=(S["d0_t4"],))
                pb.use("vector", lambda e, sBd=sBd, t4d=t4d, t5d=t5d: e.scalar_tensor_tensor(
                    t5d[:], sBd[:], 1.0 / DM, t4d[:], ALU.mult, ALU.subtract),
                    reads=(S["d0_sB"], S["d0_t4"]), writes=(S["d0_t5"],))
                pb.use("scalar", lambda e, t4d=t4d, t5d=t5d: e.activation(
                    t4d[:], t5d[:], AF.Ln, bias=sb_cst[:, 0:1], scale=1.0),
                    reads=(S["d0_t5"], S["consts"]), writes=(S["d0_t4"],))
                pb.use("scalar", lambda e, rs1=rs1, t4d=t4d: e.activation(
                    rs1[:], t4d[:], AF.Exp, bias=0.0, scale=-0.5),
                    reads=(S["d0_t4"],), writes=(S["d0_rs"],))
                for st in range(4):
                    for i in range(4):
                        t = st * 4 + i
                        p = t % 2
                        PT = PT1 if p == 0 else PT2
                        pb.use("sync", lambda e, t=t, p=p: e.dma_start(
                            out=x0_ld[:, p], in_=x1_h[t]),
                            reads=(S["x1_h"],), writes=(S["x0_ld%d" % p],), dma=True)
                        pb.use("vector", lambda e, t=t, p=p, nm1=nm1, rs1=rs1: e.tensor_scalar(
                            d0_xn[:, p][:, :], x0_ld[:, p][:, :], nm1[:, t:t + 1], rs1[:, t:t + 1],
                            ALU.add, ALU.mult),
                            reads=(S["x0_ld%d" % p], S["d0_nm"], S["d0_rs"]), writes=(S["d0_xn%d" % p],))
                        def tp(e, p=p, PT=PT):
                            for c in range(8):
                                ins = e.transpose(PT[:, c * 128:(c + 1) * 128],
                                                  d0_xn[:, p][:, c * 128:(c + 1) * 128], sb_ident[:])
                            return ins
                        pb.use("tensor", tp, reads=(S["d0_xn%d" % p], S["consts"]),
                               writes=(S["pt%d" % (p + 1)],))
                        pb.use("vector", lambda e, t=t, PT=PT: e.tensor_copy(
                            xn1_ap[:, :, t * 128:(t + 1) * 128],
                            PT.ap()[:, 0:1024].rearrange("p (c s) -> p c s", s=128)),
                            reads=(S["pt%d" % (p + 1)],), writes=(S["xn1f%d" % st],))
                        def rmm(e, t=t, i=i):
                            for kc in range(8):
                                ins = e.matmul(PS[2 + i][:, 0:8], xn1_ap[:, kc, t * 128:(t + 1) * 128],
                                               gw_sb[:, kc * 8:(kc + 1) * 8],
                                               start=(kc == 0), stop=(kc == 7), skip_group_check=True)
                            return ins
                        pb.use("tensor", rmm, reads=(S["xn1f%d" % st], S["gw"]),
                               writes=(S["ps%d" % (2 + i)],))
                    def u1(e, st=st):
                        for i in range(4):
                            ins = e.reduce_max(d0_r[:, i][:, 8:9], PS[2 + i][:, 0:8], AX.X, negate=True)
                        return ins
                    pb.use("vector", u1, reads=(S["ps2"], S["ps3"], S["ps4"], S["ps5"]),
                           writes=(S["d0_r"],))
                    def u2(e, st=st):
                        for i in range(4):
                            ins = e.activation(d0_r[:, i][:, 0:8], PS[2 + i][:, 0:8], AF.Exp,
                                               bias=d0_r[:, i][:, 8:9], scale=1.0)
                        return ins
                    pb.use("scalar", u2, reads=(S["ps2"], S["ps3"], S["ps4"], S["ps5"], S["d0_r"]),
                           writes=(S["d0_r"],))
                    R = [d0_r[:, i] for i in range(4)]
                    steps = [
                        lambda e, r: e.reduce_sum(r[:, 9:10], r[:, 0:8], AX.X),
                        lambda e, r: e.reciprocal(r[:, 10:11], r[:, 9:10]),
                        lambda e, r: e.scalar_tensor_tensor(r[:, 0:8], r[:, 0:8], r[:, 10:11],
                                                            gb_sb[:], ALU.mult, ALU.add),
                        lambda e, r: e.reduce_max(r[:, 11:12], r[:, 0:8], AX.X),
                        lambda e, r: e.tensor_scalar(r[:, 12:20], r[:, 0:8], r[:, 11:12],
                                                     None, ALU.is_ge),
                        lambda e, r: e.scalar_tensor_tensor(r[:, 21:29], r[:, 12:20], -1e30,
                                                            r[:, 0:8], ALU.mult, ALU.add),
                        lambda e, r: e.reduce_max(r[:, 30:31], r[:, 21:29], AX.X),
                        lambda e, r: e.tensor_scalar(r[:, 12:20], r[:, 0:8], r[:, 30:31],
                                                     None, ALU.is_ge),
                        lambda e, r: e.tensor_tensor(r[:, 0:8], r[:, 0:8], gb_sb[:], ALU.subtract),
                    ]
                    for stp in steps:
                        def wave(e, stp=stp):
                            for r in R:
                                ins = stp(e, r)
                            return ins
                        pb.use("vector", wave, reads=(S["d0_r"], S["gb"]), writes=(S["d0_r"],))
                    def wfin(e, st=st):
                        for i, r in enumerate(R):
                            ins = e.tensor_tensor(rw_tok[:, (st * 4 + i) * 8:(st * 4 + i + 1) * 8],
                                                  r[:, 0:8], r[:, 12:20], ALU.mult)
                        return ins
                    pb.use("vector", wfin, reads=(S["d0_r"],), writes=(S["rw_tok"],))
            pb.barrier()
            if "xn1" in dbg:
                pb.use("scalar", lambda e: e.dma_start(out=dbg_out["xn1"], in_=xn1_fm[:]),
                       reads=(S["xn1f0"], S["xn1f1"], S["xn1f2"], S["xn1f3"]), dma=True)
            if "rw" in dbg:
                pb.use("scalar", lambda e: e.dma_start(out=dbg_out["rw"], in_=rw_tok[:]),
                       reads=(S["rw_tok"],), dma=True)
            if "x1" in dbg:
                pb.use("scalar", lambda e: e.dma_start(out=dbg_out["x1"], in_=x1_h), dma=True)
                pb.barrier()

        if "D" in phases:
          with ExitStack() as pctx:
            P = pctx.enter_context
            wqkv_sb = P(nc.sbuf_tensor("wqkv_sb", [128, 8 * 3072], BF16))
            sb_causal = P(nc.sbuf_tensor("sb_causal", [128, 128], F32))
            slot("causal")
            pb.use("sync", lambda e: e.dma_start(out=sb_causal[:], in_=causal_neg),
                   writes=(S["causal"],), dma=True)
            K_sb = P(nc.sbuf_tensor("K_sb", [128, 8 * SEQ], BF16))
            Vt_sb = P(nc.sbuf_tensor("Vt_sb", [128, NT * 1024], BF16))
            qbuf = P(nc.sbuf_tensor("qbuf", [128, 8 * 512], BF16))
            vtmp = P(nc.sbuf_tensor("vtmp", [128, 8 * 512], BF16))
            srow = P(nc.sbuf_tensor("srow", [128, SEQ], F32))
            pbf = P(nc.sbuf_tensor("pbf", [128, 2, SEQ], BF16))
            pts = P(nc.sbuf_tensor("pts", [128, 2, 1024], BF16))
            dsc = P(nc.sbuf_tensor("dsc", [128, 2], F32))
            dscm = P(nc.sbuf_tensor("dscm", [128, 4], F32))
            s1b = P(nc.sbuf_tensor("s1b", [128, DM], F32))
            for nm in ("wqkv", "K_sb", "Vt_sb", "qbuf", "vtmp", "srow",
                       "pbf00", "pbf01", "pbf10", "pbf11",
                       "pts0", "pts1", "dsc", "dscm", "s1b"):
                slot(nm)
            pb.use("gpsimd", lambda e: e.dma_start(out=wqkv_sb[:], in_=cWqkv), writes=(S["wqkv"],), dma=True)

            def emit_pv(t):
                tp2 = t % 2
                for gr in range((t + 8) // 8):
                    j0 = gr * 8
                    jn = min(8, t + 1 - j0)
                    pj = gr % 2
                    PT = PT1 if pj == 0 else PT2
                    def tpg(e, j0=j0, jn=jn, PT=PT, tp2=tp2):
                        for jj in range(jn):
                            ins = e.transpose(PT[:, jj * 128:(jj + 1) * 128],
                                              pbf[:, tp2][:, (j0 + jj) * 128:(j0 + jj + 1) * 128],
                                              sb_ident[:])
                        return ins
                    pb.use("tensor", tpg, reads=(S["pbf%d%d" % (tp2, gr)], S["consts"]),
                           writes=(S["pt%d" % (pj + 1)],))
                    pb.use("vector", lambda e, pj=pj, jn=jn, PT=PT: e.tensor_copy(
                        pts[:, pj][:, 0:jn * 128], PT[:, 0:jn * 128]),
                        reads=(S["pt%d" % (pj + 1)],), writes=(S["pts%d" % pj],))
                    def pvmm(e, j0=j0, jn=jn, pj=pj, t=t):
                        for jj in range(jn):
                            j = j0 + jj
                            lh = pts[:, pj][:, jj * 128:(jj + 1) * 128]
                            e.matmul(PS[2][:, 0:512], lh, Vt_sb[:, j * 1024: j * 1024 + 512],
                                     start=(j == 0), stop=(j == t), skip_group_check=True)
                            e.matmul(PS[3][:, 0:512], lh, Vt_sb[:, j * 1024 + 512: j * 1024 + 1024],
                                     start=(j == 0), stop=(j == t), skip_group_check=True)
                            ins = e.matmul(PS[4][:, 0:1], lh, sb_ones[:, 0:1],
                                           start=(j == 0), stop=(j == t), skip_group_check=True)
                        return ins
                    pb.use("tensor", pvmm, reads=(S["pts%d" % pj], S["Vt_sb"], S["consts"]),
                           writes=(S["ps2"], S["ps3"], S["ps4"]))
                pb.use("vector", lambda e: e.reciprocal(dsc[:, 1:2], PS[4][:, 0:1]),
                       reads=(S["ps4"],), writes=(S["dsc"],))
                pb.use("vector", lambda e: e.tensor_scalar(
                    s1b[:, 0:512], PS[2][:, 0:512], dsc[:, 1:2], None, ALU.mult),
                    reads=(S["ps2"], S["dsc"]), writes=(S["s1b"],))
                pb.use("vector", lambda e: e.tensor_scalar(
                    s1b[:, 512:1024], PS[3][:, 0:512], dsc[:, 1:2], None, ALU.mult),
                    reads=(S["ps3"], S["dsc"]), writes=(S["s1b"],))
                pb.use("scalar", lambda e, t=t: e.dma_start(out=s1_h[t], in_=s1b[:]),
                       reads=(S["s1b"],), writes=(S["s1_h"],), dma=True)

            for st in range(4):
                for oc in range(24):
                    pi = oc % 2
                    def mm(e, oc=oc, st=st, pi=pi):
                        for kc in range(8):
                            ins = e.matmul(PS[pi][:, 0:512],
                                           wqkv_sb[:, kc * 3072 + oc * 128: kc * 3072 + oc * 128 + 128],
                                           xn1_ap[:, kc, st * 512:(st + 1) * 512],
                                           start=(kc == 0), stop=(kc == 7), skip_group_check=True)
                        return ins
                    pb.use("tensor", mm, reads=(S["xn1f%d" % st], S["wqkv"]), writes=(PSLOT[pi],))
                    if oc < 8:
                        dstt, dsl, off = qbuf, "qbuf", oc * 512
                        pb.use("vector", lambda e, pi=pi, off=off, dstt=dstt: e.tensor_copy(
                            dstt[:, off:off + 512], PS[pi][:, 0:512]), reads=(PSLOT[pi],), writes=(S[dsl],))
                    elif oc < 16:
                        off = (oc - 8) * SEQ + st * 512
                        pb.use("vector", lambda e, pi=pi, off=off: e.tensor_copy(
                            K_sb[:, off:off + 512], PS[pi][:, 0:512]), reads=(PSLOT[pi],), writes=(S["K_sb"],))
                    else:
                        off = (oc - 16) * 512
                        pb.use("vector", lambda e, pi=pi, off=off: e.tensor_copy(
                            vtmp[:, off:off + 512], PS[pi][:, 0:512]), reads=(PSLOT[pi],), writes=(S["vtmp"],))
                for i in range(4):
                    t = st * 4 + i
                    def tpv(e, i=i):
                        for c in range(8):
                            ins = e.transpose(PT1[:, c * 128:(c + 1) * 128],
                                              vtmp[:, c * 512 + i * 128: c * 512 + i * 128 + 128],
                                              sb_ident[:])
                        return ins
                    pb.use("tensor", tpv, reads=(S["vtmp"], S["consts"]), writes=(S["pt1"],))
                    pb.use("vector", lambda e, t=t: e.tensor_copy(
                        Vt_sb[:, t * 1024:(t + 1) * 1024], PT1.ap()[:, 0:1024]),
                        reads=(S["pt1"],), writes=(S["Vt_sb"],))
                    L = (t + 1) * 128
                    ng = (L + 511) // 512
                    for g in range(ng):
                        w = min(512, L - g * 512)
                        pi = g % 2
                        def smm(e, g=g, w=w, i=i, pi=pi):
                            for kc in range(8):
                                ins = e.matmul(PS[pi][:, 0:w],
                                               qbuf[:, kc * 512 + i * 128: kc * 512 + i * 128 + 128],
                                               K_sb[:, kc * SEQ + g * 512: kc * SEQ + g * 512 + w],
                                               start=(kc == 0), stop=(kc == 7), skip_group_check=True)
                            return ins
                        pb.use("tensor", smm, reads=(S["qbuf"], S["K_sb"]), writes=(PSLOT[pi],))
                        pb.use("vector", lambda e, g=g, w=w, pi=pi: e.tensor_scalar(
                            srow[:, g * 512: g * 512 + w], PS[pi][:, 0:w], 1.0 / 32, None, ALU.mult),
                            reads=(PSLOT[pi],), writes=(S["srow"],))
                        if g == ng - 1:
                            pb.use("vector", lambda e, t=t: e.tensor_tensor(
                                srow[:, t * 128:(t + 1) * 128], srow[:, t * 128:(t + 1) * 128],
                                sb_causal[:], ALU.add),
                                reads=(S["srow"], S["causal"]), writes=(S["srow"],))
                        pb.use("vector", lambda e, g=g, w=w: e.reduce_max(
                            dscm[:, g:g + 1], srow[:, g * 512: g * 512 + w], AX.X),
                            reads=(S["srow"],), writes=(S["dscm"],))
                    pb.use("vector", lambda e, ng=ng: e.reduce_max(
                        dsc[:, 0:1], dscm[:, 0:ng], AX.X, negate=True),
                        reads=(S["dscm"],), writes=(S["dsc"],))
                    tp2 = t % 2
                    for hh in range((L + 1023) // 1024):
                        w2h = min(1024, L - hh * 1024)
                        pb.use("scalar", lambda e, hh=hh, w2h=w2h, tp2=tp2: e.activation(
                            pbf[:, tp2][:, hh * 1024: hh * 1024 + w2h],
                            srow[:, hh * 1024: hh * 1024 + w2h],
                            AF.Exp, bias=dsc[:, 0:1], scale=1.0),
                            reads=(S["srow"], S["dsc"]), writes=(S["pbf%d%d" % (tp2, hh)],))
                    if t > 0:
                        emit_pv(t - 1)
            emit_pv(NT - 1)
            pb.barrier()
            if "s1" in dbg:
                pb.use("scalar", lambda e: e.dma_start(out=dbg_out["s1"], in_=s1_h), dma=True)
                pb.barrier()

        if "E" in phases:
          with ExitStack() as pctx:
            P = pctx.enter_context
            w1_sb = P(nc.sbuf_tensor("w1_sb", [128, 2, 8 * 1408], BF16))
            w3_sb = P(nc.sbuf_tensor("w3_sb", [128, 2, 8 * 1408], BF16))
            w2_sb = P(nc.sbuf_tensor("w2_sb", [128, 11 * 1024], BF16))
            b1c_sb = P(nc.sbuf_tensor("b1c_sb", [128, 11], F32))
            b3c_sb = P(nc.sbuf_tensor("b3c_sb", [128, 11], F32))
            b2m_sb = P(nc.sbuf_tensor("b2m_sb", [9, 1024], BF16))
            rwa = P(nc.sbuf_tensor("rwa", [128, 9], BF16))
            rwt = P(nc.sbuf_tensor("rwt", [9, 128], BF16))
            h_sb = P(nc.sbuf_tensor("h_sb", [128, 11 * 512], BF16))
            prod = P(nc.sbuf_tensor("prod", [128, 2, 512], F32))
            y_half = P(nc.sbuf_tensor("y_half", [128, 8 * 1024], F32))
            s1p = P(nc.sbuf_tensor("s1p", [128, 2, DM], F32))
            x1e = P(nc.sbuf_tensor("x1e", [128, DM], F32))
            slot("yout")
            for nm in ("w10", "w11", "w30", "w31", "w2", "b1c", "b3c", "b2m", "rwa", "rwt",
                       "h_sb", "prod0", "prod1", "s1p0", "s1p1", "x1e",
                       "yh0", "yh1", "yh2", "yh3", "yh4", "yh5", "yh6", "yh7"):
                slot(nm)

            def ld_s1(gi):
                bi = gi % 2
                pb.use("scalar", lambda e, gi=gi, bi=bi: e.dma_start(
                    out=s1p[:, bi], in_=s1_h[gi]),
                    reads=(S["s1_h"],), writes=(S["s1p%d" % bi],), dma=True)

            def ld_x1(gi):
                pb.use("sync", lambda e, gi=gi: e.dma_start(out=x1e[:], in_=x1_h[gi]),
                       reads=(S["x1_h"],), writes=(S["x1e"],), dma=True)
            pb.use("gpsimd", lambda e: e.dma_start(out=b2m_sb[0:8, :], in_=b2flat.rearrange("o (e v) -> (o e) v", v=1024)),
                   writes=(S["b2m"],), dma=True)
            pb.use("gpsimd", lambda e: e.dma_start(out=b2m_sb[8:9, :], in_=sb2flat),
                   writes=(S["b2m"],), dma=True)
            pb.use("vector", lambda e: e.memset(rwa[:, 8:9], 1.0), writes=(S["rwa"],))
            for half in range(2):
                for lt in range(8):
                    tg = half * 8 + lt
                    pb.use("vector", lambda e, tg=tg: e.tensor_copy(
                        rwa[:, 0:8], rw_tok[:, tg * 8:(tg + 1) * 8]),
                        reads=(S["rw_tok"],), writes=(S["rwa"],))
                    pb.use("tensor", lambda e: e.transpose(PT1[0:9, 0:128], rwa[:, 0:9], sb_ident[:]),
                           reads=(S["rwa"], S["consts"]), writes=(S["pt1"],))
                    pb.use("vector", lambda e: e.tensor_copy(rwt[0:9, :], PT1[0:9, 0:128]),
                           reads=(S["pt1"],), writes=(S["rwt"],))
                    def b2mm(e):
                        e.matmul(PS[2][:, 0:512], rwt[0:9, 0:128], b2m_sb[0:9, 0:512],
                                 start=True, stop=True, skip_group_check=True)
                        return e.matmul(PS[3][:, 0:512], rwt[0:9, 0:128], b2m_sb[0:9, 512:1024],
                                        start=True, stop=True, skip_group_check=True)
                    pb.use("tensor", b2mm, reads=(S["rwt"], S["b2m"]), writes=(S["ps2"], S["ps3"]))
                    pb.use("vector", lambda e, lt=lt: e.tensor_copy(
                        y_half[:, lt * 1024: lt * 1024 + 512], PS[2][:, 0:512]),
                        reads=(S["ps2"],), writes=(S["yh%d" % lt],))
                    pb.use("vector", lambda e, lt=lt: e.tensor_copy(
                        y_half[:, lt * 1024 + 512: lt * 1024 + 1024], PS[3][:, 0:512]),
                        reads=(S["ps3"],), writes=(S["yh%d" % lt],))
                for ex in range(9):
                    shared = ex == 8
                    wp = (half * 9 + ex) % 2
                    pb.use("gpsimd", lambda e, ex=ex, wp=wp: e.dma_start(
                        out=w1_sb[:, wp], in_=(csw1 if ex == 8 else cw1[ex])), writes=(S["w1%d" % wp],), dma=True)
                    pb.use("gpsimd", lambda e, ex=ex, wp=wp: e.dma_start(
                        out=w3_sb[:, wp], in_=(csw3 if ex == 8 else cw3[ex])), writes=(S["w3%d" % wp],), dma=True)
                    pb.use("gpsimd", lambda e, ex=ex: e.dma_start(
                        out=w2_sb[:], in_=(csw2 if ex == 8 else cw2[ex])), writes=(S["w2"],), dma=True)
                    pb.use("sync", lambda e, ex=ex: e.dma_start(
                        out=b1c_sb[:], in_=(sb1c if ex == 8 else b1c[ex])), writes=(S["b1c"],), dma=True)
                    pb.use("sync", lambda e, ex=ex: e.dma_start(
                        out=b3c_sb[:], in_=(sb3c if ex == 8 else b3c[ex])), writes=(S["b3c"],), dma=True)
                    if shared:
                        ld_s1(half * 8 + 0)
                        ld_s1(half * 8 + 1)
                        ld_x1(half * 8 + 0)
                    for st2 in range(2):
                        tok0 = half * 1024 + st2 * 512
                        for hc in range(11):
                            pA, pB = (0, 1) if hc % 2 == 0 else (4, 5)
                            pp = hc % 2
                            def mmA(e, hc=hc, tok0=tok0, pA=pA, wp=wp):
                                for kc in range(8):
                                    ins = e.matmul(PS[pA][:, 0:512],
                                                   w1_sb[:, wp][:, kc * 1408 + hc * 128: kc * 1408 + hc * 128 + 128],
                                                   xn1_ap[:, kc, tok0:tok0 + 512],
                                                   start=(kc == 0), stop=(kc == 7), skip_group_check=True)
                                return ins
                            pb.use("tensor", mmA, reads=(S["xn1f%d" % (half * 2 + st2)], S["w1%d" % wp]), writes=(S["ps%d" % pA],))
                            def mmB(e, hc=hc, tok0=tok0, pB=pB, wp=wp):
                                for kc in range(8):
                                    ins = e.matmul(PS[pB][:, 0:512],
                                                   w3_sb[:, wp][:, kc * 1408 + hc * 128: kc * 1408 + hc * 128 + 128],
                                                   xn1_ap[:, kc, tok0:tok0 + 512],
                                                   start=(kc == 0), stop=(kc == 7), skip_group_check=True)
                                return ins
                            pb.use("tensor", mmB, reads=(S["xn1f%d" % (half * 2 + st2)], S["w3%d" % wp]), writes=(S["ps%d" % pB],))
                            pb.use("scalar", lambda e, hc=hc, pB=pB, pp=pp: e.activation(
                                prod[:, pp], PS[pB][:, 0:512], AF.Identity, bias=b3c_sb[:, hc:hc + 1], scale=1.0),
                                reads=(S["ps%d" % pB], S["b3c"]), writes=(S["prod%d" % pp],))
                            pb.use("vector", lambda e, hc=hc, pA=pA, pp=pp: e.scalar_tensor_tensor(
                                prod[:, pp], PS[pA][:, 0:512], b1c_sb[:, hc:hc + 1], prod[:, pp],
                                ALU.add, ALU.mult),
                                reads=(S["ps%d" % pA], S["b1c"], S["prod%d" % pp]), writes=(S["prod%d" % pp],))
                            pb.use("scalar", lambda e, hc=hc, pp=pp: e.activation(
                                h_sb[:, hc * 512:(hc + 1) * 512], prod[:, pp], AF.Silu),
                                reads=(S["prod%d" % pp],), writes=(S["h_sb"],))
                        for i2 in range(4):
                            lt = st2 * 4 + i2
                            tg = half * 8 + lt
                            oA, oB = (2, 3) if i2 % 2 == 0 else (0, 1)
                            def omm(e, i2=i2, oA=oA, oB=oB):
                                for kc2 in range(11):
                                    e.matmul(PS[oA][:, 0:512],
                                             h_sb[:, kc2 * 512 + i2 * 128: kc2 * 512 + i2 * 128 + 128],
                                             w2_sb[:, kc2 * 1024: kc2 * 1024 + 512],
                                             start=(kc2 == 0), stop=(kc2 == 10), skip_group_check=True)
                                    ins = e.matmul(PS[oB][:, 0:512],
                                                   h_sb[:, kc2 * 512 + i2 * 128: kc2 * 512 + i2 * 128 + 128],
                                                   w2_sb[:, kc2 * 1024 + 512: kc2 * 1024 + 1024],
                                                   start=(kc2 == 0), stop=(kc2 == 10), skip_group_check=True)
                                return ins
                            pb.use("tensor", omm, reads=(S["h_sb"], S["w2"]), writes=(S["ps%d" % oA], S["ps%d" % oB]))
                            for v in range(2):
                                pv = oA if v == 0 else oB
                                if shared:
                                    pb.use("vector", lambda e, lt=lt, v=v, pv=pv: e.tensor_tensor(
                                        y_half[:, lt * 1024 + v * 512: lt * 1024 + v * 512 + 512],
                                        y_half[:, lt * 1024 + v * 512: lt * 1024 + v * 512 + 512],
                                        PS[pv][:, 0:512], ALU.add),
                                        reads=(S["ps%d" % pv], S["yh%d" % lt]), writes=(S["yh%d" % lt],))
                                else:
                                    pb.use("vector", lambda e, lt=lt, v=v, tg=tg, ex=ex, pv=pv: e.scalar_tensor_tensor(
                                        y_half[:, lt * 1024 + v * 512: lt * 1024 + v * 512 + 512],
                                        PS[pv][:, 0:512], rw_tok[:, tg * 8 + ex: tg * 8 + ex + 1],
                                        y_half[:, lt * 1024 + v * 512: lt * 1024 + v * 512 + 512],
                                        ALU.mult, ALU.add),
                                        reads=(S["ps%d" % pv], S["rw_tok"], S["yh%d" % lt]),
                                        writes=(S["yh%d" % lt],))
                        if shared:
                            for lt in range(st2 * 4, st2 * 4 + 4):
                                tg = half * 8 + lt
                                bi = lt % 2
                                yv = y_half[:, lt * 1024:(lt + 1) * 1024]
                                pb.use("scalar", lambda e, yv=yv: e.activation(yv, yv, AF.Silu),
                                       reads=(S["yh%d" % lt],), writes=(S["yh%d" % lt],))
                                pb.use("vector", lambda e, yv=yv, bi=bi: e.tensor_tensor(
                                    yv, yv, s1p[:, bi], ALU.mult),
                                    reads=(S["yh%d" % lt], S["s1p%d" % bi]), writes=(S["yh%d" % lt],))
                                if lt + 2 < 8:
                                    ld_s1(tg + 2)
                                pb.use("vector", lambda e, yv=yv: e.tensor_tensor(
                                    yv, yv, x1e[:], ALU.add),
                                    reads=(S["yh%d" % lt], S["x1e"]), writes=(S["yh%d" % lt],))
                                if lt + 1 < 8:
                                    ld_x1(tg + 1)
                                pb.use("sync", lambda e, tg=tg, yv=yv: e.dma_start(out=yout[tg], in_=yv),
                                       reads=(S["yh%d" % lt],), writes=(S["yout"],), dma=True)
            pb.barrier()

        blk = E(nc.Block())
        blk.sync(lambda e: [f(e) for f in pb.ops["sync"]])
        blk.gpsimd(lambda e: [f(e) for f in pb.ops["gpsimd"]])
        blk.vector(lambda e: [f(e) for f in pb.ops["vector"]])
        blk.scalar(lambda e: [f(e) for f in pb.ops["scalar"]])
        blk.tensor(lambda e: [f(e) for f in pb.ops["tensor"]])
    return nc


def kernel(**inputs):
    shared = prep_inputs(inputs)
    x = np.asarray(inputs["L0_attn_x"], dtype=np.float32)
    nc = build_program()
    in_maps = []
    for b in range(8):
        m = dict(shared)
        m["x0"] = np.ascontiguousarray(x[b].reshape(NT, 128, DM))
        in_maps.append(m)
    res = run_bass_kernel_spmd(nc, in_maps, core_ids=list(range(8)))
    out = np.stack([np.asarray(res.results[b]["y"], dtype=np.float32).reshape(SEQ, DM)
                    for b in range(8)], axis=0)
    return out
